# version 12
# baseline (speedup 1.0000x reference)
import numpy as np
import ml_dtypes
import concourse.bass as bass
import concourse.mybir as mybir
from concourse.bass_utils import run_bass_kernel_spmd

F32 = mybir.dt.float32
BF16 = mybir.dt.bfloat16
F32R = mybir.dt.float32r
AF = mybir.ActivationFunctionType
ALU = mybir.AluOpType
AX = mybir.AxisListType

MAGIC = float(np.float32(1.5 * 2 ** 23))
D = 1024
SEQ = 4096
SPAN = 2304
NSL = SPAN // 128
NSUB = SEQ // 128
QT = 384
NQT = SPAN // QT

QUAT_TAB = {
    (0, 0): (0, 1), (0, 1): (1, -1), (0, 2): (2, -1), (0, 3): (3, -1),
    (1, 0): (1, 1), (1, 1): (0, 1), (1, 2): (3, -1), (1, 3): (2, 1),
    (2, 0): (2, 1), (2, 1): (3, 1), (2, 2): (0, 1), (2, 3): (1, -1),
    (3, 0): (3, 1), (3, 1): (2, -1), (3, 2): (1, 1), (3, 3): (0, 1),
}
BIAS_SGN = np.array([[1, -1, -1, -1], [1, 1, 1, -1], [1, -1, 1, 1], [1, 1, -1, 1]],
                    np.float32)

ENGS = ("t", "v", "s", "g", "y")
DMA_SEMS = ("da", "db", "dw0", "dw1", "dw2", "dw3", "dm", "dgm", "dsp", "dy2", "dy3", "dwd", "dl0", "dl1", "dx0", "dx1", "dxa", "dxb", "dout")


class Sched:
    def __init__(self):
        self.ops = {e: [] for e in ENGS}
        self.cnt = {n: 0 for n in ENGS + DMA_SEMS}

    def op(self, eng, fn, waits=(), mark=False):
        waits = list(waits)
        if eng in ("v", "s"):
            mark = True
        if mark:
            waits.append((eng, self.cnt[eng]))
        best = {}
        for (sn, sv) in waits:
            if sv > 0 and sv > best.get(sn, 0):
                best[sn] = sv
        self.ops[eng].append((tuple(best.items()), fn, eng if mark else None, 1))
        if mark:
            self.cnt[eng] += 1
        return (eng, self.cnt[eng])

    def dma(self, eng, sem, fn, waits=()):
        best = {}
        for (sn, sv) in waits:
            if sv > 0 and sv > best.get(sn, 0):
                best[sn] = sv
        self.ops[eng].append((tuple(best.items()), fn, sem, 16))
        self.cnt[sem] += 16
        return (sem, self.cnt[sem])

    def emit(self, block, sems):
        def run(engobj, eng):
            for waits, fn, incsem, incval in self.ops[eng]:
                for (sn, sv) in waits:
                    engobj.wait_ge(sems[sn], sv)
                inst = fn(engobj)
                if incsem is not None:
                    inst.then_inc(sems[incsem], incval)

        @block.tensor
        def _(t):
            run(t, "t")

        @block.vector
        def _(v):
            run(v, "v")

        @block.scalar
        def _(s):
            run(s, "s")

        @block.gpsimd
        def _(g):
            run(g, "g")

        @block.sync
        def _(y):
            run(y, "y")


def build_program(stage="full"):
    nc = bass.Bass("TRN2", target_bir_lowering=False)

    def din(name, shape, dt=F32):
        return nc.dram_tensor(name, shape, dt, kind="ExternalInput").ap()

    XSF = din("XSF", [D, SPAN])
    XFF = din("XFF", [D, SEQ])
    XST = din("XST", [SPAN, D])
    QW = din("QW", [4, 256, 256]); QBI = din("QBI", [4, 256])
    KW = din("KW", [4, 256, 256]); KBI = din("KBI", [4, 256])
    VW = din("VW", [4, 256, 256]); VBI = din("VBI", [4, 256])
    OW = din("OW", [4, 256, 256]); OBI = din("OBI", [4, 256])
    JV = din("JV", [8, 128])
    FW = din("FW", [D, D]); FB = din("FB", [1, D])
    W1Q = din("W1Q", [D, D]); B1Q = din("B1Q", [128, 8])
    W1K = din("W1K", [D, 128]); B1K = din("B1K", [128, 1])
    W1V = din("W1V", [D, 128]); B1VC = din("B1VC", [128, 1])
    W1O = din("W1O", [D, D]); B1O = din("B1O", [1, D])
    WRT = din("WRT", [D, D]); WKT = din("WKT", [D, D]); WVT = din("WVT", [D, D])
    MUR = din("MUR", [128, 8]); MUK = din("MUK", [128, 8])
    MASKS = din("MASKS", [128, 5 * QT], BF16)
    IDENT = din("IDENT", [128, 128])

    ATT0 = nc.dram_tensor("ATT0", [SPAN, D], BF16, kind="Internal").ap()
    Y2SP = nc.dram_tensor("Y2SP", [SPAN, D], BF16, kind="Internal").ap()
    Y3SP = nc.dram_tensor("Y3SP", [SPAN, D], BF16, kind="Internal").ap()
    ATT1 = nc.dram_tensor("ATT1", [SPAN, D], BF16, kind="Internal").ap()
    YOUT = nc.dram_tensor("YOUT", [SPAN, D], F32, kind="ExternalOutput").ap()
    DBG = nc.dram_tensor("DBG", [128, 8 * SPAN], F32, kind="ExternalOutput").ap()

    sch = Sched()

    import contextlib
    stack = contextlib.ExitStack()
    with (
        nc.sbuf_tensor("TY2", [128, 18440], F32) as TY2t,
        nc.sbuf_tensor("TY3", [128, 18440], F32) as TY3t,
        nc.sbuf_tensor("TE", [128, 20480], BF16) as TEt,
        nc.sbuf_tensor("ROWB", [1, 1280], F32) as ROWB_t,
        nc.sbuf_tensor("ROWR", [1, 2048], F32R) as ROWR_t,
        nc.sbuf_tensor("BIASB", [128, 64], F32) as BIASB_t,
        nc.sbuf_tensor("ONER", [1, 128], F32R) as ONER_t,
        nc.sbuf_tensor("ONEF", [1, 128], F32) as ONEF_t,
        nc.sbuf_tensor("SMALL", [128, 192], F32) as SMALL_t,
        nc.sbuf_tensor("PVAB", [128, 408], F32) as PVA_t,
        nc.sbuf_tensor("IDF", [128, 128], F32) as IDF_t,
        nc.sbuf_tensor("IDB", [128, 128], BF16) as IDB_t,
        nc.psum_tensor("PALL", [128, 4096], F32) as PALL_t,
        nc.Block() as block,
    ):
        sem_handles = {}
        for n in ENGS + DMA_SEMS:
            sem_handles[n] = stack.enter_context(nc.semaphore("SEM_" + n))

        TY2 = TY2t.ap(); TY3 = TY3t.ap(); TE = TEt.ap()
        ROWB = ROWB_t.ap(); ROWR = ROWR_t.ap(); BIASB = BIASB_t.ap()
        ONER = ONER_t.ap(); ONEF = ONEF_t.ap(); SMALL = SMALL_t.ap()
        PVA = PVA_t.ap(); PALL = PALL_t.ap(); IDF = IDF_t.ap(); IDB = IDB_t.ap()

        def QBUF(c, a, n, p0=0, np_=128):
            return TY2.bitcast(BF16)[p0:p0 + np_, c * SPAN + a: c * SPAN + a + n]

        def VTB(slot, a, n, p0=0, np_=128):
            base = 36864 // 2
            return TY2.bitcast(BF16)[p0:p0 + np_, base + slot * 65 + a: base + slot * 65 + a + n]

        def XSG(p, c, a, n):
            base = (36864 + p * 8192) // 2
            return TY2.bitcast(BF16)[:, base + c * 512 + a: base + c * 512 + a + n]

        def P1FS1(c, n):
            return TY2.bitcast(F32)[:, 53248 // 4 + c * 512: 53248 // 4 + c * 512 + n]

        def P1FS2(c, n):
            if c >= 2:
                return PALL[:, 512 * 6: 512 * 6 + n]
            return TY2.bitcast(F32)[:, 69632 // 4 + c * 512: 69632 // 4 + c * 512 + n]

        def TY2V(byte_off, dt, cols, p0=0, np_=128):
            esz = 2 if dt == BF16 else 4
            return TY2.bitcast(dt)[p0:p0 + np_, byte_off // esz: byte_off // esz + cols]

        def KBUF(c, a, n, p0=0, np_=128):
            return TY3.bitcast(BF16)[p0:p0 + np_, c * SEQ + a: c * SEQ + a + n]

        def TY3V(byte_off, dt, cols, p0=0, np_=128):
            esz = 2 if dt == BF16 else 4
            return TY3.bitcast(dt)[p0:p0 + np_, byte_off // esz: byte_off // esz + cols]

        def Y3C(c, a, n, dt=BF16, p0=0, np_=128):
            return TY3.bitcast(dt)[p0:p0 + np_, c * SPAN + a: c * SPAN + a + n]

        def TEV(byte_off, dt, cols, p0=0, np_=128):
            esz = 2 if dt == BF16 else 4
            return TE.bitcast(dt)[p0:p0 + np_, byte_off // esz: byte_off // esz + cols]

        def EXPT(tile, a, n, p0=0, np_=128):
            return TE[p0:p0 + np_, tile * QT + a: tile * QT + a + n]

        def PB(b, a, n, dt=None):
            if dt is not None:
                esz = 2 if dt == BF16 else 4
                scale = 4 // esz
                return PALL.bitcast(dt)[:, scale * 512 * b + a: scale * 512 * b + a + n]
            return PALL[:, 512 * b + a: 512 * b + a + n]

        t_onef = sch.op("v", lambda v: v.memset(ONEF, 1.0))
        t_oner = sch.op("v", lambda v: v.tensor_copy(ONER, ONEF), waits=[t_onef])
        t_eps6 = sch.op("v", lambda v: v.memset(BIASB[:, 62:63], 1e-6))
        t_eps5 = sch.op("v", lambda v: v.memset(BIASB[:, 61:62], 1e-5))
        t_m30 = sch.op("v", lambda v: v.memset(BIASB[:, 60:61], -30.0))
        t_ones8 = sch.op("v", lambda v: v.memset(SMALL[0:8, 131:132], 1.0))

        tJ = sch.dma("y", "dm", lambda y: y.dma_start(out=SMALL[0:8, 0:128], in_=JV))
        t_sin = sch.op("s", lambda s: s.activation(SMALL[32:40, 0:128], SMALL[0:8, 0:128],
                                                   AF.Sin, accum_out=SMALL[0:8, 130:131]),
                       waits=[tJ])
        t_csum = sch.op("t", lambda t: t.matmul(
            PALL[0:1, 0:1], SMALL[0:8, 131:132], SMALL[0:8, 130:131],
            start=True, stop=True), waits=[t_sin, t_ones8], mark=True)
        t_cs = sch.op("v", lambda v: v.tensor_scalar(SMALL[0:1, 132:133], PALL[0:1, 0:1],
                                                     0.1 / 1024.0, None, ALU.mult),
                      waits=[t_csum])
        t_crep = sch.op("t", lambda t: t.matmul(
            PB(1, 0, 1), ONEF, SMALL[0:1, 132:133],
            start=True, stop=True), waits=[t_cs, t_onef], mark=True)
        t_cv = sch.op("v", lambda v: v.tensor_copy(BIASB[:, 63:64], PB(1, 0, 1)),
                      waits=[t_crep])

        dm_chain = [tJ]

        def q_bias_cols(col0, B_ap):
            tok = dm_chain[-1]
            for oc in range(8):
                ob, half = oc // 2, oc % 2
                for w in range(4):
                    sgn = float(BIAS_SGN[ob, w])
                    stg = SMALL[:, 160 + w: 161 + w]
                    tdma = sch.dma("y", "dm", (lambda y, s_=B_ap[w, 128 * half:128 * half + 128].unsqueeze(1),
                                               d_=stg: y.dma_start(out=d_, in_=s_)),
                                   waits=[tok])
                    dcol = BIASB[:, col0 + oc: col0 + oc + 1]
                    if w == 0:
                        tok = sch.op("v", (lambda v, d_=dcol, s_=stg, g_=sgn:
                                           v.tensor_scalar(d_, s_, g_, None, ALU.mult)),
                                     waits=[tdma])
                    else:
                        tok = sch.op("v", (lambda v, d_=dcol, s_=stg, g_=sgn:
                                           v.scalar_tensor_tensor(d_, s_, g_, d_, ALU.mult, ALU.add)),
                                     waits=[tdma])
            dm_chain.append(tok)
            return tok

        def quat_bias_row(dst_off, B_ap, as_f32r_off=None, final_extra=()):
            tok = dm_chain[-1]
            for ob in range(4):
                for w in range(4):
                    sgn = float(BIAS_SGN[ob, w])
                    stg = ROWB[0:1, 1024:1280]
                    tdma = sch.dma("y", "dm", (lambda y, s_=B_ap[w:w + 1, :], d_=stg:
                                               y.dma_start(out=d_, in_=s_)), waits=[tok])
                    dst = ROWB[0:1, dst_off + ob * 256: dst_off + (ob + 1) * 256]
                    if w == 0:
                        tok = sch.op("v", (lambda v, d_=dst, s_=stg, g_=sgn:
                                           v.tensor_scalar(d_, s_, g_, None, ALU.mult)),
                                     waits=[tdma])
                    else:
                        tok = sch.op("v", (lambda v, d_=dst, s_=stg, g_=sgn:
                                           v.scalar_tensor_tensor(d_, s_, g_, d_, ALU.mult, ALU.add)),
                                     waits=[tdma])
            if as_f32r_off is not None:
                tok = sch.op("v", (lambda v, o_=as_f32r_off: v.tensor_copy(
                    ROWR[0:1, o_:o_ + 1024], ROWB[0:1, 0:1024])),
                             waits=[tok] + list(final_extra))
            dm_chain.append(tok)
            return tok

        tbq = q_bias_cols(0, QBI)
        tbk = q_bias_cols(8, KBI)

        t_id = sch.dma("y", "dm", lambda y: y.dma_start(out=IDF, in_=IDENT),
                       waits=[dm_chain[-1]])
        dm_chain.append(t_id)
        l_idb = sch.dma("g", "dgm", lambda g: g.dma_start(out=IDB, in_=IDENT))
        t_fbrow = sch.dma("g", "dgm", lambda g: g.dma_start(
            out=ROWR[0:1, 1024:2048], in_=FB), waits=[l_idb])

        quatw_lastv = [dm_chain[-1]] * 4

        def quat_w(dst_off, W_ap):
            lastv = quatw_lastv
            tok = None
            for ic in range(8):
                comp, half = ic // 2, ic % 2
                for ob in range(4):
                    w_idx, sgn = QUAT_TAB[(ob, comp)]
                    stg = P1FS1(ob, 256)
                    tdma = sch.dma("y", "dw%d" % ob, (lambda y, s_=W_ap[w_idx, 128 * half:128 * half + 128, :],
                                               d_=stg: y.dma_start(out=d_, in_=s_)),
                                   waits=[lastv[ob]])
                    dst = TEV(dst_off, BF16, 8 * D)[:, ic * D + ob * 256: ic * D + ob * 256 + 256]
                    tok = sch.op("v", (lambda v, d_=dst, s_=stg, g_=float(sgn):
                                       v.tensor_scalar(d_, s_, g_, None, ALU.mult)),
                                 waits=[tdma])
                    lastv[ob] = tok
            return tok

        twq = quat_w(0, QW)
        twk = quat_w(16384, KW)

        psum_free = {}
        stg_free = [None, None]
        norm_free = [None]

        def proj_norm(si, n, src_fn, wbig_off, bias_col, dst_fn, extra_first=()):
            p = si % 2
            dsem = "dx0" if p == 0 else "dx1"
            tld = None
            for ic in range(8):
                wt = [stg_free[p]] if stg_free[p] else []
                tld = sch.dma("g", dsem, (lambda g, s_=src_fn(ic), d_=XSG(p, ic, 0, n):
                                          g.dma_start(out=d_, in_=s_)), waits=wt)
            ev = []
            for oc in range(8):
                bank = oc % 4
                last_mm = None
                for ic in range(8):
                    wt = []
                    if ic == 0:
                        wt = [tld] + list(extra_first)
                        if psum_free.get(bank):
                            wt.append(psum_free[bank])
                    lhsT = TEV(wbig_off, BF16, 8 * D)[:, ic * D + oc * 128: ic * D + oc * 128 + 128]
                    rhs = XSG(p, ic, 0, n)
                    last_mm = sch.op("t", (lambda t, o_=PB(bank, 0, n), l_=lhsT, r_=rhs,
                                           s0=(ic == 0), s1=(ic == 7):
                                           t.matmul(o_, l_, r_, start=s0, stop=s1)),
                                     mark=(ic == 7), waits=wt)
                ev.append(sch.op("v", (lambda v, d_=P1FS1(oc, n), p_=PB(bank, 0, n),
                                       b_=BIASB[:, bias_col + oc: bias_col + oc + 1]:
                                       v.tensor_scalar(d_, p_, b_, None, ALU.add)),
                                 waits=[last_mm]))
                psum_free[bank] = ev[-1]
            stg_free[p] = sch.op("t", lambda t: t.nop(), mark=True)

            out_toks = []
            for h in range(2):
                ssum = P1FS2(h, n)
                wt = [ev[h]] + ([norm_free[0]] if norm_free[0] else [])
                t0s = sch.op("s", (lambda ss, d_=ssum, a_=P1FS1(h, n):
                                   ss.activation(d_, a_, AF.Square)),
                             waits=wt)
                t0 = None
                for j in (2, 4, 6):
                    tmp = P1FS2(2, n)
                    tsq = sch.op("s", (lambda ss, d_=tmp, a_=P1FS1(h + j, n):
                                       ss.activation(d_, a_, AF.Square)),
                                 waits=[ev[h + j]] + ([t0] if t0 else []))
                    t0 = sch.op("v", (lambda v, d_=ssum, a_=tmp:
                                      v.tensor_tensor(d_, d_, a_, ALU.add)),
                                waits=[tsq] + ([t0s] if j == 2 else []))
                t2 = sch.op("s", (lambda s, d_=ssum: s.activation(d_, d_, AF.Ln,
                                                                  bias=BIASB[:, 62:63])),
                            waits=[t0, t_eps6])
                t3 = sch.op("s", (lambda s, d_=ssum: s.activation(d_, d_, AF.Exp, scale=-0.5)),
                            waits=[t2])
                out_toks.append(t3)
            final = None
            for oc in range(8):
                h = oc % 2
                final = sch.op("v", (lambda v, d_=dst_fn(oc), a_=P1FS1(oc, n), b_=P1FS2(h, n):
                                     v.tensor_tensor(d_, a_, b_, ALU.mult)),
                               waits=[out_toks[h]])
            norm_free[0] = final
            return final

        wready = (twk, tbk, t_cv)
        k_done = q_done = None
        for si in range(8):
            k_done = proj_norm(
                si, 512,
                (lambda ic, si_=si: XFF[128 * ic:128 * ic + 128, 512 * si_:512 * si_ + 512]),
                16384, 8,
                (lambda oc, si_=si: KBUF(oc, 512 * si_, 512)),
                extra_first=(wready if si == 0 else ()))
        for sj in range(5):
            n = 512 if sj < 4 else 256
            q_done = proj_norm(
                8 + sj, n,
                (lambda ic, s_=sj, n_=n: XSF[128 * ic:128 * ic + 128, 512 * s_:512 * s_ + n_]),
                0, 0,
                (lambda oc, s_=sj, n_=n: QBUF(oc, 512 * s_, n_)))

        if stage == "p1a":
            for c in range(8):
                sch.dma("g", "dout", (lambda g, c_=c, q_=QBUF(c, 0, SPAN): g.dma_start(
                    out=DBG[:, c_ * SPAN: (c_ + 1) * SPAN],
                    in_=q_)), waits=[q_done, k_done])
            sch.op("y", lambda y: y.wait_ge(sem_handles["dout"], sch.cnt["dout"]))
            sch.emit(block, sem_handles)
            stack.close()
            return nc

        tk_t = None
        tkq_nop = sch.op("t", lambda t: t.nop(), mark=True)

        t_vrow = quat_bias_row(0, VBI, as_f32r_off=0)

        def quat_w2(dst_off, W_ap, firstw):
            lastv = [firstw] * 4
            tok = None
            for ic in range(8):
                comp, half = ic // 2, ic % 2
                for ob in range(4):
                    w_idx, sgn = QUAT_TAB[(ob, comp)]
                    stg = TEV(16384 + ob * 1024, F32, 256)
                    tdma = sch.dma("y", "dw%d" % ob,
                                   (lambda y, s_=W_ap[w_idx, 128 * half:128 * half + 128, :],
                                    d_=stg: y.dma_start(out=d_, in_=s_)),
                                   waits=[lastv[ob]])
                    dst = TEV(dst_off, BF16, 8 * D)[:, ic * D + ob * 256: ic * D + ob * 256 + 256]
                    tok = sch.op("v", (lambda v, d_=dst, s_=stg, g_=float(sgn):
                                       v.tensor_scalar(d_, s_, g_, None, ALU.mult)),
                                 waits=[tdma])
                    lastv[ob] = tok
            return tok

        twv = quat_w2(24576, VW, tkq_nop)

        VT3 = TY2.bitcast(BF16)[:, 36864 // 2: 36864 // 2 + 256 * 65].rearrange(
            "p (s c) -> p s c", c=65)
        t_vones = sch.op("v", lambda v: v.memset(VT3[:, :, 64:65], 1.0),
                         waits=[q_done, tkq_nop])

        def XSG2(c, a, n):
            base = 65536 // 2
            return TY3.bitcast(BF16)[:, base + c * 512 + a: base + c * 512 + a + n]

        def ATMv(a, n):
            base = 65536 // 2
            return TY3.bitcast(BF16)[:, base + a: base + a + n]

        def emit_p1b(g, first_waits):
            xsg2_free = None
            done = None
            for blk in range(8):
                wt = list(first_waits) + ([xsg2_free] if xsg2_free else [])
                tld0 = tld1 = None
                for ic in range(8):
                    sem = "db" if ic % 2 == 0 else "dx0"
                    t_ = sch.dma("g", sem,
                                 (lambda gg, s_=XFF[128 * ic:128 * ic + 128, 512 * blk:512 * blk + 512],
                                  d_=XSG2(ic, 0, 512): gg.dma_start(out=d_, in_=s_)),
                                 waits=wt if ic < 2 else [])
                    if ic % 2 == 0:
                        tld0 = t_
                    else:
                        tld1 = t_
                tld = tld0
                for sub4 in range(4):
                    bank = sub4 % 2
                    last = None
                    for ic in range(8):
                        w = []
                        if ic == 0:
                            w = [tld0, tld1]
                            if psum_free.get(bank):
                                w.append(psum_free[bank])
                            if ic == 0 and blk == 0 and sub4 == 0:
                                w.append(twv)
                                w.append(t_vrow)
                        lhsT = XSG2(ic, sub4 * 128, 128)
                        rhs = TEV(24576, BF16, 8 * D)[:, ic * D + g * 512: ic * D + g * 512 + 512]
                        last = sch.op("t", (lambda t, o_=PB(bank, 0, 512), l_=lhsT, r_=rhs,
                                            s0=(ic == 0): t.matmul(o_, l_, r_, start=s0, stop=False)),
                                      waits=w)
                    last = sch.op("t", (lambda t, o_=PB(bank, 0, 512),
                                        r_=ROWR[0:1, g * 512:(g + 1) * 512]:
                                        t.matmul(o_, ONER, r_, start=False, stop=True)),
                                  mark=True)
                    slot0 = (blk * 4 + sub4) * 8
                    dst3 = TY2.bitcast(BF16)[:, 36864 // 2 + slot0 * 65: 36864 // 2 + slot0 * 65 + 8 * 65].rearrange(
                        "p (s c) -> p s c", c=65)[:, :, 0:64]
                    src3 = PB(bank, 0, 512).rearrange("p (s c) -> p s c", c=64)
                    ev = sch.op("v", (lambda v, d_=dst3, s_=src3: v.tensor_copy(d_, s_)),
                                waits=[last, t_vones])
                    psum_free[bank] = ev
                    done = ev
                xsg2_free = sch.op("t", lambda t: t.nop(), mark=True)
            return done, xsg2_free

        def emit_p1c(g, ready, state, hook=None, nsets=2):
            PVB = (6, 7)
            for qt in range(NQT):
                batches = [(hp, H, half) for hp in range(4)
                           for H in (0, 1) for half in (0, 1)]
                etoks = {}

                def emit_A(j, qt=qt, batches=batches, etoks=etoks):
                    hp, H, half = batches[j]
                    kch = g * 4 + hp
                    toks = []
                    for sq in range(8):
                        par = (j * 8 + sq) % nsets
                        set_base = 2 * par
                        fw = []
                        if state["exp_prev"][par]:
                            fw.append(state["exp_prev"][par])
                        if qt == 0 and j == 0 and sq == 0:
                            fw += list(ready)
                        sub = H * 16 + half * 8 + sq
                        last = None
                        for hh in (0, 1):
                            bank = set_base + hh
                            lhsT = KBUF(kch, sub * 128, 128, p0=64 * hh, np_=64)
                            rhs = QBUF(kch, qt * QT, QT, p0=64 * hh, np_=64)
                            w = fw if hh == 0 else []
                            last = sch.op("t", (lambda t, o_=PB(bank, 0, QT), l_=lhsT,
                                                r_=rhs, tp=(64 * hh, 0):
                                                t.matmul(o_, l_, r_, start=True, stop=True,
                                                         tile_position=tp)),
                                          waits=w, mark=(hh == 1))
                        ew = [last]
                        if state["expt_free"][j % 2]:
                            ew.append(state["expt_free"][j % 2])
                        tile0 = (j % 2) * 16 + sq * 2
                        in_ap = PALL[:, set_base * 512: set_base * 512 + 1024].rearrange(
                            "p (b x) -> p b x", x=512)[:, :, 0:QT]
                        out_ap = TE[:, tile0 * QT: (tile0 + 2) * QT].rearrange(
                            "p (b x) -> p b x", x=QT)
                        etok = sch.op("s", (lambda ss, o_=out_ap, i_=in_ap:
                                            ss.activation(o_, i_, AF.Exp,
                                                          bias=BIASB[:, 63:64], scale=0.125)),
                                      waits=ew)
                        state["exp_prev"][par] = etok
                        toks.append(etok)
                    etoks[j] = toks

                def emit_P(j, qt=qt, batches=batches, etoks=etoks):
                    hp, H, half = batches[j]
                    b = PVB[hp % 2]
                    first_of_hp = (H == 0 and half == 0)
                    last_of_hp = (H == 1 and half == 1)
                    pv_last = None
                    for sq in range(8):
                        sub = H * 16 + half * 8 + sq
                        for hh in (0, 1):
                            hl = 2 * hp + hh
                            tile = (j % 2) * 16 + sq * 2 + hh
                            for qs in (0, 1, 2):
                                w = []
                                if hh == 0 and qs == 0:
                                    w.append(etoks[j][sq])
                                    if first_of_hp and sq == 0 and state["pvfree"][b]:
                                        w.append(state["pvfree"][b])
                                st = first_of_hp and sq == 0 and hh == 0 and qs == 0
                                lhsT = EXPT(tile, qs * 128, 128)
                                rhs = VTB(sub * 8 + hl, 0, 65)
                                out = PB(b, (hh * 3 + qs) * 65, 65)
                                pv_last = sch.op("t", (lambda t, o_=out, l_=lhsT, r_=rhs,
                                                       s0=st:
                                                       t.matmul(o_, l_, r_, start=s0, stop=False,
                                                                skip_group_check=True)),
                                                 waits=w,
                                                 mark=(sq == 7 and hh == 1 and qs == 2))
                    state["expt_free"][j % 2] = pv_last
                    if last_of_hp:
                        dens3 = PB(b, 0, 390).rearrange("p (s c) -> p s c", c=65)[:, :, 64:65]
                        t2 = sch.op("v", (lambda v, d3=dens3: v.reciprocal(SMALL[:, 80:86], d3)),
                                    waits=[pv_last])
                        ntok = None
                        for hh in (0, 1):
                            for qs in (0, 1, 2):
                                idx = hh * 3 + qs
                                hl = 2 * hp + hh
                                dst = ATMv(qs * 512 + hl * 64, 64)
                                w = [t2]
                                if state["atm_free"]:
                                    w.append(state["atm_free"])
                                ntok = sch.op("v", (lambda v, d_=dst,
                                                    s_=PB(b, idx * 65, 64),
                                                    r_=SMALL[:, 80 + idx: 81 + idx]:
                                                    v.tensor_scalar(d_, s_, r_, None, ALU.mult)),
                                              waits=w)
                        state["pvfree"][b] = ntok
                        state["pv_norm"] = ntok
                        if hook:
                            hook(3 * qt - 1, 2)

                emit_A(0)
                emit_A(1)
                for j in range(16):
                    emit_P(j)
                    if j + 2 < 16:
                        emit_A(j + 2)
                sp = None
                for qs in range(3):
                    sp = sch.dma("y", "dsp", (lambda y, r0=qt * QT + qs * 128,
                                              a_=ATMv(qs * 512, 512), g_=g:
                                              y.dma_start(out=ATT0[r0:r0 + 128, g_ * 512:(g_ + 1) * 512],
                                                          in_=a_)),
                                 waits=[state["pv_norm"]])
                state["atm_free"] = sp
                if hook:
                    hook(3 * qt + 2, 4)
            return state

        st_p1c = {"exp_prev": {0: None, 1: None, 2: None}, "pv_norm": None,
                  "expt_free": [None, None], "atm_free": None,
                  "pvfree": {6: None, 7: None}}
        v_done0, xf0 = emit_p1b(0, [t_vones])
        st_p1c = emit_p1c(0, [v_done0, tkq_nop, q_done], st_p1c, nsets=3)
        tg0 = sch.op("t", lambda t: t.nop(), mark=True)
        g0_exps = [t for t in (st_p1c["exp_prev"][0], st_p1c["exp_prev"][1],
                               st_p1c["exp_prev"][2]) if t]
        v_done1, xf1 = emit_p1b(1, [st_p1c["atm_free"], st_p1c["pv_norm"]])
        st_p1c["atm_free"] = xf1

        t_orow = quat_bias_row(0, OBI, as_f32r_off=0, final_extra=[xf1])

        def quat_w3(dst_fn, W_ap, firstw, stg_fn, extraw=()):
            lastv = [firstw] * 2
            tok = None
            for ic in range(8):
                comp, half = ic // 2, ic % 2
                for ob in range(4):
                    w_idx, sgn = QUAT_TAB[(ob, comp)]
                    slot = ob % 2
                    stg = stg_fn(slot)
                    tdma = sch.dma("y", "dw%d" % slot,
                                   (lambda y, s_=W_ap[w_idx, 128 * half:128 * half + 128, :],
                                    d_=stg: y.dma_start(out=d_, in_=s_)),
                                   waits=[lastv[slot]] + (list(extraw) if ic == 0 and ob < 2 else []))
                    dst = dst_fn(ic, ob)
                    tok = sch.op("v", (lambda v, d_=dst, s_=stg, g_=float(sgn):
                                       v.tensor_scalar(d_, s_, g_, None, ALU.mult)),
                                 waits=[tdma] + ([xf1] if ic == 0 and ob == 0 else []))
                    lastv[slot] = tok
            return tok

        two = quat_w3(
            (lambda ic, ob: TEV(24576, BF16, 8 * D)[:, ic * D + ob * 256: ic * D + ob * 256 + 256]),
            OW, twv,
            (lambda slot: TEV(22528 + slot * 1024, F32, 256)),
            extraw=[tg0] + g0_exps)

        def emit_quant(src_ap, dst_ap, wait0, slots):
            Aap, Bap, Cap, Dap = slots
            bmax = SMALL[:, 0:32]
            scl = SMALL[:, 32:64]
            rec = SMALL[:, 134:166]
            src3 = src_ap.rearrange("p (b w) -> p b w", w=32)
            t1 = sch.op("v", (lambda v: v.tensor_reduce(bmax, src3, AX.X, ALU.max,
                                                        apply_absolute_value=True)),
                        waits=list(wait0))
            t2 = sch.op("v", lambda v: v.tensor_scalar(scl, bmax, 1.0 / 7.5, 1e-30,
                                                       ALU.mult, ALU.max))
            t3 = sch.op("v", lambda v: v.reciprocal(rec, scl))
            rb = rec.unsqueeze(2).broadcast_to([128, 32, 32])
            sb = scl.unsqueeze(2).broadcast_to([128, 32, 32])
            A3 = Aap.rearrange("p (b w) -> p b w", w=32)
            t4 = sch.op("v", (lambda v: v.tensor_tensor(A3, src3, rb, ALU.mult)))
            t5 = sch.op("s", (lambda ss: ss.activation(Bap, Aap, AF.Abs)), waits=[t4])
            t6 = sch.op("v", lambda v: v.tensor_scalar(Cap, Bap, 2.0, None, ALU.is_lt),
                        waits=[t5])
            t7 = sch.op("v", lambda v: v.tensor_scalar(Bap, Bap, 4.0, None, ALU.is_lt))
            t8 = sch.op("v", lambda v: v.tensor_scalar(Dap, Bap, -0.25, 0.5, ALU.mult, ALU.add))
            t9 = sch.op("v", lambda v: v.scalar_tensor_tensor(Dap, Cap, -0.125, Dap,
                                                              ALU.mult, ALU.add))
            ta = sch.op("v", lambda v: v.scalar_tensor_tensor(Bap, Cap, 2.0, Bap,
                                                              ALU.mult, ALU.add))
            tb = sch.op("v", lambda v: v.tensor_scalar(Bap, Bap, 2.0, 2.0, ALU.mult, ALU.add))
            tc = sch.op("v", lambda v: v.tensor_tensor(Cap, Aap, Bap, ALU.mult))
            td = sch.op("v", lambda v: v.tensor_scalar(Cap, Cap, MAGIC, MAGIC,
                                                       ALU.add, ALU.subtract))
            te = sch.op("v", lambda v: v.tensor_tensor(Cap, Cap, Dap, ALU.mult))
            C3 = Cap.rearrange("p (b w) -> p b w", w=32)
            d3 = dst_ap.rearrange("p (b w) -> p b w", w=32)
            tf = sch.op("v", (lambda v: v.tensor_tensor(d3, C3, sb, ALU.mult)))
            return tf

        def QSLOT(i):
            return TY2.bitcast(F32)[:, i * 1024:(i + 1) * 1024]
        P2_SLOTS = (QSLOT(0), QSLOT(1), QSLOT(2), QSLOT(3))
        XDT_AP = TY2.bitcast(BF16)[:, 8192:9216]
        XDF_AP = TY2.bitcast(BF16)[:, 35072:36096]

        def Y1AP(p):
            off = 16384 if p == 0 else 24576
            return TY3.bitcast(F32)[:, off // 4: off // 4 + 1024]
        XTM_AP = TY3.bitcast(F32)[:, 20480 // 4: 20480 // 4 + 1024]

        def AF0(p, c, a, n):
            base = (28672 + p * 2048) // 2
            return TY3.bitcast(BF16)[:, base + c * 128 + a: base + c * 128 + a + n]

        p2s = {"oproj": {}, "y1evac": {}, "quant": {}, "transp": {},
               "ffn0": {}, "z2": {}, "spill": {}, "ln2": {}}

        def p2_f1(sl):
            p = sl % 2
            dl = "dl0" if p == 0 else "dl1"
            tldA = None
            for c in range(8):
                w = []
                if c == 0:
                    w = [("dsp", sch.cnt["dsp"]), tg0]
                    if p2s["oproj"].get(sl - 2):
                        w.append(p2s["oproj"][sl - 2])
                tldA = sch.dma("y", dl, (lambda y, s_=ATT0[128 * sl:128 * sl + 128,
                                                           128 * c:128 * c + 128],
                                         d_=AF0(p, c, 0, 128):
                                         y.dma_start_transpose(d_, s_)), waits=w)
            w = [tg0] + ([p2s["y1evac"][sl - 1]] if sl >= 1 else [])
            tldX = sch.dma("g", "dxa", (lambda g, s_=XST[128 * sl:128 * sl + 128, :]:
                                        g.dma_start(out=XTM_AP, in_=s_)), waits=w)
            o_marks = []
            for nh in range(2):
                bank = 4 + nh
                for ic in range(8):
                    w = []
                    if ic == 0:
                        w = [tldA, two, t_orow]
                        if psum_free.get(bank):
                            w.append(psum_free[bank])
                    sch.op("t", (lambda t, o_=PB(bank, 0, 512),
                                 l_=AF0(p, ic, 0, 128),
                                 r_=TEV(24576, BF16, 8 * D)[:, ic * D + nh * 512: ic * D + nh * 512 + 512],
                                 s0=(ic == 0): t.matmul(o_, l_, r_, start=s0, stop=False)),
                           waits=w)
                last = sch.op("t", (lambda t, o_=PB(bank, 0, 512),
                                    r_=ROWR[0:1, nh * 512:(nh + 1) * 512]:
                                    t.matmul(o_, ONER, r_, start=False, stop=True)), mark=True)
                o_marks.append(last)
            p2s["oproj"][sl] = o_marks[-1]
            ev1 = None
            for nh in range(2):
                ev1 = sch.op("v", (lambda v, d_=Y1AP(p)[:, nh * 512:(nh + 1) * 512],
                                   a_=PB(4 + nh, 0, 512), b_=XTM_AP[:, nh * 512:(nh + 1) * 512]:
                                   v.tensor_tensor(d_, a_, b_, ALU.add)),
                             waits=[o_marks[nh], tldX])
                psum_free[4 + nh] = ev1
            p2s["y1evac"][sl] = ev1

        def p2_f2(sl):
            p = sl % 2
            y1ap = Y1AP(p)
            tsum = sch.op("v", (lambda v, z_=y1ap: v.tensor_scalar(QSLOT(1), z_, 1.0, 0.0,
                                                          ALU.mult, ALU.add,
                                                          accum_out=SMALL[:, 70:71])),
                          waits=[p2s["y1evac"][sl]])
            tsq = sch.op("s", (lambda ss, z_=y1ap: ss.activation(QSLOT(2), z_, AF.Square,
                                                        accum_out=SMALL[:, 71:72])),
                         waits=[tsum])
            tm1 = sch.op("v", lambda v: v.tensor_scalar(SMALL[:, 72:73], SMALL[:, 70:71],
                                                        -1.0 / 1024.0, None, ALU.mult))
            tm2 = sch.op("v", lambda v: v.scalar_tensor_tensor(SMALL[:, 73:74], SMALL[:, 72:73],
                                                               1.0, SMALL[:, 72:73],
                                                               ALU.mult, ALU.mult))
            tvv = sch.op("v", lambda v: v.scalar_tensor_tensor(SMALL[:, 74:75], SMALL[:, 71:72],
                                                               1.0 / 1024.0, SMALL[:, 73:74],
                                                               ALU.mult, ALU.subtract),
                         waits=[tsq])
            tl1 = sch.op("s", lambda ss: ss.activation(SMALL[:, 74:75], SMALL[:, 74:75], AF.Ln,
                                                       bias=BIASB[:, 61:62]), waits=[tvv, t_eps5])
            tl2 = sch.op("s", lambda ss: ss.activation(SMALL[:, 74:75], SMALL[:, 74:75], AF.Exp,
                                                       scale=-0.5))
            tap = sch.op("v", (lambda v, z_=y1ap: v.tensor_scalar(z_, z_, SMALL[:, 72:73],
                                                         SMALL[:, 74:75], ALU.add, ALU.mult)),
                         waits=[tl2])
            w0 = [tap, p2s["transp"][sl - 1]]
            p2s["quant"][sl] = emit_quant(y1ap, XDT_AP, w0, P2_SLOTS)

        def p2_back(sl):
            p = sl % 2
            last_tt = None
            evs = []
            for c in range(8):
                bank = 4 + (c % 2)
                w = [p2s["quant"][sl], l_idb]
                if psum_free.get(bank):
                    w.append(psum_free[bank])
                last_tt = sch.op("t", (lambda t, o_=PB(bank, (c // 2) * 128, 128, BF16),
                                  i_=XDT_AP[:, c * 128:(c + 1) * 128]:
                                  t.transpose(o_, i_, IDB)),
                            waits=w, mark=True)
                ew = [last_tt]
                if c == 0 and sl >= 1:
                    ew.append(p2s["spill"][sl - 1])
                ev = sch.op("v", (lambda v, d_=XDF_AP[:, c * 128:(c + 1) * 128],
                                  s_=PB(bank, (c // 2) * 128, 128, BF16): v.tensor_copy(d_, s_)),
                            waits=ew)
                psum_free[bank] = ev
                evs.append(ev)
            p2s["transp"][sl] = last_tt
            f_marks = []
            for nh in range(2):
                bank = 4 + nh
                for ic in range(8):
                    w = []
                    if ic == 0:
                        w = [evs[-1], wd_state["done"], t_fbrow]
                        if psum_free.get(bank):
                            w.append(psum_free[bank])
                    sch.op("t", (lambda t, o_=PB(bank, 0, 512),
                                 l_=XDF_AP[:, ic * 128:(ic + 1) * 128],
                                 r_=TY3V(0, BF16, 8 * D)[:, ic * D + nh * 512: ic * D + nh * 512 + 512],
                                 s0=(ic == 0): t.matmul(o_, l_, r_, start=s0, stop=False)),
                           waits=w)
                last = sch.op("t", (lambda t, o_=PB(bank, 0, 512),
                                    r_=ROWR[0:1, 1024 + nh * 512: 1024 + (nh + 1) * 512]:
                                    t.matmul(o_, ONER, r_, start=False, stop=True)), mark=True)
                f_marks.append(last)
            ev2 = None
            for nh in range(2):
                ev2 = sch.op("v", (lambda v, d_=QSLOT(0)[:, nh * 512:(nh + 1) * 512],
                                   a_=PB(4 + nh, 0, 512), b_=Y1AP(p)[:, nh * 512:(nh + 1) * 512]:
                                   v.tensor_tensor(d_, a_, b_, ALU.add)),
                             waits=[f_marks[nh]])
                psum_free[4 + nh] = ev2
            p2s["z2"][sl] = ev2
            zap = QSLOT(0)
            t2sum = sch.op("v", (lambda v, z_=zap: v.tensor_scalar(QSLOT(1), z_, 1.0, 0.0,
                                                           ALU.mult, ALU.add,
                                                           accum_out=SMALL[:, 75:76])),
                           waits=[ev2])
            t2sq = sch.op("s", (lambda ss, z_=zap: ss.activation(QSLOT(3), z_, AF.Square,
                                                         accum_out=SMALL[:, 76:77])),
                          waits=[t2sum])
            t2m1 = sch.op("v", lambda v: v.tensor_scalar(SMALL[:, 77:78], SMALL[:, 75:76],
                                                         -1.0 / 1024.0, None, ALU.mult))
            t2m2 = sch.op("v", lambda v: v.scalar_tensor_tensor(SMALL[:, 78:79], SMALL[:, 77:78],
                                                                1.0, SMALL[:, 77:78],
                                                                ALU.mult, ALU.mult))
            t2v = sch.op("v", lambda v: v.scalar_tensor_tensor(SMALL[:, 79:80], SMALL[:, 76:77],
                                                               1.0 / 1024.0, SMALL[:, 78:79],
                                                               ALU.mult, ALU.subtract),
                         waits=[t2sq])
            t2l1 = sch.op("s", lambda ss: ss.activation(SMALL[:, 79:80], SMALL[:, 79:80], AF.Ln,
                                                        bias=BIASB[:, 61:62]), waits=[t2v])
            t2l2 = sch.op("s", lambda ss: ss.activation(SMALL[:, 79:80], SMALL[:, 79:80], AF.Exp,
                                                        scale=-0.5))
            t2ap = sch.op("v", (lambda v, z_=zap: v.tensor_scalar(XDF_AP, z_,
                                                          SMALL[:, 77:78], SMALL[:, 79:80],
                                                          ALU.add, ALU.mult)),
                          waits=[t2l2])
            p2s["ln2"][sl] = t2ap
            p2s["spill"][sl] = sch.dma("y", "dy2", (lambda y, r0=128 * sl:
                                                    y.dma_start(out=Y2SP[r0:r0 + 128, :],
                                                                in_=XDF_AP)),
                                       waits=[t2ap])

        wd_state = {"tld": {}, "tq": {}, "done": None, "wt_last": None}

        def emit_wd(oc):
            if oc == 0:
                wd_state["tld"][0] = sch.dma(
                    "g", "da", (lambda g: g.dma_start(out=Y1AP(0), in_=FW[0:128, :])),
                    waits=[tg0])
            if oc + 1 < 8:
                w = [wd_state["tq"][oc - 1]] if oc >= 1 else [tg0]
                wd_state["tld"][oc + 1] = sch.dma(
                    "g", "da" if (oc + 1) % 2 == 0 else "dwd",
                    (lambda g, s_=FW[128 * (oc + 1):128 * (oc + 1) + 128, :],
                     d_=Y1AP((oc + 1) % 2): g.dma_start(out=d_, in_=s_)),
                    waits=w)
            w0 = [wd_state["tld"][oc], tg0]
            if oc >= 1:
                w0.append(wd_state["wt_last"])
            tq = emit_quant(Y1AP(oc % 2), XDT_AP, w0, P2_SLOTS)
            wd_state["tq"][oc] = tq
            evs = []
            for c in range(8):
                bank = 4 + (c % 2)
                w = [tq, l_idb]
                if psum_free.get(bank):
                    w.append(psum_free[bank])
                tt = sch.op("t", (lambda t, o_=PB(bank, (c // 2) * 128, 128, BF16),
                                  i_=XDT_AP[:, c * 128:(c + 1) * 128]:
                                  t.transpose(o_, i_, IDB)),
                            waits=w, mark=True)
                ev = sch.op("v", (lambda v, d_=TY3V(0, BF16, 8 * D)[:, c * D + oc * 128: c * D + oc * 128 + 128],
                                  s_=PB(bank, (c // 2) * 128, 128, BF16): v.tensor_copy(d_, s_)),
                            waits=[tt])
                psum_free[bank] = ev
                evs.append(ev)
                wd_state["wt_last"] = tt
            wd_state["done"] = evs[-1]
            if oc == 7:
                p2s["transp"][-1] = wd_state["wt_last"]

        p2drv = {"nf1": 0, "nf2": 0, "nb": 0, "nwd": 0}

        def p2_pump(avail, budget):
            done = 0
            while done < budget:
                if p2drv["nwd"] < 8:
                    emit_wd(p2drv["nwd"]); p2drv["nwd"] += 1; done += 1; continue
                k = p2drv["nb"]
                if (k < NSL and p2drv["nf2"] >= min(k + 1, NSL)
                        and p2drv["nf1"] >= min(k + 2, NSL)):
                    p2_back(k); p2drv["nb"] += 1; done += 1; continue
                k = p2drv["nf2"]
                if k < NSL and p2drv["nf1"] >= k + 1 and p2drv["nb"] >= k:
                    p2_f2(k); p2drv["nf2"] += 1; done += 1; continue
                k = p2drv["nf1"]
                if k < NSL and k <= avail and p2drv["nb"] >= k - 1:
                    p2_f1(k); p2drv["nf1"] += 1; done += 1; continue
                break

        st_p1c["expt_free"] = [two, two]
        st_p1c["expt_free"] = [two, two]
        st_p1c = emit_p1c(1, [v_done1], st_p1c, hook=p2_pump)
        att0_done = st_p1c["atm_free"]
        p2_pump(NSL, 10 ** 9)
        wdt_done = wd_state["done"]
        tg1 = sch.op("t", lambda t: t.nop(), mark=True)
        p2v_last = p2s["ln2"][NSL - 1]

        if stage == "p1c":
            for sl in range(NSL):
                sch.dma("g", "dout", (lambda gg, sl_=sl: gg.dma_start(
                    out=DBG[:, sl_ * 1024: (sl_ + 1) * 1024],
                    in_=ATT0[128 * sl_:128 * sl_ + 128, :])),
                    waits=[("dsp", sch.cnt["dsp"])])
            sch.op("y", lambda y: y.wait_ge(sem_handles["dout"], sch.cnt["dout"]))
            sch.emit(block, sem_handles)
            stack.close()
            return nc

        if stage == "p2":
            for sl in range(NSL):
                sch.dma("g", "dout", (lambda gg, sl_=sl: gg.dma_start(
                    out=DBG[:, sl_ * 1024: (sl_ + 1) * 1024],
                    in_=Y2SP[128 * sl_:128 * sl_ + 128, :])),
                    waits=[("dy2", 16 * NSL)])
            sch.op("y", lambda y: y.wait_ge(sem_handles["dout"], sch.cnt["dout"]))
            sch.emit(block, sem_handles)
            stack.close()
            return nc

        tp2_nop = sch.op("t", lambda t: t.nop(), mark=True)

        def EXL1(tile, a, n):
            return TE[:, tile * QT + a: tile * QT + a + n]

        def MT(var, a, n):
            base = 8192 // 2
            return TE[:, base + var * QT + a: base + var * QT + a + n]

        SIG_RANGE = {0: (0, 128), 1: (0, 256), 2: (0, 384), 3: (128, 384), 4: (256, 384)}

        l_wq1 = None
        for ic in range(8):
            l_wq1 = sch.dma("g", "da", (lambda g, s_=W1Q[128 * ic:128 * ic + 128, :],
                                        d_=TY3V(0, BF16, 8 * D)[:, ic * D: (ic + 1) * D]:
                                        g.dma_start(out=d_, in_=s_)),
                            waits=[tp2_nop] if ic == 0 else [])
        l_wk1 = None
        for ic in range(8):
            l_wk1 = sch.dma("g", "db", (lambda g, s_=W1K[128 * ic:128 * ic + 128, :],
                                        d_=TY3V(49408, BF16, 8 * 128)[:, ic * 128:(ic + 1) * 128]:
                                        g.dma_start(out=d_, in_=s_)),
                            waits=[tp2_nop] if ic == 0 else [])
        l_wv1 = None
        for ic in range(8):
            l_wv1 = sch.dma("g", "db", (lambda g, s_=W1V[128 * ic:128 * ic + 128, :],
                                        d_=TY3V(53504, BF16, 8 * 128)[:, ic * 128:(ic + 1) * 128]:
                                        g.dma_start(out=d_, in_=s_)),
                            waits=[l_wk1] if ic == 0 else [])
        l_b1q = sch.dma("y", "dm", lambda y: y.dma_start(out=BIASB[:, 32:40], in_=B1Q),
                        waits=[dm_chain[-1]])
        l_b1k = sch.dma("y", "dm", lambda y: y.dma_start(out=BIASB[:, 40:41], in_=B1K),
                        waits=[l_b1q])
        l_b1v = sch.dma("y", "dm", lambda y: y.dma_start(out=BIASB[:, 41:42], in_=B1VC),
                        waits=[l_b1k])
        l_mask = sch.dma("y", "dm", (lambda y: y.dma_start(out=MT(0, 0, 5 * QT), in_=MASKS)),
                         waits=[tp2_nop, l_b1v, wdt_done])
        def K1B(a, n, p0=0, np_=128):
            return TY3.bitcast(BF16)[p0:p0 + np_, 32768 // 2 + a: 32768 // 2 + a + n]

        def V1T(slot, a, n):
            return TY3.bitcast(BF16)[:, 37888 // 2 + slot * 65 + a: 37888 // 2 + slot * 65 + a + n]

        def Q1B(c, a, n, p0=0, np_=128):
            return TY3.bitcast(BF16)[p0:p0 + np_, 43264 // 2 + c * QT + a: 43264 // 2 + c * QT + a + n]

        def VFM(a, n):
            return TY3.bitcast(BF16)[:, 57600 // 2 + a: 57600 // 2 + a + n]

        z1 = sch.op("v", lambda v: v.memset(K1B(0, 128), 0.0), waits=[tp2_nop])
        z2 = sch.op("v", lambda v: v.memset(K1B(2432, 128), 0.0))
        V1T3 = TY3.bitcast(BF16)[:, 37888 // 2: 37888 // 2 + 40 * 65].rearrange(
            "p (s c) -> p s c", c=65)
        z3 = sch.op("v", lambda v: v.memset(V1T3[:, 0:2, :], 0.0))
        z4 = sch.op("v", lambda v: v.memset(V1T3[:, 38:40, :], 0.0))
        z5 = sch.op("v", lambda v: v.memset(V1T3[:, :, 64:65], 1.0))

        def YSTG(pb, ic, n):
            return TY3V(16384 + pb * 8192, BF16, 8 * 512)[:, ic * 512: ic * 512 + n]

        kv_done = None
        ystg_free = {0: None, 1: None}
        kv_cast = {}

        def emit_kv_cast(si, extra=()):
            n = 512 if si < 4 else 256
            pb = si % 2
            dls = "dl0" if pb == 0 else "dl1"
            cast_tok = None
            for ic in range(8):
                w = []
                if ic == 0:
                    w = [("dy2", 16 * NSL), tp2_nop, p2v_last] + list(extra)
                cast_tok = sch.dma("y", dls, (lambda y, d_=YSTG(pb, ic, n),
                                              s_=Y2SP[512 * si:512 * si + n,
                                                      128 * ic:128 * ic + 128]:
                                              y.dma_start_transpose(d_, s_)),
                                   waits=w)
            kv_cast[si] = cast_tok

        emit_kv_cast(0)
        emit_kv_cast(1)
        for si in range(5):
            n = 512 if si < 4 else 256
            pb = si % 2
            for which in (0, 1):
                bank = which
                woff = 49408 if which == 0 else 53504
                last = None
                for ic in range(8):
                    w = []
                    if ic == 0:
                        w = [l_wv1, l_b1v, kv_cast[si]]
                        if psum_free.get(bank):
                            w.append(psum_free[bank])
                    last = sch.op("t", (lambda t, o_=PB(bank, 0, n),
                                        l_=TY3V(woff, BF16, 8 * 128)[:, ic * 128:(ic + 1) * 128],
                                        r_=YSTG(pb, ic, n),
                                        s0=(ic == 0), s1=(ic == 7):
                                        t.matmul(o_, l_, r_, start=s0, stop=s1)),
                                  waits=w, mark=(ic == 7))
                if which == 0:
                    ev = sch.op("v", (lambda v, d_=K1B(128 + 512 * si, n), p_=PB(bank, 0, n):
                                      v.tensor_scalar(d_, p_, BIASB[:, 40:41], None, ALU.add)),
                                waits=[last, z2])
                else:
                    ev = sch.op("v", (lambda v, d_=VFM(512 * si, n), p_=PB(bank, 0, n):
                                      v.tensor_scalar(d_, p_, BIASB[:, 41:42], None, ALU.add)),
                                waits=[last])
                psum_free[bank] = ev
                kv_done = ev
            ystg_free[pb] = sch.op("t", lambda t: t.nop(), mark=True)
            if si + 2 < 5:
                emit_kv_cast(si + 2, extra=[ystg_free[pb]])
        vt_done = None
        for sub in range(NSL):
            bank = 4 + (sub % 2)
            tt = sch.op("t", (lambda t, o_=PB(bank, 0, 128, BF16),
                              i_=VFM(sub * 128, 128): t.transpose(o_, i_, IDB)),
                        waits=[kv_done, l_idb] + ([psum_free.get(bank)] if psum_free.get(bank) else []),
                        mark=True)
            slot0 = (sub + 1) * 2
            dst3 = TY3.bitcast(BF16)[:, 37888 // 2 + slot0 * 65: 37888 // 2 + slot0 * 65 + 2 * 65].rearrange(
                "p (s c) -> p s c", c=65)[:, :, 0:64]
            src3 = PB(bank, 0, 128, BF16).rearrange("p (s c) -> p s c", c=64)
            ev = sch.op("v", (lambda v, d_=dst3, s_=src3: v.tensor_copy(d_, s_)),
                        waits=[tt, z5])
            psum_free[bank] = ev
            vt_done = ev

        l_w1o = None
        for ic in range(8):
            l_w1o = sch.dma("g", "da", (lambda g, s_=W1O[128 * ic:128 * ic + 128, :],
                                        d_=TEV(24576, BF16, 8 * D)[:, ic * D:(ic + 1) * D]:
                                        g.dma_start(out=d_, in_=s_)),
                            waits=[l_wq1, tp2_nop, p2v_last] if ic == 0 else [])
        l_b1o = sch.dma("g", "dgm", lambda g: g.dma_start(out=ROWR[0:1, 0:1024], in_=B1O),
                        waits=[t_fbrow, tp2_nop, p2v_last])

        def AF1(p, c, a, n):
            base = p * 3072
            return TY2.bitcast(BF16)[:, base + c * QT + a: base + c * QT + a + n]

        def Y3B(p):
            return TY2.bitcast(BF16)[:, 6144 + p * 1024: 6144 + p * 1024 + 1024]

        Z3AP = TEV(12288, F32, 1024)
        JUNK3 = TEV(16384, F32, 1024)

        p4s = {"oproj": {}, "z3": {}, "spill": {}, "af1": {}, "n": 0}

        def p4_slice(sl):
            p = sl % 2
            g = sl // 3
            pg = g % 2
            if sl % 3 == 0:
                tldA = None
                for c in range(8):
                    w = []
                    if c == 0:
                        w = [("dsp", sch.cnt["dsp"])]
                        if p4s["oproj"].get(3 * (g - 2) + 2):
                            w.append(p4s["oproj"][3 * (g - 2) + 2])
                        if g < 2:
                            w += [tp2_nop, p2v_last]
                    tldA = sch.dma("y", "dw2" if pg == 0 else "dw3",
                                   (lambda y, s_=ATT1[QT * g:QT * g + QT,
                                                      128 * c:128 * c + 128],
                                    d_=AF1(pg, c, 0, QT): y.dma_start_transpose(d_, s_)),
                                   waits=w)
                p4s["af1"][g] = tldA
            tldA = p4s["af1"][g]
            wy = [p4s["z3"][sl - 2]] if sl >= 2 else [("dy2", 16 * NSL), p2v_last]
            ldy2 = sch.dma("g", "dx0" if p == 0 else "dx1",
                           (lambda g, d_=TEV(20480 + p * 2048, BF16, 1024),
                            s_=Y2SP[128 * sl:128 * sl + 128, :]:
                            g.dma_start(out=d_, in_=s_)), waits=wy)
            o_marks = []
            for nh in range(2):
                bank = 4 + nh
                for ic in range(8):
                    w = []
                    if ic == 0:
                        w = [tldA, l_w1o, l_b1o]
                        if psum_free.get(bank):
                            w.append(psum_free[bank])
                    sch.op("t", (lambda t, o_=PB(bank, 0, 512),
                                 l_=AF1(pg, ic, (sl % 3) * 128, 128),
                                 r_=TEV(24576, BF16, 8 * D)[:, ic * D + nh * 512: ic * D + nh * 512 + 512],
                                 s0=(ic == 0): t.matmul(o_, l_, r_, start=s0, stop=False)),
                           waits=w)
                last = sch.op("t", (lambda t, o_=PB(bank, 0, 512),
                                    r_=ROWR[0:1, nh * 512:(nh + 1) * 512]:
                                    t.matmul(o_, ONER, r_, start=False, stop=True)), mark=True)
                o_marks.append(last)
            p4s["oproj"][sl] = o_marks[-1]
            ev3 = None
            for nh in range(2):
                ev3 = sch.op("v", (lambda v, d_=Z3AP[:, nh * 512:(nh + 1) * 512],
                                   a_=PB(4 + nh, 0, 512),
                                   b_=TEV(20480 + p * 2048, BF16, 1024)[:, nh * 512:(nh + 1) * 512]:
                                   v.tensor_tensor(d_, a_, b_, ALU.add)),
                             waits=[o_marks[nh], ldy2])
                psum_free[4 + nh] = ev3
            p4s["z3"][sl] = ev3
            t3s = sch.op("v", (lambda v: v.tensor_scalar(JUNK3, Z3AP, 1.0, 0.0,
                                                         ALU.mult, ALU.add,
                                                         accum_out=SMALL[:, 70:71])),
                         waits=[ev3])
            t3q = sch.op("s", (lambda ss: ss.activation(JUNK3, Z3AP, AF.Square,
                                                        accum_out=SMALL[:, 71:72])),
                         waits=[t3s])
            t3m = sch.op("v", lambda v: v.tensor_scalar(SMALL[:, 72:73], SMALL[:, 70:71],
                                                        -1.0 / 1024.0, None, ALU.mult))
            t3m2 = sch.op("v", lambda v: v.scalar_tensor_tensor(SMALL[:, 73:74], SMALL[:, 72:73],
                                                                1.0, SMALL[:, 72:73],
                                                                ALU.mult, ALU.mult))
            t3v = sch.op("v", lambda v: v.scalar_tensor_tensor(SMALL[:, 74:75], SMALL[:, 71:72],
                                                               1.0 / 1024.0, SMALL[:, 73:74],
                                                               ALU.mult, ALU.subtract),
                         waits=[t3q])
            t3l = sch.op("s", lambda ss: ss.activation(SMALL[:, 74:75], SMALL[:, 74:75], AF.Ln,
                                                       bias=BIASB[:, 61:62]), waits=[t3v])
            t3e = sch.op("s", lambda ss: ss.activation(SMALL[:, 74:75], SMALL[:, 74:75], AF.Exp,
                                                       scale=-0.5))
            wa = [t3e]
            if sl >= 2:
                wa.append(p4s["spill"][sl - 2])
            t3a = sch.op("v", (lambda v, y_=Y3B(p): v.tensor_scalar(y_, Z3AP,
                                                          SMALL[:, 72:73], SMALL[:, 74:75],
                                                          ALU.add, ALU.mult)),
                         waits=wa)
            wsp = [t3a]
            if sl >= 1:
                wsp.append(p4s["spill"][sl - 1])
            p4s["spill"][sl] = sch.dma("y", "dy3", (lambda y, r0=128 * sl, y_=Y3B(p):
                                                    y.dma_start(out=Y3SP[r0:r0 + 128, :],
                                                                in_=y_)),
                                       waits=wsp)

        def p4_pump(avail, budget):
            done = 0
            while done < budget and p4s["n"] < NSL and p4s["n"] <= avail:
                p4_slice(p4s["n"]); p4s["n"] += 1; done += 1

        st1 = {"exp_prev": {0: None, 1: None, 2: None}, "pv_norm": None,
               "b6_free": None, "atm_free": None, "expt_free": None}
        q_cast = {}

        def emit_q_cast(qt, extra=()):
            pb = qt % 2
            dls = "dl0" if pb == 0 else "dl1"
            tok = None
            for ic in range(8):
                w = list(extra) if ic == 0 else []
                tok = sch.dma("y", dls, (lambda y, d_=YSTG(pb, ic, QT),
                                         s_=Y2SP[qt * QT:(qt + 1) * QT,
                                                 128 * ic:128 * ic + 128]:
                                         y.dma_start_transpose(d_, s_)),
                              waits=w)
            q_cast[qt] = tok

        emit_q_cast(0, [ystg_free[0], vt_done])
        emit_q_cast(1, [ystg_free[1], vt_done])
        qfree = {}
        for qt in range(NQT):
            qcast = q_cast[qt]
            q_marks = []
            for oc in range(8):
                bank = oc % 2
                last = None
                for ic in range(8):
                    w = []
                    if ic == 0:
                        w = [l_wq1, l_b1q, vt_done, qcast]
                        if psum_free.get(bank):
                            w.append(psum_free[bank])
                    last = sch.op("t", (lambda t, o_=PB(bank, 0, QT),
                                        l_=TY3V(0, BF16, 8 * D)[:, ic * D + oc * 128: ic * D + oc * 128 + 128],
                                        r_=YSTG(qt % 2, ic, QT),
                                        s0=(ic == 0), s1=(ic == 7):
                                        t.matmul(o_, l_, r_, start=s0, stop=s1)),
                                  waits=w, mark=(ic == 7))
                ev = sch.op("v", (lambda v, d_=Q1B(oc, 0, QT), p_=PB(bank, 0, QT),
                                  b_=BIASB[:, 32 + oc:33 + oc]:
                                  v.tensor_scalar(d_, p_, b_, None, ALU.add)),
                            waits=[last] + ([st1["pv_norm"]] if st1["pv_norm"] else []))
                psum_free[bank] = ev
                q_marks.append(ev)
            q_ready = q_marks[-1]
            qfree[qt] = sch.op("t", lambda t: t.nop(), mark=True)
            if qt + 2 < NQT:
                emit_q_cast(qt + 2, [qfree[qt]])
            for hp in range(8):
                exp_toks = {}
                for sg in range(5):
                    sb = qt * 3 + sg
                    lo, hi = SIG_RANGE[sg]
                    w_ = hi - lo
                    par = sg % 2
                    set_base = 2 * par
                    fw = [q_ready, l_mask]
                    if st1["exp_prev"][par]:
                        fw.append(st1["exp_prev"][par])
                    last = None
                    for hh in (0, 1):
                        bank = set_base + hh
                        lhsT = K1B(sb * 128, 128, p0=64 * hh, np_=64)
                        rhs = Q1B(hp, lo, w_, p0=64 * hh, np_=64)
                        ww = fw if hh == 0 else []
                        last = sch.op("t", (lambda t, o_=PB(bank, lo, w_), l_=lhsT, r_=rhs,
                                            tp=(64 * hh, 0):
                                            t.matmul(o_, l_, r_, start=True, stop=True,
                                                     tile_position=tp)),
                                      waits=ww, mark=(hh == 1))
                    ew = [last]
                    if st1["expt_free"]:
                        ew.append(st1["expt_free"])
                    in_ap = PALL[:, set_base * 512 + lo: set_base * 512 + lo + 1024].rearrange(
                        "p (b x) -> p b x", x=512)[:, :, 0:w_]
                    out_ap = TE[:, (sg * 2) * QT + lo: (sg * 2) * QT + lo + w_] if False else None
                    out3 = TE[:, 0:2 * QT].rearrange("p (b x) -> p b x", x=QT)[
                        :, 0:2, 0:1]
                    o3 = TE[:, (sg * 2) * QT + lo: (sg * 2 + 1) * QT + lo + w_] if False else None
                    out_ap = TE[:, (sg * 2) * QT: (sg * 2 + 2) * QT].rearrange(
                        "p (b x) -> p b x", x=QT)[:, :, lo:lo + w_]
                    etok = sch.op("s", (lambda ss, o_=out_ap, i_=in_ap:
                                        ss.activation(o_, i_, AF.Exp,
                                                      bias=BIASB[:, 60:61], scale=1.0)),
                                  waits=ew)
                    st1["exp_prev"][par] = etok
                    mtok = None
                    for hh in (0, 1):
                        mtok = sch.op("v", (lambda v, d_=EXL1(sg * 2 + hh, lo, w_),
                                            m_=MT(sg, lo, w_):
                                            v.tensor_tensor(d_, d_, m_, ALU.mult)),
                                      waits=[etok])
                    exp_toks[sg] = mtok
                pv_first = True
                pv_last = None
                for qs in (0, 1, 2):
                    for hh in (0, 1):
                        for sg in (qs, qs + 1, qs + 2):
                            w = [exp_toks[sg]]
                            if pv_first and st1["b6_free"]:
                                w.append(st1["b6_free"])
                            lhsT = EXL1(sg * 2 + hh, qs * 128, 128)
                            rhs = V1T((qt * 3 + sg) * 2 + hh, 0, 65)
                            out = PB(6, (hh * 3 + qs) * 65, 65)
                            pv_last = sch.op("t", (lambda t, o_=out, l_=lhsT, r_=rhs,
                                                   s0=pv_first:
                                                   t.matmul(o_, l_, r_, start=s0, stop=False,
                                                            skip_group_check=True)),
                                             waits=w,
                                             mark=(qs == 2 and hh == 1 and sg == qs + 2))
                            pv_first = False
                st1["expt_free"] = pv_last
                dens3 = PB(6, 0, 390).rearrange("p (s c) -> p s c", c=65)[:, :, 64:65]
                t2 = sch.op("v", (lambda v: v.reciprocal(SMALL[:, 86:92], dens3)),
                            waits=[pv_last])
                ntok = None
                for hh in (0, 1):
                    for qs in (0, 1, 2):
                        idx = hh * 3 + qs
                        hcol = (hp + 8 * hh) * 64
                        dst = ATMv(qs * 1024 + hcol, 64)
                        w = [t2]
                        if st1["atm_free"]:
                            w.append(st1["atm_free"])
                        ntok = sch.op("v", (lambda v, d_=dst,
                                            s_=PB(6, idx * 65, 64),
                                            r_=SMALL[:, 86 + idx: 87 + idx]:
                                            v.tensor_scalar(d_, s_, r_, None, ALU.mult)),
                                      waits=w)
                st1["pv_norm"] = ntok
                st1["b6_free"] = ntok
                p4_pump(3 * qt - 1, 1)
            sp = None
            for qs in range(3):
                sp = sch.dma("y", "dsp", (lambda y, r0=qt * QT + qs * 128,
                                          a_=ATMv(qs * 1024, 1024):
                                          y.dma_start(out=ATT1[r0:r0 + 128, :], in_=a_)),
                             waits=[st1["pv_norm"]])
            st1["atm_free"] = sp
            p4_pump(3 * qt + 2, 2)
        att1_done = st1["atm_free"]
        p4_pump(NSL, 10 ** 9)

        if stage == "p3":
            for sl in range(NSL):
                sch.dma("g", "dout", (lambda gg, sl_=sl: gg.dma_start(
                    out=DBG[:, sl_ * 1024: (sl_ + 1) * 1024],
                    in_=ATT1[128 * sl_:128 * sl_ + 128, :])),
                    waits=[("dsp", sch.cnt["dsp"])])
            sch.op("y", lambda y: y.wait_ge(sem_handles["dout"], sch.cnt["dout"]))
            sch.emit(block, sem_handles)
            stack.close()
            return nc

        tp3_nop = sch.op("t", lambda t: t.nop(), mark=True)

        zc = sch.op("v", lambda v: v.memset(SMALL[:, 166:167], 0.0))
        RLG = [(0, 512), (512, 512), (1024, 512), (1536, 512), (2048, 256)]
        rl_grp = {}
        for gi, (a0, ng) in enumerate(RLG):
            dl = "dl0" if gi % 2 == 0 else "dl1"
            tok = None
            for c in range(8):
                w = []
                if c == 0:
                    w = [("dy3", 16 * ((a0 + ng) // 128)), tp3_nop, st1["pv_norm"]]
                    if gi >= 2:
                        w.append(rl_grp[gi - 2])
                tok = sch.dma("y", dl, (lambda y, d_=Y3C(c, a0, ng),
                                        s_=Y3SP[a0:a0 + ng, 128 * c:128 * c + 128]:
                                        y.dma_start_transpose(d_, s_)), waits=w)
            rl_grp[gi] = tok

        if stage == "p4":
            for sl in range(NSL):
                sch.dma("g", "dout", (lambda gg, sl_=sl: gg.dma_start(
                    out=DBG[:, sl_ * 1024: (sl_ + 1) * 1024],
                    in_=Y3SP[128 * sl_:128 * sl_ + 128, :])),
                    waits=[("dy3", 16 * NSL)])
            sch.op("y", lambda y: y.wait_ge(sem_handles["dout"], sch.cnt["dout"]))
            sch.emit(block, sem_handles)
            stack.close()
            return nc

        tp4_nop = sch.op("t", lambda t: t.nop(), mark=True)

        l_mur = sch.dma("y", "dm", lambda y: y.dma_start(out=BIASB[:, 42:50], in_=MUR),
                        waits=[l_mask])
        l_muk = sch.dma("y", "dm", lambda y: y.dma_start(out=BIASB[:, 50:58], in_=MUK),
                        waits=[l_mur])
        l_wr = None
        for ic in range(8):
            l_wr = sch.dma("g", "da", (lambda g, s_=WRT[128 * ic:128 * ic + 128, :],
                                       d_=TY2V(0, BF16, 8 * D)[:, ic * D:(ic + 1) * D]:
                                       g.dma_start(out=d_, in_=s_)),
                           waits=[tp4_nop, ("dy3", 16 * NSL), l_w1o] if ic == 0 else [])
        TILES = [(0, 512), (512, 512), (1024, 512), (1536, 512), (2048, 256)]

        def rl_need(ti):
            return [rl_grp[ti]] + ([rl_grp[ti - 1]] if ti >= 1 else [])
        stash_tok = None
        for ti in range(1, 5):
            a = TILES[ti][0]
            for c in range(8):
                stash_tok = sch.op("v", (lambda v, d_=SMALL[:, 96 + (ti - 1) * 8 + c: 97 + (ti - 1) * 8 + c],
                                         s_=Y3C(c, a - 1, 1): v.tensor_copy(d_, s_)),
                                   waits=[rl_grp[ti - 1]] if c == 0 else [])
        r_done = None
        xr_free = None
        for ti, (a, n) in enumerate(TILES):
            xr_toks = []
            for ic in range(8):
                if ti == 0:
                    td = sch.op("v", (lambda v, d_=TEV(36864, F32, 512)[:, 1:n],
                                      p_=Y3C(ic, 1, n - 1), q_=Y3C(ic, 0, n - 1):
                                      v.tensor_tensor(d_, p_, q_, ALU.subtract)),
                                waits=rl_need(ti) + [zc, l_muk] + ([xr_free] if xr_free else []))
                    td2 = sch.op("v", (lambda v, d_=TEV(36864, F32, 512)[:, 0:1],
                                       p_=Y3C(ic, 0, 1), q_=SMALL[:, 166:167]:
                                       v.tensor_tensor(d_, p_, q_, ALU.subtract)))
                    tx = sch.op("v", (lambda v, d_=TY2V(32768, BF16, 8 * 512)[:, ic * 512 + 1: ic * 512 + n],
                                      s_=TEV(36864, F32, 512)[:, 1:n], m_=BIASB[:, 42 + ic:43 + ic],
                                      q_=Y3C(ic, 0, n - 1):
                                      v.scalar_tensor_tensor(d_, s_, m_, q_, ALU.mult, ALU.add)))
                    tx = sch.op("v", (lambda v, d_=TY2V(32768, BF16, 8 * 512)[:, ic * 512: ic * 512 + 1],
                                      s_=TEV(36864, F32, 512)[:, 0:1], m_=BIASB[:, 42 + ic:43 + ic],
                                      q_=SMALL[:, 166:167]:
                                      v.scalar_tensor_tensor(d_, s_, m_, q_, ALU.mult, ALU.add)))
                else:
                    td = sch.op("v", (lambda v, d_=TEV(36864, F32, 512)[:, 0:n],
                                      p_=Y3C(ic, a, n), q_=Y3C(ic, a - 1, n):
                                      v.tensor_tensor(d_, p_, q_, ALU.subtract)),
                                waits=rl_need(ti) + [zc, l_muk] + ([xr_free] if xr_free else []))
                    tx = sch.op("v", (lambda v, d_=TY2V(32768, BF16, 8 * 512)[:, ic * 512: ic * 512 + n],
                                      s_=TEV(36864, F32, 512)[:, 0:n], m_=BIASB[:, 42 + ic:43 + ic],
                                      q_=Y3C(ic, a - 1, n):
                                      v.scalar_tensor_tensor(d_, s_, m_, q_, ALU.mult, ALU.add)))
                xr_toks.append(tx)
            for oc in range(8):
                bank = oc % 2
                last = None
                for ic in range(8):
                    w = []
                    if ic == 0:
                        w = [xr_toks[-1], l_wr]
                        if psum_free.get(bank):
                            w.append(psum_free[bank])
                    last = sch.op("t", (lambda t, o_=PB(bank, 0, n),
                                        l_=TY2V(0, BF16, 8 * D)[:, ic * D + oc * 128: ic * D + oc * 128 + 128],
                                        r_=TY2V(32768, BF16, 8 * 512)[:, ic * 512: ic * 512 + n],
                                        s0=(ic == 0), s1=(ic == 7):
                                        t.matmul(o_, l_, r_, start=s0, stop=s1)),
                                  waits=w, mark=(ic == 7))
                ev = sch.op("s", (lambda ss, d_=TE[:, oc * SPAN + a: oc * SPAN + a + n],
                                  p_=PB(bank, 0, n):
                                  ss.activation(d_, p_, AF.Sigmoid)),
                            waits=[last])
                psum_free[bank] = ev
                r_done = ev
            xr_free = sch.op("t", lambda t: t.nop(), mark=True)
        l_wk = None
        for ic in range(8):
            l_wk = sch.dma("g", "db", (lambda g, s_=WKT[128 * ic:128 * ic + 128, :],
                                       d_=TY2V(0, BF16, 8 * D)[:, ic * D:(ic + 1) * D]:
                                       g.dma_start(out=d_, in_=s_)),
                           waits=[r_done] if ic == 0 else [])
        l_wv = None
        for ic in range(8):
            l_wv = sch.dma("g", "db", (lambda g, s_=WKT[0:1, 0:1] if False else WVT[128 * ic:128 * ic + 128, :],
                                       d_=TY2V(49152, BF16, 8 * D)[:, ic * D:(ic + 1) * D]:
                                       g.dma_start(out=d_, in_=s_)),
                           waits=[l_wk] if ic == 0 else [])
        out_tok = None
        xk_free = None
        y4_free = [None]
        for ti, (a, n) in enumerate(TILES):
            xk_toks = []
            for ic in range(8):
                dslot = TEV(36864, F32, 512)
                std = (SMALL[:, 166:167] if ti == 0 else
                       SMALL[:, 96 + (ti - 1) * 8 + ic: 97 + (ti - 1) * 8 + ic])
                td = sch.op("v", (lambda v, d_=dslot[:, 1:n],
                                  p_=Y3C(ic, 1 + a, n - 1), q_=Y3C(ic, a, n - 1):
                                  v.tensor_tensor(d_, p_, q_, ALU.subtract)),
                            waits=[l_wk, l_muk, stash_tok] + ([xk_free] if xk_free else [])
                                  + ([y4_free[0]] if y4_free[0] else []))
                td2 = sch.op("v", (lambda v, d_=dslot[:, 0:1],
                                   p_=Y3C(ic, a, 1), q_=std:
                                   v.tensor_tensor(d_, p_, q_, ALU.subtract)))
                tx = sch.op("v", (lambda v, d_=TY2V(32768, BF16, 8 * 512)[:, ic * 512 + 1: ic * 512 + n],
                                  s_=dslot[:, 1:n], m_=BIASB[:, 50 + ic:51 + ic],
                                  q_=Y3C(ic, a, n - 1):
                                  v.scalar_tensor_tensor(d_, s_, m_, q_, ALU.mult, ALU.add)))
                tx = sch.op("v", (lambda v, d_=TY2V(32768, BF16, 8 * 512)[:, ic * 512: ic * 512 + 1],
                                  s_=dslot[:, 0:1], m_=BIASB[:, 50 + ic:51 + ic],
                                  q_=std: v.scalar_tensor_tensor(d_, s_, m_, q_, ALU.mult, ALU.add)))
                xk_toks.append(tx)
            kact_done = None
            for oc in range(8):
                bank = oc % 2
                last = None
                for ic in range(8):
                    w = []
                    if ic == 0:
                        w = [xk_toks[-1]]
                        if psum_free.get(bank):
                            w.append(psum_free[bank])
                    last = sch.op("t", (lambda t, o_=PB(bank, 0, n),
                                        l_=TY2V(0, BF16, 8 * D)[:, ic * D + oc * 128: ic * D + oc * 128 + 128],
                                        r_=TY2V(32768, BF16, 8 * 512)[:, ic * 512: ic * 512 + n],
                                        s0=(ic == 0), s1=(ic == 7):
                                        t.matmul(o_, l_, r_, start=s0, stop=s1)),
                                  waits=w, mark=(ic == 7))
                trl = sch.op("v", (lambda v, d_=TEV(38912, F32, 512)[:, 0:n], p_=PB(bank, 0, n):
                                   v.tensor_scalar(d_, p_, 0.0, None, ALU.max)),
                             waits=[last])
                tsq2 = sch.op("v", (lambda v, d_=TY2V(65536, BF16, 8 * 512)[:, oc * 512: oc * 512 + n],
                                    s_=TEV(38912, F32, 512)[:, 0:n]:
                                    v.tensor_tensor(d_, s_, s_, ALU.mult)))
                psum_free[bank] = tsq2
                kact_done = tsq2
            xk_free = sch.op("t", lambda t: t.nop(), mark=True)
            z4_done = None
            for oc in range(8):
                bank = 2 + (oc % 2)
                last = None
                for ic in range(8):
                    w = []
                    if ic == 0:
                        w = [kact_done, l_wv]
                        if psum_free.get(bank):
                            w.append(psum_free[bank])
                    last = sch.op("t", (lambda t, o_=PB(bank, 0, n),
                                        l_=TY2V(49152, BF16, 8 * D)[:, ic * D + oc * 128: ic * D + oc * 128 + 128],
                                        r_=TY2V(65536, BF16, 8 * 512)[:, ic * 512: ic * 512 + n],
                                        s0=(ic == 0), s1=(ic == 7):
                                        t.matmul(o_, l_, r_, start=s0, stop=s1)),
                                  waits=w, mark=(ic == 7))
                to1 = sch.op("v", (lambda v, d_=TEV(38912, F32, 512)[:, 0:n], p_=PB(bank, 0, n),
                                   r_=TE[:, oc * SPAN + a: oc * SPAN + a + n]:
                                   v.tensor_tensor(d_, p_, r_, ALU.mult)),
                             waits=[last])
                tz4 = sch.op("v", (lambda v, d_=Y3C(oc, a, n),
                                   s_=TEV(38912, F32, 512)[:, 0:n]:
                                   v.tensor_tensor(d_, d_, s_, ALU.add)))
                psum_free[bank] = tz4
                z4_done = tz4
            for ls in range(n // 128):
                tok0 = a + ls * 128
                tts = []
                for c in range(8):
                    bank = 4
                    tt = sch.op("t", (lambda t, o_=PB(bank, c * 128, 128, BF16),
                                      i_=Y3C(c, tok0, 128):
                                      t.transpose(o_, i_, IDB)),
                                waits=[z4_done] + ([psum_free[bank]] if psum_free.get(bank) else []),
                                mark=True)
                    tts.append(tt)
                    psum_free[bank] = None
                z4p = PB(4, 0, 1024, BF16)
                junk = TY3V(40960, F32, 1024)
                t4s = sch.op("v", (lambda v, z_=z4p, j_=junk: v.tensor_scalar(
                    j_, z_, 1.0, 0.0, ALU.mult, ALU.add, accum_out=SMALL[:, 70:71])),
                             waits=[tts[-1]] + ([psum_free[5]] if psum_free.get(5) else []))
                t4q = sch.op("s", (lambda ss, z_=z4p, j_=junk: ss.activation(
                    j_, z_, AF.Square, accum_out=SMALL[:, 71:72])),
                             waits=[t4s])
                t4m = sch.op("v", lambda v: v.tensor_scalar(SMALL[:, 72:73], SMALL[:, 70:71],
                                                            -1.0 / 1024.0, None, ALU.mult),
                             waits=[t4q])
                t4m2 = sch.op("v", lambda v: v.scalar_tensor_tensor(SMALL[:, 73:74], SMALL[:, 72:73],
                                                                    1.0, SMALL[:, 72:73],
                                                                    ALU.mult, ALU.mult))
                t4v = sch.op("v", lambda v: v.scalar_tensor_tensor(SMALL[:, 74:75], SMALL[:, 71:72],
                                                                   1.0 / 1024.0, SMALL[:, 73:74],
                                                                   ALU.mult, ALU.subtract))
                t4l = sch.op("s", lambda ss: ss.activation(SMALL[:, 74:75], SMALL[:, 74:75], AF.Ln,
                                                           bias=BIASB[:, 61:62]), waits=[t4v])
                t4e = sch.op("s", lambda ss: ss.activation(SMALL[:, 74:75], SMALL[:, 74:75], AF.Exp,
                                                           scale=-0.5))
                y4ap = TEV(36864, F32, 1024)
                t4a = sch.op("v", (lambda v, z_=z4p, y_=y4ap: v.tensor_scalar(
                    y_, z_, SMALL[:, 72:73], SMALL[:, 74:75], ALU.add, ALU.mult)),
                             waits=[t4e, y4_free[0]] if y4_free[0] else [t4e])
                for bank in (4, 5):
                    psum_free[bank] = t4a
                out_tok = sch.dma("y", "dout", (lambda y, r0=tok0, z_=y4ap:
                                                y.dma_start(out=YOUT[r0:r0 + 128, :], in_=z_)),
                                  waits=[t4a])
                y4_free[0] = out_tok

        sch.op("y", lambda y: y.wait_ge(sem_handles["dout"], sch.cnt["dout"]))

        sch.emit(block, sem_handles)
        stack.close()
    return nc


def make_inputs(core, inp):
    b = core // 2
    half = core % 2
    s0 = 0 if half == 0 else SEQ - SPAN
    x = np.asarray(inp["L0_attn_x"][b], np.float32)
    xT = np.ascontiguousarray(x.T)
    perm = np.array([[h, h + 8] for h in range(8)], np.int64).reshape(-1)
    colperm = (perm[:, None] * 64 + np.arange(64)[None, :]).reshape(-1)
    wq = np.asarray(inp["L1_attn_Wq"], np.float32)
    bq = np.asarray(inp["L1_attn_bq"], np.float32)
    i = np.arange(128)[:, None]
    j = np.arange(QT)[None, :]
    masks = np.zeros((128, 5 * QT), np.float32)
    masks[:, 0 * QT:0 * QT + 128] = (i >= j[:, 0:128]).astype(np.float32)
    masks[:, 1 * QT:1 * QT + 256] = ((j[:, 0:256] - i) <= 128).astype(np.float32)
    masks[:, 2 * QT:3 * QT] = (((j - i) >= 1) & ((j - i) <= 256)).astype(np.float32)
    masks[:, 3 * QT + 128:4 * QT] = ((j[:, 128:QT] - i) >= 129).astype(np.float32)
    masks[:, 4 * QT + 256:5 * QT] = ((j[:, 256:QT] - i) >= 257).astype(np.float32)
    d = {
        "XSF": np.ascontiguousarray(xT[:, s0:s0 + SPAN]),
        "XFF": xT,
        "XST": np.ascontiguousarray(x[s0:s0 + SPAN]),
        "QW": np.asarray(inp["L0_attn_qW"], np.float32),
        "QBI": np.asarray(inp["L0_attn_qb"], np.float32),
        "KW": np.asarray(inp["L0_attn_kW"], np.float32),
        "KBI": np.asarray(inp["L0_attn_kb"], np.float32),
        "VW": np.asarray(inp["L0_attn_vW"], np.float32),
        "VBI": np.asarray(inp["L0_attn_vb"], np.float32),
        "OW": np.asarray(inp["L0_attn_oW"], np.float32),
        "OBI": np.asarray(inp["L0_attn_ob"], np.float32),
        "JV": np.asarray(inp["L0_attn_J"], np.float32).reshape(8, 128),
        "FW": np.asarray(inp["L0_ffn_weight"], np.float32),
        "FB": np.asarray(inp["L0_ffn_bias"], np.float32).reshape(1, D),
        "W1Q": np.ascontiguousarray(wq[:, colperm]),
        "B1Q": np.ascontiguousarray(bq[colperm]).reshape(8, 128).T.copy(),
        "W1K": np.asarray(inp["L1_attn_Wk"], np.float32),
        "B1K": np.asarray(inp["L1_attn_bk"], np.float32).reshape(128, 1),
        "W1V": np.asarray(inp["L1_attn_Wv"], np.float32),
        "B1VC": np.asarray(inp["L1_attn_bv"], np.float32).reshape(128, 1),
        "W1O": np.asarray(inp["L1_attn_Wo"], np.float32),
        "B1O": np.asarray(inp["L1_attn_bo"], np.float32).reshape(1, D),
        "WRT": np.ascontiguousarray(np.asarray(inp["L1_ffn_W_r"], np.float32).T),
        "WKT": np.ascontiguousarray(np.asarray(inp["L1_ffn_W_k"], np.float32).T),
        "WVT": np.ascontiguousarray(np.asarray(inp["L1_ffn_W_v"], np.float32).T),
        "MUR": np.asarray(inp["L1_ffn_mu_r"], np.float32).reshape(8, 128).T.copy(),
        "MUK": np.asarray(inp["L1_ffn_mu_k"], np.float32).reshape(8, 128).T.copy(),
        "MASKS": masks.astype(ml_dtypes.bfloat16),
        "IDENT": np.eye(128, dtype=np.float32),
    }
    return d


def kernel(**inputs):
    nc = build_program()
    in_maps = [make_inputs(c, inputs) for c in range(8)]
    res = run_bass_kernel_spmd(nc, in_maps, core_ids=list(range(8)))
    out = np.empty((4, SEQ, D), np.float32)
    for c in range(8):
        b, half = c // 2, c % 2
        v0 = 0 if half == 0 else SPAN - 2048
        out[b, half * 2048:(half + 1) * 2048] = res.results[c]["YOUT"][v0:v0 + 2048]
    return out


# version 13
# speedup vs baseline: 1.1799x; 1.1799x over previous
import numpy as np
import ml_dtypes
import concourse.bass as bass
import concourse.mybir as mybir
from concourse.bass_utils import run_bass_kernel_spmd

F32 = mybir.dt.float32
BF16 = mybir.dt.bfloat16
F32R = mybir.dt.float32r
AF = mybir.ActivationFunctionType
ALU = mybir.AluOpType
AX = mybir.AxisListType

MAGIC = float(np.float32(1.5 * 2 ** 23))
D = 1024
SEQ = 4096
SPAN = 2304
NSL = SPAN // 128
NSUB = SEQ // 128
QT = 384
NQT = SPAN // QT

QUAT_TAB = {
    (0, 0): (0, 1), (0, 1): (1, -1), (0, 2): (2, -1), (0, 3): (3, -1),
    (1, 0): (1, 1), (1, 1): (0, 1), (1, 2): (3, -1), (1, 3): (2, 1),
    (2, 0): (2, 1), (2, 1): (3, 1), (2, 2): (0, 1), (2, 3): (1, -1),
    (3, 0): (3, 1), (3, 1): (2, -1), (3, 2): (1, 1), (3, 3): (0, 1),
}
BIAS_SGN = np.array([[1, -1, -1, -1], [1, 1, 1, -1], [1, -1, 1, 1], [1, 1, -1, 1]],
                    np.float32)

ENGS = ("t", "v", "s", "g", "y")
DMA_SEMS = ("da", "db", "dw0", "dw1", "dw2", "dw3", "dm", "dgm", "dsp", "dy2", "dy3", "dwd", "dl0", "dl1", "dx0", "dx1", "dxa", "dxb", "dout")


class Sched:
    def __init__(self):
        self.ops = {e: [] for e in ENGS}
        self.cnt = {n: 0 for n in ENGS + DMA_SEMS}

    def op(self, eng, fn, waits=(), mark=False):
        waits = list(waits)
        if eng in ("v", "s"):
            mark = True
        if mark:
            waits.append((eng, self.cnt[eng]))
        best = {}
        for (sn, sv) in waits:
            if sv > 0 and sv > best.get(sn, 0):
                best[sn] = sv
        self.ops[eng].append((tuple(best.items()), fn, eng if mark else None, 1))
        if mark:
            self.cnt[eng] += 1
        return (eng, self.cnt[eng])

    def dma(self, eng, sem, fn, waits=()):
        best = {}
        for (sn, sv) in waits:
            if sv > 0 and sv > best.get(sn, 0):
                best[sn] = sv
        self.ops[eng].append((tuple(best.items()), fn, sem, 16))
        self.cnt[sem] += 16
        return (sem, self.cnt[sem])

    def emit(self, block, sems):
        def run(engobj, eng):
            for waits, fn, incsem, incval in self.ops[eng]:
                for (sn, sv) in waits:
                    engobj.wait_ge(sems[sn], sv)
                inst = fn(engobj)
                if incsem is not None:
                    inst.then_inc(sems[incsem], incval)

        @block.tensor
        def _(t):
            run(t, "t")

        @block.vector
        def _(v):
            run(v, "v")

        @block.scalar
        def _(s):
            run(s, "s")

        @block.gpsimd
        def _(g):
            run(g, "g")

        @block.sync
        def _(y):
            run(y, "y")


def build_program(stage="full"):
    nc = bass.Bass("TRN2", target_bir_lowering=False)

    def din(name, shape, dt=F32):
        return nc.dram_tensor(name, shape, dt, kind="ExternalInput").ap()

    XSF = din("XSF", [D, SPAN])
    XFF = din("XFF", [D, SEQ])
    XST = din("XST", [SPAN, D])
    QW = din("QW", [4, 256, 256]); QBI = din("QBI", [4, 256])
    KW = din("KW", [4, 256, 256]); KBI = din("KBI", [4, 256])
    VW = din("VW", [4, 256, 256]); VBI = din("VBI", [4, 256])
    OW = din("OW", [4, 256, 256]); OBI = din("OBI", [4, 256])
    JV = din("JV", [8, 128])
    FW = din("FW", [D, D]); FB = din("FB", [1, D])
    W1Q = din("W1Q", [D, D]); B1Q = din("B1Q", [128, 8])
    W1K = din("W1K", [D, 128]); B1K = din("B1K", [128, 1])
    W1V = din("W1V", [D, 128]); B1VC = din("B1VC", [128, 1])
    W1O = din("W1O", [D, D]); B1O = din("B1O", [1, D])
    WRT = din("WRT", [D, D]); WKT = din("WKT", [D, D]); WVT = din("WVT", [D, D])
    MUR = din("MUR", [128, 8]); MUK = din("MUK", [128, 8])
    MASKS = din("MASKS", [128, 5 * QT], BF16)
    IDENT = din("IDENT", [128, 128])

    ATT0 = nc.dram_tensor("ATT0", [SPAN, D], BF16, kind="Internal").ap()
    Y2SP = nc.dram_tensor("Y2SP", [SPAN, D], BF16, kind="Internal").ap()
    Y3SP = nc.dram_tensor("Y3SP", [SPAN, D], BF16, kind="Internal").ap()
    ATT1 = nc.dram_tensor("ATT1", [SPAN, D], BF16, kind="Internal").ap()
    YOUT = nc.dram_tensor("YOUT", [SPAN, D], F32, kind="ExternalOutput").ap()
    DBG = nc.dram_tensor("DBG", [128, 8 * SPAN], F32, kind="ExternalOutput").ap()

    sch = Sched()

    import contextlib
    stack = contextlib.ExitStack()
    with (
        nc.sbuf_tensor("TY2", [128, 18440], F32) as TY2t,
        nc.sbuf_tensor("TY3", [128, 18440], F32) as TY3t,
        nc.sbuf_tensor("TE", [128, 20480], BF16) as TEt,
        nc.sbuf_tensor("ROWB", [1, 1280], F32) as ROWB_t,
        nc.sbuf_tensor("ROWR", [1, 2048], F32R) as ROWR_t,
        nc.sbuf_tensor("BIASB", [128, 64], F32) as BIASB_t,
        nc.sbuf_tensor("ONER", [1, 128], F32R) as ONER_t,
        nc.sbuf_tensor("ONEF", [1, 128], F32) as ONEF_t,
        nc.sbuf_tensor("SMALL", [128, 192], F32) as SMALL_t,
        nc.sbuf_tensor("PVAB", [128, 408], F32) as PVA_t,
        nc.sbuf_tensor("IDF", [128, 128], F32) as IDF_t,
        nc.sbuf_tensor("IDB", [128, 128], BF16) as IDB_t,
        nc.psum_tensor("PALL", [128, 4096], F32) as PALL_t,
        nc.Block() as block,
    ):
        sem_handles = {}
        for n in ENGS + DMA_SEMS:
            sem_handles[n] = stack.enter_context(nc.semaphore("SEM_" + n))

        TY2 = TY2t.ap(); TY3 = TY3t.ap(); TE = TEt.ap()
        ROWB = ROWB_t.ap(); ROWR = ROWR_t.ap(); BIASB = BIASB_t.ap()
        ONER = ONER_t.ap(); ONEF = ONEF_t.ap(); SMALL = SMALL_t.ap()
        PVA = PVA_t.ap(); PALL = PALL_t.ap(); IDF = IDF_t.ap(); IDB = IDB_t.ap()

        def QBUF(c, a, n, p0=0, np_=128):
            return TY2.bitcast(BF16)[p0:p0 + np_, c * SPAN + a: c * SPAN + a + n]

        def VTB(slot, a, n, p0=0, np_=128):
            base = 36864 // 2
            return TY2.bitcast(BF16)[p0:p0 + np_, base + slot * 65 + a: base + slot * 65 + a + n]

        def XSG(p, c, a, n):
            base = (36864 + p * 8192) // 2
            return TY2.bitcast(BF16)[:, base + c * 512 + a: base + c * 512 + a + n]

        def P1FS1(c, n):
            return TY2.bitcast(F32)[:, 53248 // 4 + c * 512: 53248 // 4 + c * 512 + n]

        def P1FS2(c, n):
            if c >= 2:
                return PALL[:, 512 * 6: 512 * 6 + n]
            return TY2.bitcast(F32)[:, 69632 // 4 + c * 512: 69632 // 4 + c * 512 + n]

        def TY2V(byte_off, dt, cols, p0=0, np_=128):
            esz = 2 if dt == BF16 else 4
            return TY2.bitcast(dt)[p0:p0 + np_, byte_off // esz: byte_off // esz + cols]

        def KBUF(c, a, n, p0=0, np_=128):
            return TY3.bitcast(BF16)[p0:p0 + np_, c * SEQ + a: c * SEQ + a + n]

        def TY3V(byte_off, dt, cols, p0=0, np_=128):
            esz = 2 if dt == BF16 else 4
            return TY3.bitcast(dt)[p0:p0 + np_, byte_off // esz: byte_off // esz + cols]

        def Y3C(c, a, n, dt=BF16, p0=0, np_=128):
            return TY3.bitcast(dt)[p0:p0 + np_, c * SPAN + a: c * SPAN + a + n]

        def TEV(byte_off, dt, cols, p0=0, np_=128):
            esz = 2 if dt == BF16 else 4
            return TE.bitcast(dt)[p0:p0 + np_, byte_off // esz: byte_off // esz + cols]

        def EXPT(tile, a, n, p0=0, np_=128):
            return TE[p0:p0 + np_, tile * QT + a: tile * QT + a + n]

        def PB(b, a, n, dt=None):
            if dt is not None:
                esz = 2 if dt == BF16 else 4
                scale = 4 // esz
                return PALL.bitcast(dt)[:, scale * 512 * b + a: scale * 512 * b + a + n]
            return PALL[:, 512 * b + a: 512 * b + a + n]

        t_onef = sch.op("v", lambda v: v.memset(ONEF, 1.0))
        t_oner = sch.op("v", lambda v: v.tensor_copy(ONER, ONEF), waits=[t_onef])
        t_eps6 = sch.op("v", lambda v: v.memset(BIASB[:, 62:63], 1e-6))
        t_eps5 = sch.op("v", lambda v: v.memset(BIASB[:, 61:62], 1e-5))
        t_m30 = sch.op("v", lambda v: v.memset(BIASB[:, 60:61], -30.0))
        t_ones8 = sch.op("v", lambda v: v.memset(SMALL[0:8, 131:132], 1.0))

        tJ = sch.dma("y", "dm", lambda y: y.dma_start(out=SMALL[0:8, 0:128], in_=JV))
        t_sin = sch.op("s", lambda s: s.activation(SMALL[32:40, 0:128], SMALL[0:8, 0:128],
                                                   AF.Sin, accum_out=SMALL[0:8, 130:131]),
                       waits=[tJ])
        t_csum = sch.op("t", lambda t: t.matmul(
            PALL[0:1, 0:1], SMALL[0:8, 131:132], SMALL[0:8, 130:131],
            start=True, stop=True), waits=[t_sin, t_ones8], mark=True)
        t_cs = sch.op("v", lambda v: v.tensor_scalar(SMALL[0:1, 132:133], PALL[0:1, 0:1],
                                                     0.1 / 1024.0, None, ALU.mult),
                      waits=[t_csum])
        t_crep = sch.op("t", lambda t: t.matmul(
            PB(1, 0, 1), ONEF, SMALL[0:1, 132:133],
            start=True, stop=True), waits=[t_cs, t_onef], mark=True)
        t_cv = sch.op("v", lambda v: v.tensor_copy(BIASB[:, 63:64], PB(1, 0, 1)),
                      waits=[t_crep])

        dm_chain = [tJ]

        def q_bias_cols(col0, B_ap):
            tok = dm_chain[-1]
            for oc in range(8):
                ob, half = oc // 2, oc % 2
                for w in range(4):
                    sgn = float(BIAS_SGN[ob, w])
                    stg = SMALL[:, 160 + w: 161 + w]
                    tdma = sch.dma("y", "dm", (lambda y, s_=B_ap[w, 128 * half:128 * half + 128].unsqueeze(1),
                                               d_=stg: y.dma_start(out=d_, in_=s_)),
                                   waits=[tok])
                    dcol = BIASB[:, col0 + oc: col0 + oc + 1]
                    if w == 0:
                        tok = sch.op("v", (lambda v, d_=dcol, s_=stg, g_=sgn:
                                           v.tensor_scalar(d_, s_, g_, None, ALU.mult)),
                                     waits=[tdma])
                    else:
                        tok = sch.op("v", (lambda v, d_=dcol, s_=stg, g_=sgn:
                                           v.scalar_tensor_tensor(d_, s_, g_, d_, ALU.mult, ALU.add)),
                                     waits=[tdma])
            dm_chain.append(tok)
            return tok

        def quat_bias_row(dst_off, B_ap, as_f32r_off=None, final_extra=()):
            tok = dm_chain[-1]
            for ob in range(4):
                for w in range(4):
                    sgn = float(BIAS_SGN[ob, w])
                    stg = ROWB[0:1, 1024:1280]
                    tdma = sch.dma("y", "dm", (lambda y, s_=B_ap[w:w + 1, :], d_=stg:
                                               y.dma_start(out=d_, in_=s_)), waits=[tok])
                    dst = ROWB[0:1, dst_off + ob * 256: dst_off + (ob + 1) * 256]
                    if w == 0:
                        tok = sch.op("v", (lambda v, d_=dst, s_=stg, g_=sgn:
                                           v.tensor_scalar(d_, s_, g_, None, ALU.mult)),
                                     waits=[tdma])
                    else:
                        tok = sch.op("v", (lambda v, d_=dst, s_=stg, g_=sgn:
                                           v.scalar_tensor_tensor(d_, s_, g_, d_, ALU.mult, ALU.add)),
                                     waits=[tdma])
            if as_f32r_off is not None:
                tok = sch.op("v", (lambda v, o_=as_f32r_off: v.tensor_copy(
                    ROWR[0:1, o_:o_ + 1024], ROWB[0:1, 0:1024])),
                             waits=[tok] + list(final_extra))
            dm_chain.append(tok)
            return tok

        tbq = q_bias_cols(0, QBI)
        tbk = q_bias_cols(8, KBI)

        t_id = sch.dma("y", "dm", lambda y: y.dma_start(out=IDF, in_=IDENT),
                       waits=[dm_chain[-1]])
        dm_chain.append(t_id)
        l_idb = sch.dma("g", "dgm", lambda g: g.dma_start(out=IDB, in_=IDENT))
        t_fbrow = sch.dma("g", "dgm", lambda g: g.dma_start(
            out=ROWR[0:1, 1024:2048], in_=FB), waits=[l_idb])

        quatw_lastv = [dm_chain[-1]] * 4

        def quat_w(dst_off, W_ap):
            lastv = quatw_lastv
            tok = None
            for ic in range(8):
                comp, half = ic // 2, ic % 2
                for ob in range(4):
                    w_idx, sgn = QUAT_TAB[(ob, comp)]
                    stg = P1FS1(ob, 256)
                    tdma = sch.dma("y", "dw%d" % ob, (lambda y, s_=W_ap[w_idx, 128 * half:128 * half + 128, :],
                                               d_=stg: y.dma_start(out=d_, in_=s_)),
                                   waits=[lastv[ob]])
                    dst = TEV(dst_off, BF16, 8 * D)[:, ic * D + ob * 256: ic * D + ob * 256 + 256]
                    tok = sch.op("v", (lambda v, d_=dst, s_=stg, g_=float(sgn):
                                       v.tensor_scalar(d_, s_, g_, None, ALU.mult)),
                                 waits=[tdma])
                    lastv[ob] = tok
            return tok

        twq = quat_w(0, QW)
        twk = quat_w(16384, KW)

        psum_free = {}
        stg_free = [None, None]
        norm_free = [None]

        def proj_norm(si, n, src_fn, wbig_off, bias_col, dst_fn, extra_first=()):
            p = si % 2
            dsem = "dx0" if p == 0 else "dx1"
            tld = None
            for ic in range(8):
                wt = [stg_free[p]] if stg_free[p] else []
                tld = sch.dma("g", dsem, (lambda g, s_=src_fn(ic), d_=XSG(p, ic, 0, n):
                                          g.dma_start(out=d_, in_=s_)), waits=wt)
            ev = []
            for oc in range(8):
                bank = oc % 4
                last_mm = None
                for ic in range(8):
                    wt = []
                    if ic == 0:
                        wt = [tld] + list(extra_first)
                        if psum_free.get(bank):
                            wt.append(psum_free[bank])
                    lhsT = TEV(wbig_off, BF16, 8 * D)[:, ic * D + oc * 128: ic * D + oc * 128 + 128]
                    rhs = XSG(p, ic, 0, n)
                    last_mm = sch.op("t", (lambda t, o_=PB(bank, 0, n), l_=lhsT, r_=rhs,
                                           s0=(ic == 0), s1=(ic == 7):
                                           t.matmul(o_, l_, r_, start=s0, stop=s1)),
                                     mark=(ic == 7), waits=wt)
                ev.append(sch.op("v", (lambda v, d_=P1FS1(oc, n), p_=PB(bank, 0, n),
                                       b_=BIASB[:, bias_col + oc: bias_col + oc + 1]:
                                       v.tensor_scalar(d_, p_, b_, None, ALU.add)),
                                 waits=[last_mm]))
                psum_free[bank] = ev[-1]
            stg_free[p] = sch.op("t", lambda t: t.nop(), mark=True)

            out_toks = []
            for h in range(2):
                ssum = P1FS2(h, n)
                wt = [ev[h]] + ([norm_free[0]] if norm_free[0] else [])
                t0s = sch.op("s", (lambda ss, d_=ssum, a_=P1FS1(h, n):
                                   ss.activation(d_, a_, AF.Square)),
                             waits=wt)
                t0 = None
                for j in (2, 4, 6):
                    tmp = P1FS2(2, n)
                    tsq = sch.op("s", (lambda ss, d_=tmp, a_=P1FS1(h + j, n):
                                       ss.activation(d_, a_, AF.Square)),
                                 waits=[ev[h + j]] + ([t0] if t0 else []))
                    t0 = sch.op("v", (lambda v, d_=ssum, a_=tmp:
                                      v.tensor_tensor(d_, d_, a_, ALU.add)),
                                waits=[tsq] + ([t0s] if j == 2 else []))
                t2 = sch.op("s", (lambda s, d_=ssum: s.activation(d_, d_, AF.Ln,
                                                                  bias=BIASB[:, 62:63])),
                            waits=[t0, t_eps6])
                t3 = sch.op("s", (lambda s, d_=ssum: s.activation(d_, d_, AF.Exp, scale=-0.5)),
                            waits=[t2])
                out_toks.append(t3)
            final = None
            for oc in range(8):
                h = oc % 2
                final = sch.op("v", (lambda v, d_=dst_fn(oc), a_=P1FS1(oc, n), b_=P1FS2(h, n):
                                     v.tensor_tensor(d_, a_, b_, ALU.mult)),
                               waits=[out_toks[h]])
            norm_free[0] = final
            return final

        wready = (twk, tbk, t_cv)
        k_done = q_done = None
        for si in range(8):
            k_done = proj_norm(
                si, 512,
                (lambda ic, si_=si: XFF[128 * ic:128 * ic + 128, 512 * si_:512 * si_ + 512]),
                16384, 8,
                (lambda oc, si_=si: KBUF(oc, 512 * si_, 512)),
                extra_first=(wready if si == 0 else ()))
        for sj in range(5):
            n = 512 if sj < 4 else 256
            q_done = proj_norm(
                8 + sj, n,
                (lambda ic, s_=sj, n_=n: XSF[128 * ic:128 * ic + 128, 512 * s_:512 * s_ + n_]),
                0, 0,
                (lambda oc, s_=sj, n_=n: QBUF(oc, 512 * s_, n_)))

        if stage == "p1a":
            for c in range(8):
                sch.dma("g", "dout", (lambda g, c_=c, q_=QBUF(c, 0, SPAN): g.dma_start(
                    out=DBG[:, c_ * SPAN: (c_ + 1) * SPAN],
                    in_=q_)), waits=[q_done, k_done])
            sch.op("y", lambda y: y.wait_ge(sem_handles["dout"], sch.cnt["dout"]))
            sch.emit(block, sem_handles)
            stack.close()
            return nc

        tk_t = None
        tkq_nop = sch.op("t", lambda t: t.nop(), mark=True)

        t_vrow = quat_bias_row(0, VBI, as_f32r_off=0)

        def quat_w2(dst_off, W_ap, firstw):
            lastv = [firstw] * 4
            tok = None
            for ic in range(8):
                comp, half = ic // 2, ic % 2
                for ob in range(4):
                    w_idx, sgn = QUAT_TAB[(ob, comp)]
                    stg = TEV(16384 + ob * 1024, F32, 256)
                    tdma = sch.dma("y", "dw%d" % ob,
                                   (lambda y, s_=W_ap[w_idx, 128 * half:128 * half + 128, :],
                                    d_=stg: y.dma_start(out=d_, in_=s_)),
                                   waits=[lastv[ob]])
                    dst = TEV(dst_off, BF16, 8 * D)[:, ic * D + ob * 256: ic * D + ob * 256 + 256]
                    tok = sch.op("v", (lambda v, d_=dst, s_=stg, g_=float(sgn):
                                       v.tensor_scalar(d_, s_, g_, None, ALU.mult)),
                                 waits=[tdma])
                    lastv[ob] = tok
            return tok

        twv = quat_w2(24576, VW, tkq_nop)

        VT3 = TY2.bitcast(BF16)[:, 36864 // 2: 36864 // 2 + 256 * 65].rearrange(
            "p (s c) -> p s c", c=65)
        t_vones = sch.op("v", lambda v: v.memset(VT3[:, :, 64:65], 1.0),
                         waits=[q_done, tkq_nop])

        def XSG2(c, a, n):
            base = 65536 // 2
            return TY3.bitcast(BF16)[:, base + c * 512 + a: base + c * 512 + a + n]

        def ATMv(a, n):
            base = 65536 // 2
            return TY3.bitcast(BF16)[:, base + a: base + a + n]

        def emit_p1b(g, first_waits):
            xsg2_free = None
            done = None
            for blk in range(8):
                wt = list(first_waits) + ([xsg2_free] if xsg2_free else [])
                tld0 = tld1 = None
                for ic in range(8):
                    sem = "db" if ic % 2 == 0 else "dx0"
                    t_ = sch.dma("g", sem,
                                 (lambda gg, s_=XFF[128 * ic:128 * ic + 128, 512 * blk:512 * blk + 512],
                                  d_=XSG2(ic, 0, 512): gg.dma_start(out=d_, in_=s_)),
                                 waits=wt if ic < 2 else [])
                    if ic % 2 == 0:
                        tld0 = t_
                    else:
                        tld1 = t_
                tld = tld0
                for sub4 in range(4):
                    bank = sub4 % 2
                    last = None
                    for ic in range(8):
                        w = []
                        if ic == 0:
                            w = [tld0, tld1]
                            if psum_free.get(bank):
                                w.append(psum_free[bank])
                            if ic == 0 and blk == 0 and sub4 == 0:
                                w.append(twv)
                                w.append(t_vrow)
                        lhsT = XSG2(ic, sub4 * 128, 128)
                        rhs = TEV(24576, BF16, 8 * D)[:, ic * D + g * 512: ic * D + g * 512 + 512]
                        last = sch.op("t", (lambda t, o_=PB(bank, 0, 512), l_=lhsT, r_=rhs,
                                            s0=(ic == 0): t.matmul(o_, l_, r_, start=s0, stop=False)),
                                      waits=w)
                    last = sch.op("t", (lambda t, o_=PB(bank, 0, 512),
                                        r_=ROWR[0:1, g * 512:(g + 1) * 512]:
                                        t.matmul(o_, ONER, r_, start=False, stop=True)),
                                  mark=True)
                    slot0 = (blk * 4 + sub4) * 8
                    dst3 = TY2.bitcast(BF16)[:, 36864 // 2 + slot0 * 65: 36864 // 2 + slot0 * 65 + 8 * 65].rearrange(
                        "p (s c) -> p s c", c=65)[:, :, 0:64]
                    src3 = PB(bank, 0, 512).rearrange("p (s c) -> p s c", c=64)
                    ev = sch.op("v", (lambda v, d_=dst3, s_=src3: v.tensor_copy(d_, s_)),
                                waits=[last, t_vones])
                    psum_free[bank] = ev
                    done = ev
                xsg2_free = sch.op("t", lambda t: t.nop(), mark=True)
            return done, xsg2_free

        def emit_p1c(g, ready, state, hook=None, nsets=2):
            PVB = (6, 7)
            for qt in range(NQT):
                batches = [(hp, H, half) for hp in range(4)
                           for H in (0, 1) for half in (0, 1)]
                etoks = {}

                def emit_A(j, qt=qt, batches=batches, etoks=etoks):
                    hp, H, half = batches[j]
                    kch = g * 4 + hp
                    toks = []
                    for sq in range(8):
                        par = (j * 8 + sq) % nsets
                        set_base = 2 * par
                        fw = []
                        if state["exp_prev"][par]:
                            fw.append(state["exp_prev"][par])
                        if qt == 0 and j == 0 and sq == 0:
                            fw += list(ready)
                        sub = H * 16 + half * 8 + sq
                        last = None
                        for hh in (0, 1):
                            bank = set_base + hh
                            lhsT = KBUF(kch, sub * 128, 128, p0=64 * hh, np_=64)
                            rhs = QBUF(kch, qt * QT, QT, p0=64 * hh, np_=64)
                            w = fw if hh == 0 else []
                            last = sch.op("t", (lambda t, o_=PB(bank, 0, QT), l_=lhsT,
                                                r_=rhs, tp=(64 * hh, 0):
                                                t.matmul(o_, l_, r_, start=True, stop=True,
                                                         tile_position=tp)),
                                          waits=w, mark=(hh == 1))
                        ew = [last]
                        if state["expt_free"][j % 2]:
                            ew.append(state["expt_free"][j % 2])
                        tile0 = (j % 2) * 16 + sq * 2
                        in_ap = PALL[:, set_base * 512: set_base * 512 + 1024].rearrange(
                            "p (b x) -> p b x", x=512)[:, :, 0:QT]
                        out_ap = TE[:, tile0 * QT: (tile0 + 2) * QT].rearrange(
                            "p (b x) -> p b x", x=QT)
                        etok = sch.op("s", (lambda ss, o_=out_ap, i_=in_ap:
                                            ss.activation(o_, i_, AF.Exp,
                                                          bias=BIASB[:, 63:64], scale=0.125)),
                                      waits=ew)
                        state["exp_prev"][par] = etok
                        toks.append(etok)
                    etoks[j] = toks

                def emit_P(j, qt=qt, batches=batches, etoks=etoks):
                    hp, H, half = batches[j]
                    b = PVB[hp % 2]
                    first_of_hp = (H == 0 and half == 0)
                    last_of_hp = (H == 1 and half == 1)
                    pv_last = None
                    for sq in range(8):
                        sub = H * 16 + half * 8 + sq
                        for hh in (0, 1):
                            hl = 2 * hp + hh
                            tile = (j % 2) * 16 + sq * 2 + hh
                            for qs in (0, 1, 2):
                                w = []
                                if hh == 0 and qs == 0:
                                    w.append(etoks[j][sq])
                                    if first_of_hp and sq == 0 and state["pvfree"][b]:
                                        w.append(state["pvfree"][b])
                                st = first_of_hp and sq == 0 and hh == 0 and qs == 0
                                lhsT = EXPT(tile, qs * 128, 128)
                                rhs = VTB(sub * 8 + hl, 0, 65)
                                out = PB(b, (hh * 3 + qs) * 65, 65)
                                pv_last = sch.op("t", (lambda t, o_=out, l_=lhsT, r_=rhs,
                                                       s0=st:
                                                       t.matmul(o_, l_, r_, start=s0, stop=False,
                                                                skip_group_check=True)),
                                                 waits=w,
                                                 mark=(sq == 7 and hh == 1 and qs == 2))
                    state["expt_free"][j % 2] = pv_last
                    if last_of_hp:
                        dens3 = PB(b, 0, 390).rearrange("p (s c) -> p s c", c=65)[:, :, 64:65]
                        t2 = sch.op("v", (lambda v, d3=dens3: v.reciprocal(SMALL[:, 80:86], d3)),
                                    waits=[pv_last])
                        ntok = None
                        for hh in (0, 1):
                            for qs in (0, 1, 2):
                                idx = hh * 3 + qs
                                hl = 2 * hp + hh
                                dst = ATMv(qs * 512 + hl * 64, 64)
                                w = [t2]
                                if state["atm_free"]:
                                    w.append(state["atm_free"])
                                ntok = sch.op("v", (lambda v, d_=dst,
                                                    s_=PB(b, idx * 65, 64),
                                                    r_=SMALL[:, 80 + idx: 81 + idx]:
                                                    v.tensor_scalar(d_, s_, r_, None, ALU.mult)),
                                              waits=w)
                        state["pvfree"][b] = ntok
                        state["pv_norm"] = ntok
                        if hook:
                            hook(3 * qt - 1, 2)

                emit_A(0)
                emit_A(1)
                for j in range(16):
                    emit_P(j)
                    if j + 2 < 16:
                        emit_A(j + 2)
                sp = None
                for qs in range(3):
                    sp = sch.dma("y", "dsp", (lambda y, r0=qt * QT + qs * 128,
                                              a_=ATMv(qs * 512, 512), g_=g:
                                              y.dma_start(out=ATT0[r0:r0 + 128, g_ * 512:(g_ + 1) * 512],
                                                          in_=a_)),
                                 waits=[state["pv_norm"]])
                state["atm_free"] = sp
                if hook:
                    hook(3 * qt + 2, 4)
            return state

        st_p1c = {"exp_prev": {0: None, 1: None, 2: None}, "pv_norm": None,
                  "expt_free": [None, None], "atm_free": None,
                  "pvfree": {6: None, 7: None}}
        v_done0, xf0 = emit_p1b(0, [t_vones])
        st_p1c = emit_p1c(0, [v_done0, tkq_nop, q_done], st_p1c, nsets=3)
        tg0 = sch.op("t", lambda t: t.nop(), mark=True)
        g0_exps = [t for t in (st_p1c["exp_prev"][0], st_p1c["exp_prev"][1],
                               st_p1c["exp_prev"][2]) if t]
        v_done1, xf1 = emit_p1b(1, [st_p1c["atm_free"], st_p1c["pv_norm"]])
        st_p1c["atm_free"] = xf1

        t_orow = quat_bias_row(0, OBI, as_f32r_off=0, final_extra=[xf1])

        def quat_w3(dst_fn, W_ap, firstw, stg_fn, extraw=()):
            lastv = [firstw] * 2
            tok = None
            for ic in range(8):
                comp, half = ic // 2, ic % 2
                for ob in range(4):
                    w_idx, sgn = QUAT_TAB[(ob, comp)]
                    slot = ob % 2
                    stg = stg_fn(slot)
                    tdma = sch.dma("y", "dw%d" % slot,
                                   (lambda y, s_=W_ap[w_idx, 128 * half:128 * half + 128, :],
                                    d_=stg: y.dma_start(out=d_, in_=s_)),
                                   waits=[lastv[slot]] + (list(extraw) if ic == 0 and ob < 2 else []))
                    dst = dst_fn(ic, ob)
                    tok = sch.op("v", (lambda v, d_=dst, s_=stg, g_=float(sgn):
                                       v.tensor_scalar(d_, s_, g_, None, ALU.mult)),
                                 waits=[tdma] + ([xf1] if ic == 0 and ob == 0 else []))
                    lastv[slot] = tok
            return tok

        two = quat_w3(
            (lambda ic, ob: TEV(24576, BF16, 8 * D)[:, ic * D + ob * 256: ic * D + ob * 256 + 256]),
            OW, twv,
            (lambda slot: TEV(22528 + slot * 1024, F32, 256)),
            extraw=[tg0] + g0_exps)

        def emit_quant(src_ap, dst_ap, wait0, slots):
            Aap, Bap, Cap, Dap = slots
            bmax = SMALL[:, 0:32]
            scl = SMALL[:, 32:64]
            rec = SMALL[:, 134:166]
            src3 = src_ap.rearrange("p (b w) -> p b w", w=32)
            t1 = sch.op("v", (lambda v: v.tensor_reduce(bmax, src3, AX.X, ALU.max,
                                                        apply_absolute_value=True)),
                        waits=list(wait0))
            t2 = sch.op("v", lambda v: v.tensor_scalar(scl, bmax, 1.0 / 7.5, 1e-30,
                                                       ALU.mult, ALU.max))
            t3 = sch.op("v", lambda v: v.reciprocal(rec, scl))
            rb = rec.unsqueeze(2).broadcast_to([128, 32, 32])
            sb = scl.unsqueeze(2).broadcast_to([128, 32, 32])
            A3 = Aap.rearrange("p (b w) -> p b w", w=32)
            t4 = sch.op("v", (lambda v: v.tensor_tensor(A3, src3, rb, ALU.mult)))
            t5 = sch.op("v", (lambda v: v.tensor_tensor(Bap, Aap, Aap, ALU.mult)))
            t6 = sch.op("v", lambda v: v.tensor_scalar(Cap, Bap, 4.0, None, ALU.is_lt))
            t7 = sch.op("v", lambda v: v.tensor_scalar(Bap, Bap, 16.0, None, ALU.is_lt))
            t8 = sch.op("v", lambda v: v.tensor_scalar(Dap, Bap, -0.25, 0.5, ALU.mult, ALU.add))
            t9 = sch.op("v", lambda v: v.scalar_tensor_tensor(Dap, Cap, -0.125, Dap,
                                                              ALU.mult, ALU.add))
            ta = sch.op("v", lambda v: v.scalar_tensor_tensor(Bap, Cap, 2.0, Bap,
                                                              ALU.mult, ALU.add))
            tb = sch.op("v", lambda v: v.tensor_scalar(Bap, Bap, 2.0, 2.0, ALU.mult, ALU.add))
            tc = sch.op("v", lambda v: v.tensor_tensor(Cap, Aap, Bap, ALU.mult))
            td = sch.op("v", lambda v: v.tensor_scalar(Cap, Cap, MAGIC, MAGIC,
                                                       ALU.add, ALU.subtract))
            te = sch.op("v", lambda v: v.tensor_tensor(Cap, Cap, Dap, ALU.mult))
            C3 = Cap.rearrange("p (b w) -> p b w", w=32)
            d3 = dst_ap.rearrange("p (b w) -> p b w", w=32)
            tf = sch.op("v", (lambda v: v.tensor_tensor(d3, C3, sb, ALU.mult)))
            return tf

        def QSLOT(i):
            return TY2.bitcast(F32)[:, i * 1024:(i + 1) * 1024]
        P2_SLOTS = (QSLOT(0), QSLOT(1), QSLOT(2), QSLOT(3))
        XDT_AP = TY2.bitcast(BF16)[:, 8192:9216]
        XDF_AP = TY2.bitcast(BF16)[:, 35072:36096]

        def Y1AP(p):
            off = 16384 if p == 0 else 24576
            return TY3.bitcast(F32)[:, off // 4: off // 4 + 1024]
        XTM_AP = TY3.bitcast(F32)[:, 20480 // 4: 20480 // 4 + 1024]

        def AF0(p, c, a, n):
            base = (28672 + p * 2048) // 2
            return TY3.bitcast(BF16)[:, base + c * 128 + a: base + c * 128 + a + n]

        p2s = {"oproj": {}, "y1evac": {}, "quant": {}, "transp": {},
               "ffn0": {}, "z2": {}, "spill": {}, "ln2": {}, "ln1e": {},
               "ln2e": {}}

        def p2_f1(sl):
            p = sl % 2
            dl = "dl0" if p == 0 else "dl1"
            tldA = None
            for c in range(8):
                w = []
                if c == 0:
                    w = [("dsp", sch.cnt["dsp"]), tg0]
                    if p2s["oproj"].get(sl - 2):
                        w.append(p2s["oproj"][sl - 2])
                tldA = sch.dma("y", dl, (lambda y, s_=ATT0[128 * sl:128 * sl + 128,
                                                           128 * c:128 * c + 128],
                                         d_=AF0(p, c, 0, 128):
                                         y.dma_start_transpose(d_, s_)), waits=w)
            w = [tg0] + ([p2s["y1evac"][sl - 1]] if sl >= 1 else [])
            tldX = sch.dma("g", "dxa", (lambda g, s_=XST[128 * sl:128 * sl + 128, :]:
                                        g.dma_start(out=XTM_AP, in_=s_)), waits=w)
            o_marks = []
            for nh in range(2):
                bank = 4 + nh
                for ic in range(8):
                    w = []
                    if ic == 0:
                        w = [tldA, two, t_orow]
                        if psum_free.get(bank):
                            w.append(psum_free[bank])
                    sch.op("t", (lambda t, o_=PB(bank, 0, 512),
                                 l_=AF0(p, ic, 0, 128),
                                 r_=TEV(24576, BF16, 8 * D)[:, ic * D + nh * 512: ic * D + nh * 512 + 512],
                                 s0=(ic == 0): t.matmul(o_, l_, r_, start=s0, stop=False)),
                           waits=w)
                last = sch.op("t", (lambda t, o_=PB(bank, 0, 512),
                                    r_=ROWR[0:1, nh * 512:(nh + 1) * 512]:
                                    t.matmul(o_, ONER, r_, start=False, stop=True)), mark=True)
                o_marks.append(last)
            p2s["oproj"][sl] = o_marks[-1]
            ev1 = None
            for nh in range(2):
                ev1 = sch.op("v", (lambda v, d_=Y1AP(p)[:, nh * 512:(nh + 1) * 512],
                                   a_=PB(4 + nh, 0, 512), b_=XTM_AP[:, nh * 512:(nh + 1) * 512]:
                                   v.tensor_tensor(d_, a_, b_, ALU.add)),
                             waits=[o_marks[nh], tldX])
                psum_free[4 + nh] = ev1
            p2s["y1evac"][sl] = ev1

        def p2_f2(sl):
            p = sl % 2
            y1ap = Y1AP(p)
            tsum = sch.op("v", (lambda v, z_=y1ap: v.tensor_scalar(QSLOT(1), z_, 1.0, 0.0,
                                                          ALU.mult, ALU.add,
                                                          accum_out=SMALL[:, 70:71])),
                          waits=[p2s["y1evac"][sl]])
            tsq = sch.op("v", (lambda v, z_=y1ap: v.scalar_tensor_tensor(QSLOT(2), z_, 1.0, z_,
                                                        ALU.mult, ALU.mult,
                                                        accum_out=SMALL[:, 71:72])))
            tm1 = sch.op("v", lambda v: v.tensor_scalar(SMALL[:, 72:73], SMALL[:, 70:71],
                                                        -1.0 / 1024.0, None, ALU.mult))
            tm2 = sch.op("v", lambda v: v.scalar_tensor_tensor(SMALL[:, 73:74], SMALL[:, 72:73],
                                                               1.0, SMALL[:, 72:73],
                                                               ALU.mult, ALU.mult))
            tvv = sch.op("v", lambda v: v.scalar_tensor_tensor(SMALL[:, 74:75], SMALL[:, 71:72],
                                                               1.0 / 1024.0, SMALL[:, 73:74],
                                                               ALU.mult, ALU.subtract))
            tl1 = sch.op("s", lambda ss: ss.activation(SMALL[:, 74:75], SMALL[:, 74:75], AF.Ln,
                                                       bias=BIASB[:, 61:62]), waits=[tvv, t_eps5])
            tl2 = sch.op("s", lambda ss: ss.activation(SMALL[:, 74:75], SMALL[:, 74:75], AF.Exp,
                                                       scale=-0.5))
            p2s["ln1e"][sl] = tl2

        def p2_f2b(sl):
            p = sl % 2
            y1ap = Y1AP(p)
            tap = sch.op("v", (lambda v, z_=y1ap: v.tensor_scalar(z_, z_, SMALL[:, 72:73],
                                                         SMALL[:, 74:75], ALU.add, ALU.mult)),
                         waits=[p2s["ln1e"][sl]])
            w0 = [tap, p2s["transp"][sl - 1]]
            p2s["quant"][sl] = emit_quant(y1ap, XDT_AP, w0, P2_SLOTS)

        def p2_back(sl):
            p = sl % 2
            last_tt = None
            evs = []
            for c in range(8):
                bank = 4 + (c % 2)
                w = [p2s["quant"][sl], l_idb]
                if psum_free.get(bank):
                    w.append(psum_free[bank])
                last_tt = sch.op("t", (lambda t, o_=PB(bank, (c // 2) * 128, 128, BF16),
                                  i_=XDT_AP[:, c * 128:(c + 1) * 128]:
                                  t.transpose(o_, i_, IDB)),
                            waits=w, mark=True)
                ew = [last_tt]
                if c == 0 and sl >= 1:
                    ew.append(p2s["spill"][sl - 1])
                ev = sch.op("v", (lambda v, d_=XDF_AP[:, c * 128:(c + 1) * 128],
                                  s_=PB(bank, (c // 2) * 128, 128, BF16): v.tensor_copy(d_, s_)),
                            waits=ew)
                psum_free[bank] = ev
                evs.append(ev)
            p2s["transp"][sl] = last_tt
            f_marks = []
            for nh in range(2):
                bank = 4 + nh
                for ic in range(8):
                    w = []
                    if ic == 0:
                        w = [evs[-1], wd_state["done"], t_fbrow]
                        if psum_free.get(bank):
                            w.append(psum_free[bank])
                    sch.op("t", (lambda t, o_=PB(bank, 0, 512),
                                 l_=XDF_AP[:, ic * 128:(ic + 1) * 128],
                                 r_=TY3V(0, BF16, 8 * D)[:, ic * D + nh * 512: ic * D + nh * 512 + 512],
                                 s0=(ic == 0): t.matmul(o_, l_, r_, start=s0, stop=False)),
                           waits=w)
                last = sch.op("t", (lambda t, o_=PB(bank, 0, 512),
                                    r_=ROWR[0:1, 1024 + nh * 512: 1024 + (nh + 1) * 512]:
                                    t.matmul(o_, ONER, r_, start=False, stop=True)), mark=True)
                f_marks.append(last)
            ev2 = None
            for nh in range(2):
                ev2 = sch.op("v", (lambda v, d_=QSLOT(0)[:, nh * 512:(nh + 1) * 512],
                                   a_=PB(4 + nh, 0, 512), b_=Y1AP(p)[:, nh * 512:(nh + 1) * 512]:
                                   v.tensor_tensor(d_, a_, b_, ALU.add)),
                             waits=[f_marks[nh]])
                psum_free[4 + nh] = ev2
            p2s["z2"][sl] = ev2
            zap = QSLOT(0)
            t2sum = sch.op("v", (lambda v, z_=zap: v.tensor_scalar(QSLOT(1), z_, 1.0, 0.0,
                                                           ALU.mult, ALU.add,
                                                           accum_out=SMALL[:, 75:76])),
                           waits=[ev2])
            t2sq = sch.op("v", (lambda v, z_=zap: v.scalar_tensor_tensor(QSLOT(3), z_, 1.0, z_,
                                                         ALU.mult, ALU.mult,
                                                         accum_out=SMALL[:, 76:77])))
            t2m1 = sch.op("v", lambda v: v.tensor_scalar(SMALL[:, 77:78], SMALL[:, 75:76],
                                                         -1.0 / 1024.0, None, ALU.mult))
            t2m2 = sch.op("v", lambda v: v.scalar_tensor_tensor(SMALL[:, 78:79], SMALL[:, 77:78],
                                                                1.0, SMALL[:, 77:78],
                                                                ALU.mult, ALU.mult))
            t2v = sch.op("v", lambda v: v.scalar_tensor_tensor(SMALL[:, 79:80], SMALL[:, 76:77],
                                                               1.0 / 1024.0, SMALL[:, 78:79],
                                                               ALU.mult, ALU.subtract))
            t2l1 = sch.op("s", lambda ss: ss.activation(SMALL[:, 79:80], SMALL[:, 79:80], AF.Ln,
                                                        bias=BIASB[:, 61:62]), waits=[t2v])
            t2l2 = sch.op("s", lambda ss: ss.activation(SMALL[:, 79:80], SMALL[:, 79:80], AF.Exp,
                                                        scale=-0.5))
            p2s["ln2e"][sl] = t2l2

        def p2_b2(sl):
            zap = QSLOT(0)
            t2ap = sch.op("v", (lambda v, z_=zap: v.tensor_scalar(XDF_AP, z_,
                                                          SMALL[:, 77:78], SMALL[:, 79:80],
                                                          ALU.add, ALU.mult)),
                          waits=[p2s["ln2e"][sl]])
            p2s["ln2"][sl] = t2ap
            p2s["spill"][sl] = sch.dma("y", "dy2", (lambda y, r0=128 * sl:
                                                    y.dma_start(out=Y2SP[r0:r0 + 128, :],
                                                                in_=XDF_AP)),
                                       waits=[t2ap])

        wd_state = {"tld": {}, "tq": {}, "done": None, "wt_last": None}

        def emit_wd(oc):
            if oc == 0:
                wd_state["tld"][0] = sch.dma(
                    "g", "da", (lambda g: g.dma_start(out=Y1AP(0), in_=FW[0:128, :])),
                    waits=[tg0])
            if oc + 1 < 8:
                w = [wd_state["tq"][oc - 1]] if oc >= 1 else [tg0]
                wd_state["tld"][oc + 1] = sch.dma(
                    "g", "da" if (oc + 1) % 2 == 0 else "dwd",
                    (lambda g, s_=FW[128 * (oc + 1):128 * (oc + 1) + 128, :],
                     d_=Y1AP((oc + 1) % 2): g.dma_start(out=d_, in_=s_)),
                    waits=w)
            w0 = [wd_state["tld"][oc], tg0]
            if oc >= 1:
                w0.append(wd_state["wt_last"])
            tq = emit_quant(Y1AP(oc % 2), XDT_AP, w0, P2_SLOTS)
            wd_state["tq"][oc] = tq
            evs = []
            for c in range(8):
                bank = 4 + (c % 2)
                w = [tq, l_idb]
                if psum_free.get(bank):
                    w.append(psum_free[bank])
                tt = sch.op("t", (lambda t, o_=PB(bank, (c // 2) * 128, 128, BF16),
                                  i_=XDT_AP[:, c * 128:(c + 1) * 128]:
                                  t.transpose(o_, i_, IDB)),
                            waits=w, mark=True)
                ev = sch.op("v", (lambda v, d_=TY3V(0, BF16, 8 * D)[:, c * D + oc * 128: c * D + oc * 128 + 128],
                                  s_=PB(bank, (c // 2) * 128, 128, BF16): v.tensor_copy(d_, s_)),
                            waits=[tt])
                psum_free[bank] = ev
                evs.append(ev)
                wd_state["wt_last"] = tt
            wd_state["done"] = evs[-1]
            if oc == 7:
                p2s["transp"][-1] = wd_state["wt_last"]

        p2drv = {"nf1": 0, "nf2a": 0, "nf2b": 0, "nb1": 0, "nb2": 0, "nwd": 0}

        def p2_pump(avail, budget):
            d = p2drv
            done = 0
            while done < budget:
                if d["nwd"] < 8:
                    emit_wd(d["nwd"]); d["nwd"] += 1; done += 1; continue
                k = d["nf1"]
                if k < NSL and k <= avail and d["nb1"] >= k - 1:
                    p2_f1(k); d["nf1"] += 1; done += 1; continue
                k = d["nb2"]
                if k < NSL and d["nb1"] >= k + 1:
                    p2_b2(k); d["nb2"] += 1; done += 1; continue
                k = d["nb1"]
                if k < NSL and d["nf2b"] >= k + 1 and d["nb2"] >= k:
                    p2_back(k); d["nb1"] += 1; done += 1; continue
                k = d["nf2b"]
                if k < NSL and d["nf2a"] >= k + 1 and d["nb2"] >= k and d["nb1"] >= k:
                    p2_f2b(k); d["nf2b"] += 1; done += 1; continue
                k = d["nf2a"]
                if k < NSL and d["nf1"] >= k + 1 and d["nf2b"] >= k:
                    p2_f2(k); d["nf2a"] += 1; done += 1; continue
                break

        st_p1c["expt_free"] = [two, two]
        st_p1c["expt_free"] = [two, two]
        st_p1c = emit_p1c(1, [v_done1], st_p1c, hook=p2_pump)
        att0_done = st_p1c["atm_free"]
        p2_pump(NSL, 10 ** 9)
        wdt_done = wd_state["done"]
        tg1 = sch.op("t", lambda t: t.nop(), mark=True)
        p2v_last = p2s["ln2"][NSL - 1]

        if stage == "p1c":
            for sl in range(NSL):
                sch.dma("g", "dout", (lambda gg, sl_=sl: gg.dma_start(
                    out=DBG[:, sl_ * 1024: (sl_ + 1) * 1024],
                    in_=ATT0[128 * sl_:128 * sl_ + 128, :])),
                    waits=[("dsp", sch.cnt["dsp"])])
            sch.op("y", lambda y: y.wait_ge(sem_handles["dout"], sch.cnt["dout"]))
            sch.emit(block, sem_handles)
            stack.close()
            return nc

        if stage == "p2":
            for sl in range(NSL):
                sch.dma("g", "dout", (lambda gg, sl_=sl: gg.dma_start(
                    out=DBG[:, sl_ * 1024: (sl_ + 1) * 1024],
                    in_=Y2SP[128 * sl_:128 * sl_ + 128, :])),
                    waits=[("dy2", 16 * NSL)])
            sch.op("y", lambda y: y.wait_ge(sem_handles["dout"], sch.cnt["dout"]))
            sch.emit(block, sem_handles)
            stack.close()
            return nc

        tp2_nop = sch.op("t", lambda t: t.nop(), mark=True)

        def EXL1(tile, a, n):
            return TE[:, tile * QT + a: tile * QT + a + n]

        def MT(var, a, n):
            base = 8192 // 2
            return TE[:, base + var * QT + a: base + var * QT + a + n]

        SIG_RANGE = {0: (0, 128), 1: (0, 256), 2: (0, 384), 3: (128, 384), 4: (256, 384)}

        l_wq1 = None
        for ic in range(8):
            l_wq1 = sch.dma("g", "da", (lambda g, s_=W1Q[128 * ic:128 * ic + 128, :],
                                        d_=TY3V(0, BF16, 8 * D)[:, ic * D: (ic + 1) * D]:
                                        g.dma_start(out=d_, in_=s_)),
                            waits=[tp2_nop] if ic == 0 else [])
        l_wk1 = None
        for ic in range(8):
            l_wk1 = sch.dma("g", "db", (lambda g, s_=W1K[128 * ic:128 * ic + 128, :],
                                        d_=TY3V(49408, BF16, 8 * 128)[:, ic * 128:(ic + 1) * 128]:
                                        g.dma_start(out=d_, in_=s_)),
                            waits=[tp2_nop] if ic == 0 else [])
        l_wv1 = None
        for ic in range(8):
            l_wv1 = sch.dma("g", "db", (lambda g, s_=W1V[128 * ic:128 * ic + 128, :],
                                        d_=TY3V(53504, BF16, 8 * 128)[:, ic * 128:(ic + 1) * 128]:
                                        g.dma_start(out=d_, in_=s_)),
                            waits=[l_wk1] if ic == 0 else [])
        l_b1q = sch.dma("y", "dm", lambda y: y.dma_start(out=BIASB[:, 32:40], in_=B1Q),
                        waits=[dm_chain[-1]])
        l_b1k = sch.dma("y", "dm", lambda y: y.dma_start(out=BIASB[:, 40:41], in_=B1K),
                        waits=[l_b1q])
        l_b1v = sch.dma("y", "dm", lambda y: y.dma_start(out=BIASB[:, 41:42], in_=B1VC),
                        waits=[l_b1k])
        l_mask = sch.dma("y", "dm", (lambda y: y.dma_start(out=MT(0, 0, 5 * QT), in_=MASKS)),
                         waits=[tp2_nop, l_b1v, wdt_done])
        def K1B(a, n, p0=0, np_=128):
            return TY3.bitcast(BF16)[p0:p0 + np_, 32768 // 2 + a: 32768 // 2 + a + n]

        def V1T(slot, a, n):
            return TY3.bitcast(BF16)[:, 37888 // 2 + slot * 65 + a: 37888 // 2 + slot * 65 + a + n]

        def Q1B(c, a, n, p0=0, np_=128):
            return TY3.bitcast(BF16)[p0:p0 + np_, 43264 // 2 + c * QT + a: 43264 // 2 + c * QT + a + n]

        def VFM(a, n):
            return TY3.bitcast(BF16)[:, 57600 // 2 + a: 57600 // 2 + a + n]

        z1 = sch.op("v", lambda v: v.memset(K1B(0, 128), 0.0), waits=[tp2_nop])
        z2 = sch.op("v", lambda v: v.memset(K1B(2432, 128), 0.0))
        V1T3 = TY3.bitcast(BF16)[:, 37888 // 2: 37888 // 2 + 40 * 65].rearrange(
            "p (s c) -> p s c", c=65)
        z3 = sch.op("v", lambda v: v.memset(V1T3[:, 0:2, :], 0.0))
        z4 = sch.op("v", lambda v: v.memset(V1T3[:, 38:40, :], 0.0))
        z5 = sch.op("v", lambda v: v.memset(V1T3[:, :, 64:65], 1.0))

        def YSTG(pb, ic, n):
            return TY3V(16384 + pb * 8192, BF16, 8 * 512)[:, ic * 512: ic * 512 + n]

        kv_done = None
        ystg_free = {0: None, 1: None}
        kv_cast = {}

        def emit_kv_cast(si, extra=()):
            n = 512 if si < 4 else 256
            pb = si % 2
            dls = "dl0" if pb == 0 else "dl1"
            cast_tok = None
            for ic in range(8):
                w = []
                if ic == 0:
                    w = [("dy2", 16 * NSL), tp2_nop, p2v_last] + list(extra)
                cast_tok = sch.dma("y", dls, (lambda y, d_=YSTG(pb, ic, n),
                                              s_=Y2SP[512 * si:512 * si + n,
                                                      128 * ic:128 * ic + 128]:
                                              y.dma_start_transpose(d_, s_)),
                                   waits=w)
            kv_cast[si] = cast_tok

        emit_kv_cast(0)
        emit_kv_cast(1)
        for si in range(5):
            n = 512 if si < 4 else 256
            pb = si % 2
            for which in (0, 1):
                bank = which
                woff = 49408 if which == 0 else 53504
                last = None
                for ic in range(8):
                    w = []
                    if ic == 0:
                        w = [l_wv1, l_b1v, kv_cast[si]]
                        if psum_free.get(bank):
                            w.append(psum_free[bank])
                    last = sch.op("t", (lambda t, o_=PB(bank, 0, n),
                                        l_=TY3V(woff, BF16, 8 * 128)[:, ic * 128:(ic + 1) * 128],
                                        r_=YSTG(pb, ic, n),
                                        s0=(ic == 0), s1=(ic == 7):
                                        t.matmul(o_, l_, r_, start=s0, stop=s1)),
                                  waits=w, mark=(ic == 7))
                if which == 0:
                    ev = sch.op("v", (lambda v, d_=K1B(128 + 512 * si, n), p_=PB(bank, 0, n):
                                      v.tensor_scalar(d_, p_, BIASB[:, 40:41], None, ALU.add)),
                                waits=[last, z2])
                else:
                    ev = sch.op("v", (lambda v, d_=VFM(512 * si, n), p_=PB(bank, 0, n):
                                      v.tensor_scalar(d_, p_, BIASB[:, 41:42], None, ALU.add)),
                                waits=[last])
                psum_free[bank] = ev
                kv_done = ev
            ystg_free[pb] = sch.op("t", lambda t: t.nop(), mark=True)
            if si + 2 < 5:
                emit_kv_cast(si + 2, extra=[ystg_free[pb]])
        vt_done = None
        for sub in range(NSL):
            bank = 4 + (sub % 2)
            tt = sch.op("t", (lambda t, o_=PB(bank, 0, 128, BF16),
                              i_=VFM(sub * 128, 128): t.transpose(o_, i_, IDB)),
                        waits=[kv_done, l_idb] + ([psum_free.get(bank)] if psum_free.get(bank) else []),
                        mark=True)
            slot0 = (sub + 1) * 2
            dst3 = TY3.bitcast(BF16)[:, 37888 // 2 + slot0 * 65: 37888 // 2 + slot0 * 65 + 2 * 65].rearrange(
                "p (s c) -> p s c", c=65)[:, :, 0:64]
            src3 = PB(bank, 0, 128, BF16).rearrange("p (s c) -> p s c", c=64)
            ev = sch.op("v", (lambda v, d_=dst3, s_=src3: v.tensor_copy(d_, s_)),
                        waits=[tt, z5])
            psum_free[bank] = ev
            vt_done = ev

        l_w1o = None
        for ic in range(8):
            l_w1o = sch.dma("g", "da", (lambda g, s_=W1O[128 * ic:128 * ic + 128, :],
                                        d_=TEV(24576, BF16, 8 * D)[:, ic * D:(ic + 1) * D]:
                                        g.dma_start(out=d_, in_=s_)),
                            waits=[l_wq1, tp2_nop, p2v_last] if ic == 0 else [])
        l_b1o = sch.dma("g", "dgm", lambda g: g.dma_start(out=ROWR[0:1, 0:1024], in_=B1O),
                        waits=[t_fbrow, tp2_nop, p2v_last])

        def AF1(p, c, a, n):
            base = p * 3072
            return TY2.bitcast(BF16)[:, base + c * QT + a: base + c * QT + a + n]

        def Y3B(p):
            return TY2.bitcast(BF16)[:, 6144 + p * 1024: 6144 + p * 1024 + 1024]

        Z3AP = TEV(12288, F32, 1024)
        JUNK3 = TEV(16384, F32, 1024)

        p4s = {"oproj": {}, "z3": {}, "spill": {}, "af1": {}, "ln3e": {},
               "n": 0, "nb": 0}

        def p4_slice(sl):
            p = sl % 2
            g = sl // 3
            pg = g % 2
            if sl % 3 == 0:
                tldA = None
                for c in range(8):
                    w = []
                    if c == 0:
                        w = [("dsp", sch.cnt["dsp"])]
                        if p4s["oproj"].get(3 * (g - 2) + 2):
                            w.append(p4s["oproj"][3 * (g - 2) + 2])
                        if g < 2:
                            w += [tp2_nop, p2v_last]
                    tldA = sch.dma("y", "dw2" if pg == 0 else "dw3",
                                   (lambda y, s_=ATT1[QT * g:QT * g + QT,
                                                      128 * c:128 * c + 128],
                                    d_=AF1(pg, c, 0, QT): y.dma_start_transpose(d_, s_)),
                                   waits=w)
                p4s["af1"][g] = tldA
            tldA = p4s["af1"][g]
            wy = [p4s["z3"][sl - 2]] if sl >= 2 else [("dy2", 16 * NSL), p2v_last]
            ldy2 = sch.dma("g", "dx0" if p == 0 else "dx1",
                           (lambda g, d_=TEV(20480 + p * 2048, BF16, 1024),
                            s_=Y2SP[128 * sl:128 * sl + 128, :]:
                            g.dma_start(out=d_, in_=s_)), waits=wy)
            o_marks = []
            for nh in range(2):
                bank = 4 + nh
                for ic in range(8):
                    w = []
                    if ic == 0:
                        w = [tldA, l_w1o, l_b1o]
                        if psum_free.get(bank):
                            w.append(psum_free[bank])
                    sch.op("t", (lambda t, o_=PB(bank, 0, 512),
                                 l_=AF1(pg, ic, (sl % 3) * 128, 128),
                                 r_=TEV(24576, BF16, 8 * D)[:, ic * D + nh * 512: ic * D + nh * 512 + 512],
                                 s0=(ic == 0): t.matmul(o_, l_, r_, start=s0, stop=False)),
                           waits=w)
                last = sch.op("t", (lambda t, o_=PB(bank, 0, 512),
                                    r_=ROWR[0:1, nh * 512:(nh + 1) * 512]:
                                    t.matmul(o_, ONER, r_, start=False, stop=True)), mark=True)
                o_marks.append(last)
            p4s["oproj"][sl] = o_marks[-1]
            ev3 = None
            for nh in range(2):
                ev3 = sch.op("v", (lambda v, d_=Z3AP[:, nh * 512:(nh + 1) * 512],
                                   a_=PB(4 + nh, 0, 512),
                                   b_=TEV(20480 + p * 2048, BF16, 1024)[:, nh * 512:(nh + 1) * 512]:
                                   v.tensor_tensor(d_, a_, b_, ALU.add)),
                             waits=[o_marks[nh], ldy2])
                psum_free[4 + nh] = ev3
            p4s["z3"][sl] = ev3
            t3s = sch.op("v", (lambda v: v.tensor_scalar(JUNK3, Z3AP, 1.0, 0.0,
                                                         ALU.mult, ALU.add,
                                                         accum_out=SMALL[:, 70:71])),
                         waits=[ev3])
            t3q = sch.op("v", (lambda v: v.scalar_tensor_tensor(JUNK3, Z3AP, 1.0, Z3AP,
                                                        ALU.mult, ALU.mult,
                                                        accum_out=SMALL[:, 71:72])))
            t3m = sch.op("v", lambda v: v.tensor_scalar(SMALL[:, 72:73], SMALL[:, 70:71],
                                                        -1.0 / 1024.0, None, ALU.mult))
            t3m2 = sch.op("v", lambda v: v.scalar_tensor_tensor(SMALL[:, 73:74], SMALL[:, 72:73],
                                                                1.0, SMALL[:, 72:73],
                                                                ALU.mult, ALU.mult))
            t3v = sch.op("v", lambda v: v.scalar_tensor_tensor(SMALL[:, 74:75], SMALL[:, 71:72],
                                                               1.0 / 1024.0, SMALL[:, 73:74],
                                                               ALU.mult, ALU.subtract))
            t3l = sch.op("s", lambda ss: ss.activation(SMALL[:, 74:75], SMALL[:, 74:75], AF.Ln,
                                                       bias=BIASB[:, 61:62]), waits=[t3v])
            t3e = sch.op("s", lambda ss: ss.activation(SMALL[:, 74:75], SMALL[:, 74:75], AF.Exp,
                                                       scale=-0.5))
            p4s["ln3e"][sl] = t3e

        def p4_apply(sl):
            p = sl % 2
            wa = [p4s["ln3e"][sl]]
            if sl >= 2:
                wa.append(p4s["spill"][sl - 2])
            t3a = sch.op("v", (lambda v, y_=Y3B(p): v.tensor_scalar(y_, Z3AP,
                                                          SMALL[:, 72:73], SMALL[:, 74:75],
                                                          ALU.add, ALU.mult)),
                         waits=wa)
            wsp = [t3a]
            if sl >= 1:
                wsp.append(p4s["spill"][sl - 1])
            p4s["spill"][sl] = sch.dma("y", "dy3", (lambda y, r0=128 * sl, y_=Y3B(p):
                                                    y.dma_start(out=Y3SP[r0:r0 + 128, :],
                                                                in_=y_)),
                                       waits=wsp)

        def p4_pump(avail, budget):
            done = 0
            while done < budget:
                k = p4s["nb"]
                if k < NSL and p4s["n"] >= k + 1:
                    p4_apply(k); p4s["nb"] += 1; done += 1; continue
                k = p4s["n"]
                if k < NSL and k <= avail and p4s["nb"] >= k:
                    p4_slice(k); p4s["n"] += 1; done += 1; continue
                break

        st1 = {"exp_prev": {0: None, 1: None, 2: None}, "pv_norm": None,
               "b6_free": None, "atm_free": None, "expt_free": None}
        q_cast = {}

        def emit_q_cast(qt, extra=()):
            pb = qt % 2
            dls = "dl0" if pb == 0 else "dl1"
            tok = None
            for ic in range(8):
                w = list(extra) if ic == 0 else []
                tok = sch.dma("y", dls, (lambda y, d_=YSTG(pb, ic, QT),
                                         s_=Y2SP[qt * QT:(qt + 1) * QT,
                                                 128 * ic:128 * ic + 128]:
                                         y.dma_start_transpose(d_, s_)),
                              waits=w)
            q_cast[qt] = tok

        emit_q_cast(0, [ystg_free[0], vt_done])
        emit_q_cast(1, [ystg_free[1], vt_done])
        qfree = {}
        for qt in range(NQT):
            qcast = q_cast[qt]
            q_marks = []
            for oc in range(8):
                bank = oc % 2
                last = None
                for ic in range(8):
                    w = []
                    if ic == 0:
                        w = [l_wq1, l_b1q, vt_done, qcast]
                        if psum_free.get(bank):
                            w.append(psum_free[bank])
                    last = sch.op("t", (lambda t, o_=PB(bank, 0, QT),
                                        l_=TY3V(0, BF16, 8 * D)[:, ic * D + oc * 128: ic * D + oc * 128 + 128],
                                        r_=YSTG(qt % 2, ic, QT),
                                        s0=(ic == 0), s1=(ic == 7):
                                        t.matmul(o_, l_, r_, start=s0, stop=s1)),
                                  waits=w, mark=(ic == 7))
                ev = sch.op("v", (lambda v, d_=Q1B(oc, 0, QT), p_=PB(bank, 0, QT),
                                  b_=BIASB[:, 32 + oc:33 + oc]:
                                  v.tensor_scalar(d_, p_, b_, None, ALU.add)),
                            waits=[last] + ([st1["pv_norm"]] if st1["pv_norm"] else []))
                psum_free[bank] = ev
                q_marks.append(ev)
            q_ready = q_marks[-1]
            qfree[qt] = sch.op("t", lambda t: t.nop(), mark=True)
            if qt + 2 < NQT:
                emit_q_cast(qt + 2, [qfree[qt]])
            for hp in range(8):
                exp_toks = {}
                for sg in range(5):
                    sb = qt * 3 + sg
                    lo, hi = SIG_RANGE[sg]
                    w_ = hi - lo
                    par = sg % 2
                    set_base = 2 * par
                    fw = [q_ready, l_mask]
                    if st1["exp_prev"][par]:
                        fw.append(st1["exp_prev"][par])
                    last = None
                    for hh in (0, 1):
                        bank = set_base + hh
                        lhsT = K1B(sb * 128, 128, p0=64 * hh, np_=64)
                        rhs = Q1B(hp, lo, w_, p0=64 * hh, np_=64)
                        ww = fw if hh == 0 else []
                        last = sch.op("t", (lambda t, o_=PB(bank, lo, w_), l_=lhsT, r_=rhs,
                                            tp=(64 * hh, 0):
                                            t.matmul(o_, l_, r_, start=True, stop=True,
                                                     tile_position=tp)),
                                      waits=ww, mark=(hh == 1))
                    ew = [last]
                    if st1["expt_free"]:
                        ew.append(st1["expt_free"])
                    in_ap = PALL[:, set_base * 512 + lo: set_base * 512 + lo + 1024].rearrange(
                        "p (b x) -> p b x", x=512)[:, :, 0:w_]
                    out_ap = TE[:, (sg * 2) * QT + lo: (sg * 2) * QT + lo + w_] if False else None
                    out3 = TE[:, 0:2 * QT].rearrange("p (b x) -> p b x", x=QT)[
                        :, 0:2, 0:1]
                    o3 = TE[:, (sg * 2) * QT + lo: (sg * 2 + 1) * QT + lo + w_] if False else None
                    out_ap = TE[:, (sg * 2) * QT: (sg * 2 + 2) * QT].rearrange(
                        "p (b x) -> p b x", x=QT)[:, :, lo:lo + w_]
                    etok = sch.op("s", (lambda ss, o_=out_ap, i_=in_ap:
                                        ss.activation(o_, i_, AF.Exp,
                                                      bias=BIASB[:, 60:61], scale=1.0)),
                                  waits=ew)
                    st1["exp_prev"][par] = etok
                    mtok = None
                    for hh in (0, 1):
                        mtok = sch.op("v", (lambda v, d_=EXL1(sg * 2 + hh, lo, w_),
                                            m_=MT(sg, lo, w_):
                                            v.tensor_tensor(d_, d_, m_, ALU.mult)),
                                      waits=[etok])
                    exp_toks[sg] = mtok
                pv_first = True
                pv_last = None
                for qs in (0, 1, 2):
                    for hh in (0, 1):
                        for sg in (qs, qs + 1, qs + 2):
                            w = [exp_toks[sg]]
                            if pv_first and st1["b6_free"]:
                                w.append(st1["b6_free"])
                            lhsT = EXL1(sg * 2 + hh, qs * 128, 128)
                            rhs = V1T((qt * 3 + sg) * 2 + hh, 0, 65)
                            out = PB(6, (hh * 3 + qs) * 65, 65)
                            pv_last = sch.op("t", (lambda t, o_=out, l_=lhsT, r_=rhs,
                                                   s0=pv_first:
                                                   t.matmul(o_, l_, r_, start=s0, stop=False,
                                                            skip_group_check=True)),
                                             waits=w,
                                             mark=(qs == 2 and hh == 1 and sg == qs + 2))
                            pv_first = False
                st1["expt_free"] = pv_last
                dens3 = PB(6, 0, 390).rearrange("p (s c) -> p s c", c=65)[:, :, 64:65]
                t2 = sch.op("v", (lambda v: v.reciprocal(SMALL[:, 86:92], dens3)),
                            waits=[pv_last])
                ntok = None
                for hh in (0, 1):
                    for qs in (0, 1, 2):
                        idx = hh * 3 + qs
                        hcol = (hp + 8 * hh) * 64
                        dst = ATMv(qs * 1024 + hcol, 64)
                        w = [t2]
                        if st1["atm_free"]:
                            w.append(st1["atm_free"])
                        ntok = sch.op("v", (lambda v, d_=dst,
                                            s_=PB(6, idx * 65, 64),
                                            r_=SMALL[:, 86 + idx: 87 + idx]:
                                            v.tensor_scalar(d_, s_, r_, None, ALU.mult)),
                                      waits=w)
                st1["pv_norm"] = ntok
                st1["b6_free"] = ntok
                p4_pump(3 * qt - 1, 1)
            sp = None
            for qs in range(3):
                sp = sch.dma("y", "dsp", (lambda y, r0=qt * QT + qs * 128,
                                          a_=ATMv(qs * 1024, 1024):
                                          y.dma_start(out=ATT1[r0:r0 + 128, :], in_=a_)),
                             waits=[st1["pv_norm"]])
            st1["atm_free"] = sp
            p4_pump(3 * qt + 2, 2)
        att1_done = st1["atm_free"]
        p4_pump(NSL, 10 ** 9)

        if stage == "p3":
            for sl in range(NSL):
                sch.dma("g", "dout", (lambda gg, sl_=sl: gg.dma_start(
                    out=DBG[:, sl_ * 1024: (sl_ + 1) * 1024],
                    in_=ATT1[128 * sl_:128 * sl_ + 128, :])),
                    waits=[("dsp", sch.cnt["dsp"])])
            sch.op("y", lambda y: y.wait_ge(sem_handles["dout"], sch.cnt["dout"]))
            sch.emit(block, sem_handles)
            stack.close()
            return nc

        tp3_nop = sch.op("t", lambda t: t.nop(), mark=True)

        zc = sch.op("v", lambda v: v.memset(SMALL[:, 166:167], 0.0))
        RLG = [(0, 512), (512, 512), (1024, 512), (1536, 512), (2048, 256)]
        rl_grp = {}
        for gi, (a0, ng) in enumerate(RLG):
            dl = "dl0" if gi % 2 == 0 else "dl1"
            tok = None
            for c in range(8):
                w = []
                if c == 0:
                    w = [("dy3", 16 * ((a0 + ng) // 128)), tp3_nop, st1["pv_norm"]]
                    if gi >= 2:
                        w.append(rl_grp[gi - 2])
                tok = sch.dma("y", dl, (lambda y, d_=Y3C(c, a0, ng),
                                        s_=Y3SP[a0:a0 + ng, 128 * c:128 * c + 128]:
                                        y.dma_start_transpose(d_, s_)), waits=w)
            rl_grp[gi] = tok

        if stage == "p4":
            for sl in range(NSL):
                sch.dma("g", "dout", (lambda gg, sl_=sl: gg.dma_start(
                    out=DBG[:, sl_ * 1024: (sl_ + 1) * 1024],
                    in_=Y3SP[128 * sl_:128 * sl_ + 128, :])),
                    waits=[("dy3", 16 * NSL)])
            sch.op("y", lambda y: y.wait_ge(sem_handles["dout"], sch.cnt["dout"]))
            sch.emit(block, sem_handles)
            stack.close()
            return nc

        tp4_nop = sch.op("t", lambda t: t.nop(), mark=True)

        l_mur = sch.dma("y", "dm", lambda y: y.dma_start(out=BIASB[:, 42:50], in_=MUR),
                        waits=[l_mask])
        l_muk = sch.dma("y", "dm", lambda y: y.dma_start(out=BIASB[:, 50:58], in_=MUK),
                        waits=[l_mur])
        l_wr = None
        for ic in range(8):
            l_wr = sch.dma("g", "da", (lambda g, s_=WRT[128 * ic:128 * ic + 128, :],
                                       d_=TY2V(0, BF16, 8 * D)[:, ic * D:(ic + 1) * D]:
                                       g.dma_start(out=d_, in_=s_)),
                           waits=[tp4_nop, ("dy3", 16 * NSL), l_w1o] if ic == 0 else [])
        TILES = [(0, 512), (512, 512), (1024, 512), (1536, 512), (2048, 256)]

        def rl_need(ti):
            return [rl_grp[ti]] + ([rl_grp[ti - 1]] if ti >= 1 else [])
        stash_tok = None
        for ti in range(1, 5):
            a = TILES[ti][0]
            for c in range(8):
                stash_tok = sch.op("v", (lambda v, d_=SMALL[:, 96 + (ti - 1) * 8 + c: 97 + (ti - 1) * 8 + c],
                                         s_=Y3C(c, a - 1, 1): v.tensor_copy(d_, s_)),
                                   waits=[rl_grp[ti - 1]] if c == 0 else [])
        r_done = None
        xr_free = None
        for ti, (a, n) in enumerate(TILES):
            xr_toks = []
            for ic in range(8):
                if ti == 0:
                    td = sch.op("v", (lambda v, d_=TEV(36864, F32, 512)[:, 1:n],
                                      p_=Y3C(ic, 1, n - 1), q_=Y3C(ic, 0, n - 1):
                                      v.tensor_tensor(d_, p_, q_, ALU.subtract)),
                                waits=rl_need(ti) + [zc, l_muk] + ([xr_free] if xr_free else []))
                    td2 = sch.op("v", (lambda v, d_=TEV(36864, F32, 512)[:, 0:1],
                                       p_=Y3C(ic, 0, 1), q_=SMALL[:, 166:167]:
                                       v.tensor_tensor(d_, p_, q_, ALU.subtract)))
                    tx = sch.op("v", (lambda v, d_=TY2V(32768, BF16, 8 * 512)[:, ic * 512 + 1: ic * 512 + n],
                                      s_=TEV(36864, F32, 512)[:, 1:n], m_=BIASB[:, 42 + ic:43 + ic],
                                      q_=Y3C(ic, 0, n - 1):
                                      v.scalar_tensor_tensor(d_, s_, m_, q_, ALU.mult, ALU.add)))
                    tx = sch.op("v", (lambda v, d_=TY2V(32768, BF16, 8 * 512)[:, ic * 512: ic * 512 + 1],
                                      s_=TEV(36864, F32, 512)[:, 0:1], m_=BIASB[:, 42 + ic:43 + ic],
                                      q_=SMALL[:, 166:167]:
                                      v.scalar_tensor_tensor(d_, s_, m_, q_, ALU.mult, ALU.add)))
                else:
                    td = sch.op("v", (lambda v, d_=TEV(36864, F32, 512)[:, 0:n],
                                      p_=Y3C(ic, a, n), q_=Y3C(ic, a - 1, n):
                                      v.tensor_tensor(d_, p_, q_, ALU.subtract)),
                                waits=rl_need(ti) + [zc, l_muk] + ([xr_free] if xr_free else []))
                    tx = sch.op("v", (lambda v, d_=TY2V(32768, BF16, 8 * 512)[:, ic * 512: ic * 512 + n],
                                      s_=TEV(36864, F32, 512)[:, 0:n], m_=BIASB[:, 42 + ic:43 + ic],
                                      q_=Y3C(ic, a - 1, n):
                                      v.scalar_tensor_tensor(d_, s_, m_, q_, ALU.mult, ALU.add)))
                xr_toks.append(tx)
            for oc in range(8):
                bank = oc % 2
                last = None
                for ic in range(8):
                    w = []
                    if ic == 0:
                        w = [xr_toks[-1], l_wr]
                        if psum_free.get(bank):
                            w.append(psum_free[bank])
                    last = sch.op("t", (lambda t, o_=PB(bank, 0, n),
                                        l_=TY2V(0, BF16, 8 * D)[:, ic * D + oc * 128: ic * D + oc * 128 + 128],
                                        r_=TY2V(32768, BF16, 8 * 512)[:, ic * 512: ic * 512 + n],
                                        s0=(ic == 0), s1=(ic == 7):
                                        t.matmul(o_, l_, r_, start=s0, stop=s1)),
                                  waits=w, mark=(ic == 7))
                ev = sch.op("s", (lambda ss, d_=TE[:, oc * SPAN + a: oc * SPAN + a + n],
                                  p_=PB(bank, 0, n):
                                  ss.activation(d_, p_, AF.Sigmoid)),
                            waits=[last])
                psum_free[bank] = ev
                r_done = ev
            xr_free = sch.op("t", lambda t: t.nop(), mark=True)
        l_wk = None
        for ic in range(8):
            l_wk = sch.dma("g", "db", (lambda g, s_=WKT[128 * ic:128 * ic + 128, :],
                                       d_=TY2V(0, BF16, 8 * D)[:, ic * D:(ic + 1) * D]:
                                       g.dma_start(out=d_, in_=s_)),
                           waits=[r_done] if ic == 0 else [])
        l_wv = None
        for ic in range(8):
            l_wv = sch.dma("g", "db", (lambda g, s_=WKT[0:1, 0:1] if False else WVT[128 * ic:128 * ic + 128, :],
                                       d_=TY2V(49152, BF16, 8 * D)[:, ic * D:(ic + 1) * D]:
                                       g.dma_start(out=d_, in_=s_)),
                           waits=[l_wk] if ic == 0 else [])
        out_tok = None
        xk_free = None
        y4_free = [None]
        for ti, (a, n) in enumerate(TILES):
            xk_toks = []
            for ic in range(8):
                dslot = TEV(36864, F32, 512)
                std = (SMALL[:, 166:167] if ti == 0 else
                       SMALL[:, 96 + (ti - 1) * 8 + ic: 97 + (ti - 1) * 8 + ic])
                td = sch.op("v", (lambda v, d_=dslot[:, 1:n],
                                  p_=Y3C(ic, 1 + a, n - 1), q_=Y3C(ic, a, n - 1):
                                  v.tensor_tensor(d_, p_, q_, ALU.subtract)),
                            waits=[l_wk, l_muk, stash_tok] + ([xk_free] if xk_free else [])
                                  + ([y4_free[0]] if y4_free[0] else []))
                td2 = sch.op("v", (lambda v, d_=dslot[:, 0:1],
                                   p_=Y3C(ic, a, 1), q_=std:
                                   v.tensor_tensor(d_, p_, q_, ALU.subtract)))
                tx = sch.op("v", (lambda v, d_=TY2V(32768, BF16, 8 * 512)[:, ic * 512 + 1: ic * 512 + n],
                                  s_=dslot[:, 1:n], m_=BIASB[:, 50 + ic:51 + ic],
                                  q_=Y3C(ic, a, n - 1):
                                  v.scalar_tensor_tensor(d_, s_, m_, q_, ALU.mult, ALU.add)))
                tx = sch.op("v", (lambda v, d_=TY2V(32768, BF16, 8 * 512)[:, ic * 512: ic * 512 + 1],
                                  s_=dslot[:, 0:1], m_=BIASB[:, 50 + ic:51 + ic],
                                  q_=std: v.scalar_tensor_tensor(d_, s_, m_, q_, ALU.mult, ALU.add)))
                xk_toks.append(tx)
            kact_done = None
            for oc in range(8):
                bank = oc % 2
                last = None
                for ic in range(8):
                    w = []
                    if ic == 0:
                        w = [xk_toks[-1]]
                        if psum_free.get(bank):
                            w.append(psum_free[bank])
                    last = sch.op("t", (lambda t, o_=PB(bank, 0, n),
                                        l_=TY2V(0, BF16, 8 * D)[:, ic * D + oc * 128: ic * D + oc * 128 + 128],
                                        r_=TY2V(32768, BF16, 8 * 512)[:, ic * 512: ic * 512 + n],
                                        s0=(ic == 0), s1=(ic == 7):
                                        t.matmul(o_, l_, r_, start=s0, stop=s1)),
                                  waits=w, mark=(ic == 7))
                trl = sch.op("v", (lambda v, d_=TEV(38912, F32, 512)[:, 0:n], p_=PB(bank, 0, n):
                                   v.tensor_scalar(d_, p_, 0.0, None, ALU.max)),
                             waits=[last])
                tsq2 = sch.op("v", (lambda v, d_=TY2V(65536, BF16, 8 * 512)[:, oc * 512: oc * 512 + n],
                                    s_=TEV(38912, F32, 512)[:, 0:n]:
                                    v.tensor_tensor(d_, s_, s_, ALU.mult)))
                psum_free[bank] = tsq2
                kact_done = tsq2
            xk_free = sch.op("t", lambda t: t.nop(), mark=True)
            z4_done = None
            for oc in range(8):
                bank = 2 + (oc % 2)
                last = None
                for ic in range(8):
                    w = []
                    if ic == 0:
                        w = [kact_done, l_wv]
                        if psum_free.get(bank):
                            w.append(psum_free[bank])
                    last = sch.op("t", (lambda t, o_=PB(bank, 0, n),
                                        l_=TY2V(49152, BF16, 8 * D)[:, ic * D + oc * 128: ic * D + oc * 128 + 128],
                                        r_=TY2V(65536, BF16, 8 * 512)[:, ic * 512: ic * 512 + n],
                                        s0=(ic == 0), s1=(ic == 7):
                                        t.matmul(o_, l_, r_, start=s0, stop=s1)),
                                  waits=w, mark=(ic == 7))
                to1 = sch.op("v", (lambda v, d_=TEV(38912, F32, 512)[:, 0:n], p_=PB(bank, 0, n),
                                   r_=TE[:, oc * SPAN + a: oc * SPAN + a + n]:
                                   v.tensor_tensor(d_, p_, r_, ALU.mult)),
                             waits=[last])
                tz4 = sch.op("v", (lambda v, d_=Y3C(oc, a, n),
                                   s_=TEV(38912, F32, 512)[:, 0:n]:
                                   v.tensor_tensor(d_, d_, s_, ALU.add)))
                psum_free[bank] = tz4
                z4_done = tz4
            for ls in range(n // 128):
                tok0 = a + ls * 128
                tts = []
                for c in range(8):
                    bank = 4
                    tt = sch.op("t", (lambda t, o_=PB(bank, c * 128, 128, BF16),
                                      i_=Y3C(c, tok0, 128):
                                      t.transpose(o_, i_, IDB)),
                                waits=[z4_done] + ([psum_free[bank]] if psum_free.get(bank) else []),
                                mark=True)
                    tts.append(tt)
                    psum_free[bank] = None
                z4p = PB(4, 0, 1024, BF16)
                junk = TY3V(40960, F32, 1024)
                t4s = sch.op("v", (lambda v, z_=z4p, j_=junk: v.tensor_scalar(
                    j_, z_, 1.0, 0.0, ALU.mult, ALU.add, accum_out=SMALL[:, 70:71])),
                             waits=[tts[-1]] + ([psum_free[5]] if psum_free.get(5) else []))
                t4q = sch.op("s", (lambda ss, z_=z4p, j_=junk: ss.activation(
                    j_, z_, AF.Square, accum_out=SMALL[:, 71:72])),
                             waits=[t4s])
                t4m = sch.op("v", lambda v: v.tensor_scalar(SMALL[:, 72:73], SMALL[:, 70:71],
                                                            -1.0 / 1024.0, None, ALU.mult),
                             waits=[t4q])
                t4m2 = sch.op("v", lambda v: v.scalar_tensor_tensor(SMALL[:, 73:74], SMALL[:, 72:73],
                                                                    1.0, SMALL[:, 72:73],
                                                                    ALU.mult, ALU.mult))
                t4v = sch.op("v", lambda v: v.scalar_tensor_tensor(SMALL[:, 74:75], SMALL[:, 71:72],
                                                                   1.0 / 1024.0, SMALL[:, 73:74],
                                                                   ALU.mult, ALU.subtract))
                t4l = sch.op("s", lambda ss: ss.activation(SMALL[:, 74:75], SMALL[:, 74:75], AF.Ln,
                                                           bias=BIASB[:, 61:62]), waits=[t4v])
                t4e = sch.op("s", lambda ss: ss.activation(SMALL[:, 74:75], SMALL[:, 74:75], AF.Exp,
                                                           scale=-0.5))
                y4ap = TEV(36864, F32, 1024)
                t4a = sch.op("v", (lambda v, z_=z4p, y_=y4ap: v.tensor_scalar(
                    y_, z_, SMALL[:, 72:73], SMALL[:, 74:75], ALU.add, ALU.mult)),
                             waits=[t4e, y4_free[0]] if y4_free[0] else [t4e])
                for bank in (4, 5):
                    psum_free[bank] = t4a
                out_tok = sch.dma("y", "dout", (lambda y, r0=tok0, z_=y4ap:
                                                y.dma_start(out=YOUT[r0:r0 + 128, :], in_=z_)),
                                  waits=[t4a])
                y4_free[0] = out_tok

        sch.op("y", lambda y: y.wait_ge(sem_handles["dout"], sch.cnt["dout"]))

        sch.emit(block, sem_handles)
        stack.close()
    return nc


def make_inputs(core, inp):
    b = core // 2
    half = core % 2
    s0 = 0 if half == 0 else SEQ - SPAN
    x = np.asarray(inp["L0_attn_x"][b], np.float32)
    xT = np.ascontiguousarray(x.T)
    perm = np.array([[h, h + 8] for h in range(8)], np.int64).reshape(-1)
    colperm = (perm[:, None] * 64 + np.arange(64)[None, :]).reshape(-1)
    wq = np.asarray(inp["L1_attn_Wq"], np.float32)
    bq = np.asarray(inp["L1_attn_bq"], np.float32)
    i = np.arange(128)[:, None]
    j = np.arange(QT)[None, :]
    masks = np.zeros((128, 5 * QT), np.float32)
    masks[:, 0 * QT:0 * QT + 128] = (i >= j[:, 0:128]).astype(np.float32)
    masks[:, 1 * QT:1 * QT + 256] = ((j[:, 0:256] - i) <= 128).astype(np.float32)
    masks[:, 2 * QT:3 * QT] = (((j - i) >= 1) & ((j - i) <= 256)).astype(np.float32)
    masks[:, 3 * QT + 128:4 * QT] = ((j[:, 128:QT] - i) >= 129).astype(np.float32)
    masks[:, 4 * QT + 256:5 * QT] = ((j[:, 256:QT] - i) >= 257).astype(np.float32)
    d = {
        "XSF": np.ascontiguousarray(xT[:, s0:s0 + SPAN]),
        "XFF": xT,
        "XST": np.ascontiguousarray(x[s0:s0 + SPAN]),
        "QW": np.asarray(inp["L0_attn_qW"], np.float32),
        "QBI": np.asarray(inp["L0_attn_qb"], np.float32),
        "KW": np.asarray(inp["L0_attn_kW"], np.float32),
        "KBI": np.asarray(inp["L0_attn_kb"], np.float32),
        "VW": np.asarray(inp["L0_attn_vW"], np.float32),
        "VBI": np.asarray(inp["L0_attn_vb"], np.float32),
        "OW": np.asarray(inp["L0_attn_oW"], np.float32),
        "OBI": np.asarray(inp["L0_attn_ob"], np.float32),
        "JV": np.asarray(inp["L0_attn_J"], np.float32).reshape(8, 128),
        "FW": np.asarray(inp["L0_ffn_weight"], np.float32),
        "FB": np.asarray(inp["L0_ffn_bias"], np.float32).reshape(1, D),
        "W1Q": np.ascontiguousarray(wq[:, colperm]),
        "B1Q": np.ascontiguousarray(bq[colperm]).reshape(8, 128).T.copy(),
        "W1K": np.asarray(inp["L1_attn_Wk"], np.float32),
        "B1K": np.asarray(inp["L1_attn_bk"], np.float32).reshape(128, 1),
        "W1V": np.asarray(inp["L1_attn_Wv"], np.float32),
        "B1VC": np.asarray(inp["L1_attn_bv"], np.float32).reshape(128, 1),
        "W1O": np.asarray(inp["L1_attn_Wo"], np.float32),
        "B1O": np.asarray(inp["L1_attn_bo"], np.float32).reshape(1, D),
        "WRT": np.ascontiguousarray(np.asarray(inp["L1_ffn_W_r"], np.float32).T),
        "WKT": np.ascontiguousarray(np.asarray(inp["L1_ffn_W_k"], np.float32).T),
        "WVT": np.ascontiguousarray(np.asarray(inp["L1_ffn_W_v"], np.float32).T),
        "MUR": np.asarray(inp["L1_ffn_mu_r"], np.float32).reshape(8, 128).T.copy(),
        "MUK": np.asarray(inp["L1_ffn_mu_k"], np.float32).reshape(8, 128).T.copy(),
        "MASKS": masks.astype(ml_dtypes.bfloat16),
        "IDENT": np.eye(128, dtype=np.float32),
    }
    return d


def kernel(**inputs):
    nc = build_program()
    in_maps = [make_inputs(c, inputs) for c in range(8)]
    res = run_bass_kernel_spmd(nc, in_maps, core_ids=list(range(8)))
    out = np.empty((4, SEQ, D), np.float32)
    for c in range(8):
        b, half = c // 2, c % 2
        v0 = 0 if half == 0 else SPAN - 2048
        out[b, half * 2048:(half + 1) * 2048] = res.results[c]["YOUT"][v0:v0 + 2048]
    return out


# version 14
# speedup vs baseline: 1.1825x; 1.0021x over previous
import numpy as np
import ml_dtypes
import concourse.bass as bass
import concourse.mybir as mybir
from concourse.bass_utils import run_bass_kernel_spmd

F32 = mybir.dt.float32
BF16 = mybir.dt.bfloat16
F32R = mybir.dt.float32r
AF = mybir.ActivationFunctionType
ALU = mybir.AluOpType
AX = mybir.AxisListType

MAGIC = float(np.float32(1.5 * 2 ** 23))
D = 1024
SEQ = 4096
SPAN = 2304
NSL = SPAN // 128
NSUB = SEQ // 128
QT = 384
NQT = SPAN // QT

QUAT_TAB = {
    (0, 0): (0, 1), (0, 1): (1, -1), (0, 2): (2, -1), (0, 3): (3, -1),
    (1, 0): (1, 1), (1, 1): (0, 1), (1, 2): (3, -1), (1, 3): (2, 1),
    (2, 0): (2, 1), (2, 1): (3, 1), (2, 2): (0, 1), (2, 3): (1, -1),
    (3, 0): (3, 1), (3, 1): (2, -1), (3, 2): (1, 1), (3, 3): (0, 1),
}
BIAS_SGN = np.array([[1, -1, -1, -1], [1, 1, 1, -1], [1, -1, 1, 1], [1, 1, -1, 1]],
                    np.float32)

ENGS = ("t", "v", "s", "g", "y")
DMA_SEMS = ("da", "db", "dw0", "dw1", "dw2", "dw3", "dm", "dgm", "dsp", "dy2", "dy3", "dwd", "dl0", "dl1", "dx0", "dx1", "dxa", "dxb", "dout")


class Sched:
    def __init__(self):
        self.ops = {e: [] for e in ENGS}
        self.cnt = {n: 0 for n in ENGS + DMA_SEMS}

    def op(self, eng, fn, waits=(), mark=False):
        waits = list(waits)
        if eng in ("v", "s"):
            mark = True
        if mark:
            waits.append((eng, self.cnt[eng]))
        best = {}
        for (sn, sv) in waits:
            if sv > 0 and sv > best.get(sn, 0):
                best[sn] = sv
        self.ops[eng].append((tuple(best.items()), fn, eng if mark else None, 1))
        if mark:
            self.cnt[eng] += 1
        return (eng, self.cnt[eng])

    def dma(self, eng, sem, fn, waits=()):
        best = {}
        for (sn, sv) in waits:
            if sv > 0 and sv > best.get(sn, 0):
                best[sn] = sv
        self.ops[eng].append((tuple(best.items()), fn, sem, 16))
        self.cnt[sem] += 16
        return (sem, self.cnt[sem])

    def emit(self, block, sems):
        def run(engobj, eng):
            for waits, fn, incsem, incval in self.ops[eng]:
                for (sn, sv) in waits:
                    engobj.wait_ge(sems[sn], sv)
                inst = fn(engobj)
                if incsem is not None:
                    inst.then_inc(sems[incsem], incval)

        @block.tensor
        def _(t):
            run(t, "t")

        @block.vector
        def _(v):
            run(v, "v")

        @block.scalar
        def _(s):
            run(s, "s")

        @block.gpsimd
        def _(g):
            run(g, "g")

        @block.sync
        def _(y):
            run(y, "y")


def build_program(stage="full"):
    nc = bass.Bass("TRN2", target_bir_lowering=False)

    def din(name, shape, dt=F32):
        return nc.dram_tensor(name, shape, dt, kind="ExternalInput").ap()

    XSF = din("XSF", [D, SPAN])
    XFF = din("XFF", [D, SEQ])
    XST = din("XST", [SPAN, D])
    QW = din("QW", [4, 256, 256]); QBI = din("QBI", [4, 256])
    KW = din("KW", [4, 256, 256]); KBI = din("KBI", [4, 256])
    VW = din("VW", [4, 256, 256]); VBI = din("VBI", [4, 256])
    OW = din("OW", [4, 256, 256]); OBI = din("OBI", [4, 256])
    JV = din("JV", [8, 128])
    FW = din("FW", [D, D]); FB = din("FB", [1, D])
    W1Q = din("W1Q", [D, D]); B1Q = din("B1Q", [128, 8])
    W1K = din("W1K", [D, 128]); B1K = din("B1K", [128, 1])
    W1V = din("W1V", [D, 128]); B1VC = din("B1VC", [128, 1])
    W1O = din("W1O", [D, D]); B1O = din("B1O", [1, D])
    WRT = din("WRT", [D, D]); WKT = din("WKT", [D, D]); WVT = din("WVT", [D, D])
    MUR = din("MUR", [128, 8]); MUK = din("MUK", [128, 8])
    MASKS = din("MASKS", [128, 5 * QT], BF16)
    IDENT = din("IDENT", [128, 128])

    ATT0 = nc.dram_tensor("ATT0", [SPAN, D], BF16, kind="Internal").ap()
    Y2SP = nc.dram_tensor("Y2SP", [SPAN, D], BF16, kind="Internal").ap()
    Y3SP = nc.dram_tensor("Y3SP", [SPAN, D], BF16, kind="Internal").ap()
    ATT1 = nc.dram_tensor("ATT1", [SPAN, D], BF16, kind="Internal").ap()
    YOUT = nc.dram_tensor("YOUT", [SPAN, D], F32, kind="ExternalOutput").ap()
    DBG = nc.dram_tensor("DBG", [128, 8 * SPAN], F32, kind="ExternalOutput").ap()

    sch = Sched()

    import contextlib
    stack = contextlib.ExitStack()
    with (
        nc.sbuf_tensor("TY2", [128, 18440], F32) as TY2t,
        nc.sbuf_tensor("TY3", [128, 18440], F32) as TY3t,
        nc.sbuf_tensor("TE", [128, 20480], BF16) as TEt,
        nc.sbuf_tensor("ROWB", [1, 1280], F32) as ROWB_t,
        nc.sbuf_tensor("ROWR", [1, 2048], F32R) as ROWR_t,
        nc.sbuf_tensor("BIASB", [128, 64], F32) as BIASB_t,
        nc.sbuf_tensor("ONER", [1, 128], F32R) as ONER_t,
        nc.sbuf_tensor("ONEF", [1, 128], F32) as ONEF_t,
        nc.sbuf_tensor("SMALL", [128, 192], F32) as SMALL_t,
        nc.sbuf_tensor("PVAB", [128, 408], F32) as PVA_t,
        nc.sbuf_tensor("IDF", [128, 128], F32) as IDF_t,
        nc.sbuf_tensor("IDB", [128, 128], BF16) as IDB_t,
        nc.psum_tensor("PALL", [128, 4096], F32) as PALL_t,
        nc.Block() as block,
    ):
        sem_handles = {}
        for n in ENGS + DMA_SEMS:
            sem_handles[n] = stack.enter_context(nc.semaphore("SEM_" + n))

        TY2 = TY2t.ap(); TY3 = TY3t.ap(); TE = TEt.ap()
        ROWB = ROWB_t.ap(); ROWR = ROWR_t.ap(); BIASB = BIASB_t.ap()
        ONER = ONER_t.ap(); ONEF = ONEF_t.ap(); SMALL = SMALL_t.ap()
        PVA = PVA_t.ap(); PALL = PALL_t.ap(); IDF = IDF_t.ap(); IDB = IDB_t.ap()

        def QBUF(c, a, n, p0=0, np_=128):
            return TY2.bitcast(BF16)[p0:p0 + np_, c * SPAN + a: c * SPAN + a + n]

        def VTB(slot, a, n, p0=0, np_=128):
            base = 36864 // 2
            return TY2.bitcast(BF16)[p0:p0 + np_, base + slot * 65 + a: base + slot * 65 + a + n]

        def XSG(p, c, a, n):
            base = (36864 + p * 8192) // 2
            return TY2.bitcast(BF16)[:, base + c * 512 + a: base + c * 512 + a + n]

        def P1FS1(c, n):
            return TY2.bitcast(F32)[:, 53248 // 4 + c * 512: 53248 // 4 + c * 512 + n]

        def P1FS2(c, n):
            if c >= 2:
                return PALL[:, 512 * 6: 512 * 6 + n]
            return TY2.bitcast(F32)[:, 69632 // 4 + c * 512: 69632 // 4 + c * 512 + n]

        def TY2V(byte_off, dt, cols, p0=0, np_=128):
            esz = 2 if dt == BF16 else 4
            return TY2.bitcast(dt)[p0:p0 + np_, byte_off // esz: byte_off // esz + cols]

        def KBUF(c, a, n, p0=0, np_=128):
            return TY3.bitcast(BF16)[p0:p0 + np_, c * SEQ + a: c * SEQ + a + n]

        def TY3V(byte_off, dt, cols, p0=0, np_=128):
            esz = 2 if dt == BF16 else 4
            return TY3.bitcast(dt)[p0:p0 + np_, byte_off // esz: byte_off // esz + cols]

        def Y3C(c, a, n, dt=BF16, p0=0, np_=128):
            return TY3.bitcast(dt)[p0:p0 + np_, c * SPAN + a: c * SPAN + a + n]

        def TEV(byte_off, dt, cols, p0=0, np_=128):
            esz = 2 if dt == BF16 else 4
            return TE.bitcast(dt)[p0:p0 + np_, byte_off // esz: byte_off // esz + cols]

        def EXPT(tile, a, n, p0=0, np_=128):
            return TE[p0:p0 + np_, tile * QT + a: tile * QT + a + n]

        def PB(b, a, n, dt=None):
            if dt is not None:
                esz = 2 if dt == BF16 else 4
                scale = 4 // esz
                return PALL.bitcast(dt)[:, scale * 512 * b + a: scale * 512 * b + a + n]
            return PALL[:, 512 * b + a: 512 * b + a + n]

        t_onef = sch.op("v", lambda v: v.memset(ONEF, 1.0))
        t_oner = sch.op("v", lambda v: v.tensor_copy(ONER, ONEF), waits=[t_onef])
        t_eps6 = sch.op("v", lambda v: v.memset(BIASB[:, 62:63], 1e-6))
        t_eps5 = sch.op("v", lambda v: v.memset(BIASB[:, 61:62], 1e-5))
        t_m30 = sch.op("v", lambda v: v.memset(BIASB[:, 60:61], -30.0))
        t_ones8 = sch.op("v", lambda v: v.memset(SMALL[0:8, 131:132], 1.0))

        tJ = sch.dma("y", "dm", lambda y: y.dma_start(out=SMALL[0:8, 0:128], in_=JV))
        t_sin = sch.op("s", lambda s: s.activation(SMALL[32:40, 0:128], SMALL[0:8, 0:128],
                                                   AF.Sin, accum_out=SMALL[0:8, 130:131]),
                       waits=[tJ])
        t_csum = sch.op("t", lambda t: t.matmul(
            PALL[0:1, 0:1], SMALL[0:8, 131:132], SMALL[0:8, 130:131],
            start=True, stop=True), waits=[t_sin, t_ones8], mark=True)
        t_cs = sch.op("v", lambda v: v.tensor_scalar(SMALL[0:1, 132:133], PALL[0:1, 0:1],
                                                     0.1 / 1024.0, None, ALU.mult),
                      waits=[t_csum])
        t_crep = sch.op("t", lambda t: t.matmul(
            PB(1, 0, 1), ONEF, SMALL[0:1, 132:133],
            start=True, stop=True), waits=[t_cs, t_onef], mark=True)
        t_cv = sch.op("v", lambda v: v.tensor_copy(BIASB[:, 63:64], PB(1, 0, 1)),
                      waits=[t_crep])

        dm_chain = [tJ]

        def q_bias_cols(col0, B_ap):
            tok = dm_chain[-1]
            for oc in range(8):
                ob, half = oc // 2, oc % 2
                for w in range(4):
                    sgn = float(BIAS_SGN[ob, w])
                    stg = SMALL[:, 160 + w: 161 + w]
                    tdma = sch.dma("y", "dm", (lambda y, s_=B_ap[w, 128 * half:128 * half + 128].unsqueeze(1),
                                               d_=stg: y.dma_start(out=d_, in_=s_)),
                                   waits=[tok])
                    dcol = BIASB[:, col0 + oc: col0 + oc + 1]
                    if w == 0:
                        tok = sch.op("v", (lambda v, d_=dcol, s_=stg, g_=sgn:
                                           v.tensor_scalar(d_, s_, g_, None, ALU.mult)),
                                     waits=[tdma])
                    else:
                        tok = sch.op("v", (lambda v, d_=dcol, s_=stg, g_=sgn:
                                           v.scalar_tensor_tensor(d_, s_, g_, d_, ALU.mult, ALU.add)),
                                     waits=[tdma])
            dm_chain.append(tok)
            return tok

        def quat_bias_row(dst_off, B_ap, as_f32r_off=None, final_extra=()):
            tok = dm_chain[-1]
            for ob in range(4):
                for w in range(4):
                    sgn = float(BIAS_SGN[ob, w])
                    stg = ROWB[0:1, 1024:1280]
                    tdma = sch.dma("y", "dm", (lambda y, s_=B_ap[w:w + 1, :], d_=stg:
                                               y.dma_start(out=d_, in_=s_)), waits=[tok])
                    dst = ROWB[0:1, dst_off + ob * 256: dst_off + (ob + 1) * 256]
                    if w == 0:
                        tok = sch.op("v", (lambda v, d_=dst, s_=stg, g_=sgn:
                                           v.tensor_scalar(d_, s_, g_, None, ALU.mult)),
                                     waits=[tdma])
                    else:
                        tok = sch.op("v", (lambda v, d_=dst, s_=stg, g_=sgn:
                                           v.scalar_tensor_tensor(d_, s_, g_, d_, ALU.mult, ALU.add)),
                                     waits=[tdma])
            if as_f32r_off is not None:
                tok = sch.op("v", (lambda v, o_=as_f32r_off: v.tensor_copy(
                    ROWR[0:1, o_:o_ + 1024], ROWB[0:1, 0:1024])),
                             waits=[tok] + list(final_extra))
            dm_chain.append(tok)
            return tok

        tbq = q_bias_cols(0, QBI)
        tbk = q_bias_cols(8, KBI)

        t_id = sch.dma("y", "dm", lambda y: y.dma_start(out=IDF, in_=IDENT),
                       waits=[dm_chain[-1]])
        dm_chain.append(t_id)
        l_idb = sch.dma("g", "dgm", lambda g: g.dma_start(out=IDB, in_=IDENT))
        t_fbrow = sch.dma("g", "dgm", lambda g: g.dma_start(
            out=ROWR[0:1, 1024:2048], in_=FB), waits=[l_idb])

        quatw_lastv = [dm_chain[-1]] * 4

        def quat_w(dst_off, W_ap):
            lastv = quatw_lastv
            tok = None
            for ic in range(8):
                comp, half = ic // 2, ic % 2
                for ob in range(4):
                    w_idx, sgn = QUAT_TAB[(ob, comp)]
                    stg = P1FS1(ob, 256)
                    tdma = sch.dma("y", "dw%d" % ob, (lambda y, s_=W_ap[w_idx, 128 * half:128 * half + 128, :],
                                               d_=stg: y.dma_start(out=d_, in_=s_)),
                                   waits=[lastv[ob]])
                    dst = TEV(dst_off, BF16, 8 * D)[:, ic * D + ob * 256: ic * D + ob * 256 + 256]
                    tok = sch.op("v", (lambda v, d_=dst, s_=stg, g_=float(sgn):
                                       v.tensor_scalar(d_, s_, g_, None, ALU.mult)),
                                 waits=[tdma])
                    lastv[ob] = tok
            return tok

        twq = quat_w(0, QW)
        twk = quat_w(16384, KW)

        psum_free = {}
        stg_free = [None, None]
        norm_free = [None]

        def proj_norm(si, n, src_fn, wbig_off, bias_col, dst_fn, extra_first=()):
            p = si % 2
            dsem = "dx0" if p == 0 else "dx1"
            tld = None
            for ic in range(8):
                wt = [stg_free[p]] if stg_free[p] else []
                tld = sch.dma("g", dsem, (lambda g, s_=src_fn(ic), d_=XSG(p, ic, 0, n):
                                          g.dma_start(out=d_, in_=s_)), waits=wt)
            ev = []
            for oc in range(8):
                bank = oc % 4
                last_mm = None
                for ic in range(8):
                    wt = []
                    if ic == 0:
                        wt = [tld] + list(extra_first)
                        if psum_free.get(bank):
                            wt.append(psum_free[bank])
                    lhsT = TEV(wbig_off, BF16, 8 * D)[:, ic * D + oc * 128: ic * D + oc * 128 + 128]
                    rhs = XSG(p, ic, 0, n)
                    last_mm = sch.op("t", (lambda t, o_=PB(bank, 0, n), l_=lhsT, r_=rhs,
                                           s0=(ic == 0), s1=(ic == 7):
                                           t.matmul(o_, l_, r_, start=s0, stop=s1)),
                                     mark=(ic == 7), waits=wt)
                ev.append(sch.op("v", (lambda v, d_=P1FS1(oc, n), p_=PB(bank, 0, n),
                                       b_=BIASB[:, bias_col + oc: bias_col + oc + 1]:
                                       v.tensor_scalar(d_, p_, b_, None, ALU.add)),
                                 waits=[last_mm]))
                psum_free[bank] = ev[-1]
            stg_free[p] = sch.op("t", lambda t: t.nop(), mark=True)

            out_toks = []
            for h in range(2):
                ssum = P1FS2(h, n)
                wt = [ev[h]] + ([norm_free[0]] if norm_free[0] else [])
                t0s = sch.op("s", (lambda ss, d_=ssum, a_=P1FS1(h, n):
                                   ss.activation(d_, a_, AF.Square)),
                             waits=wt)
                t0 = None
                for j in (2, 4, 6):
                    tmp = P1FS2(2, n)
                    tsq = sch.op("s", (lambda ss, d_=tmp, a_=P1FS1(h + j, n):
                                       ss.activation(d_, a_, AF.Square)),
                                 waits=[ev[h + j]] + ([t0] if t0 else []))
                    t0 = sch.op("v", (lambda v, d_=ssum, a_=tmp:
                                      v.tensor_tensor(d_, d_, a_, ALU.add)),
                                waits=[tsq] + ([t0s] if j == 2 else []))
                t2 = sch.op("s", (lambda s, d_=ssum: s.activation(d_, d_, AF.Ln,
                                                                  bias=BIASB[:, 62:63])),
                            waits=[t0, t_eps6])
                t3 = sch.op("s", (lambda s, d_=ssum: s.activation(d_, d_, AF.Exp, scale=-0.5)),
                            waits=[t2])
                out_toks.append(t3)
            final = None
            for oc in range(8):
                h = oc % 2
                final = sch.op("v", (lambda v, d_=dst_fn(oc), a_=P1FS1(oc, n), b_=P1FS2(h, n):
                                     v.tensor_tensor(d_, a_, b_, ALU.mult)),
                               waits=[out_toks[h]])
            norm_free[0] = final
            return final

        wready = (twk, tbk, t_cv)
        k_done = q_done = None
        for si in range(8):
            k_done = proj_norm(
                si, 512,
                (lambda ic, si_=si: XFF[128 * ic:128 * ic + 128, 512 * si_:512 * si_ + 512]),
                16384, 8,
                (lambda oc, si_=si: KBUF(oc, 512 * si_, 512)),
                extra_first=(wready if si == 0 else ()))
        for sj in range(5):
            n = 512 if sj < 4 else 256
            q_done = proj_norm(
                8 + sj, n,
                (lambda ic, s_=sj, n_=n: XSF[128 * ic:128 * ic + 128, 512 * s_:512 * s_ + n_]),
                0, 0,
                (lambda oc, s_=sj, n_=n: QBUF(oc, 512 * s_, n_)))

        if stage == "p1a":
            for c in range(8):
                sch.dma("g", "dout", (lambda g, c_=c, q_=QBUF(c, 0, SPAN): g.dma_start(
                    out=DBG[:, c_ * SPAN: (c_ + 1) * SPAN],
                    in_=q_)), waits=[q_done, k_done])
            sch.op("y", lambda y: y.wait_ge(sem_handles["dout"], sch.cnt["dout"]))
            sch.emit(block, sem_handles)
            stack.close()
            return nc

        tk_t = None
        tkq_nop = sch.op("t", lambda t: t.nop(), mark=True)

        t_vrow = quat_bias_row(0, VBI, as_f32r_off=0)

        def quat_w2(dst_off, W_ap, firstw):
            lastv = [firstw] * 4
            tok = None
            for ic in range(8):
                comp, half = ic // 2, ic % 2
                for ob in range(4):
                    w_idx, sgn = QUAT_TAB[(ob, comp)]
                    stg = TEV(16384 + ob * 1024, F32, 256)
                    tdma = sch.dma("y", "dw%d" % ob,
                                   (lambda y, s_=W_ap[w_idx, 128 * half:128 * half + 128, :],
                                    d_=stg: y.dma_start(out=d_, in_=s_)),
                                   waits=[lastv[ob]])
                    dst = TEV(dst_off, BF16, 8 * D)[:, ic * D + ob * 256: ic * D + ob * 256 + 256]
                    tok = sch.op("v", (lambda v, d_=dst, s_=stg, g_=float(sgn):
                                       v.tensor_scalar(d_, s_, g_, None, ALU.mult)),
                                 waits=[tdma])
                    lastv[ob] = tok
            return tok

        twv = quat_w2(24576, VW, tkq_nop)

        VT3 = TY2.bitcast(BF16)[:, 36864 // 2: 36864 // 2 + 256 * 65].rearrange(
            "p (s c) -> p s c", c=65)
        t_vones = sch.op("v", lambda v: v.memset(VT3[:, :, 64:65], 1.0),
                         waits=[q_done, tkq_nop])

        def XSG2(c, a, n):
            base = 65536 // 2
            return TY3.bitcast(BF16)[:, base + c * 512 + a: base + c * 512 + a + n]

        def ATMv(a, n):
            base = 65536 // 2
            return TY3.bitcast(BF16)[:, base + a: base + a + n]

        def emit_p1b(g, first_waits):
            xsg2_free = None
            done = None
            for blk in range(8):
                wt = list(first_waits) + ([xsg2_free] if xsg2_free else [])
                tld0 = tld1 = None
                for ic in range(8):
                    sem = "db" if ic % 2 == 0 else "dx0"
                    t_ = sch.dma("g", sem,
                                 (lambda gg, s_=XFF[128 * ic:128 * ic + 128, 512 * blk:512 * blk + 512],
                                  d_=XSG2(ic, 0, 512): gg.dma_start(out=d_, in_=s_)),
                                 waits=wt if ic < 2 else [])
                    if ic % 2 == 0:
                        tld0 = t_
                    else:
                        tld1 = t_
                tld = tld0
                for sub4 in range(4):
                    bank = sub4 % 2
                    last = None
                    for ic in range(8):
                        w = []
                        if ic == 0:
                            w = [tld0, tld1]
                            if psum_free.get(bank):
                                w.append(psum_free[bank])
                            if ic == 0 and blk == 0 and sub4 == 0:
                                w.append(twv)
                                w.append(t_vrow)
                        lhsT = XSG2(ic, sub4 * 128, 128)
                        rhs = TEV(24576, BF16, 8 * D)[:, ic * D + g * 512: ic * D + g * 512 + 512]
                        last = sch.op("t", (lambda t, o_=PB(bank, 0, 512), l_=lhsT, r_=rhs,
                                            s0=(ic == 0): t.matmul(o_, l_, r_, start=s0, stop=False)),
                                      waits=w)
                    last = sch.op("t", (lambda t, o_=PB(bank, 0, 512),
                                        r_=ROWR[0:1, g * 512:(g + 1) * 512]:
                                        t.matmul(o_, ONER, r_, start=False, stop=True)),
                                  mark=True)
                    slot0 = (blk * 4 + sub4) * 8
                    dst3 = TY2.bitcast(BF16)[:, 36864 // 2 + slot0 * 65: 36864 // 2 + slot0 * 65 + 8 * 65].rearrange(
                        "p (s c) -> p s c", c=65)[:, :, 0:64]
                    src3 = PB(bank, 0, 512).rearrange("p (s c) -> p s c", c=64)
                    ev = sch.op("v", (lambda v, d_=dst3, s_=src3: v.tensor_copy(d_, s_)),
                                waits=[last, t_vones])
                    psum_free[bank] = ev
                    done = ev
                xsg2_free = sch.op("t", lambda t: t.nop(), mark=True)
            return done, xsg2_free

        def emit_p1c(g, ready, state, hook=None, nsets=2):
            PVB = (6, 7)
            for qt in range(NQT):
                batches = [(hp, H, half) for hp in range(4)
                           for H in (0, 1) for half in (0, 1)]
                etoks = {}

                def emit_A(j, qt=qt, batches=batches, etoks=etoks):
                    hp, H, half = batches[j]
                    kch = g * 4 + hp
                    toks = []
                    for sq in range(8):
                        par = (j * 8 + sq) % nsets
                        set_base = 2 * par
                        fw = []
                        if state["exp_prev"][par]:
                            fw.append(state["exp_prev"][par])
                        if qt == 0 and j == 0 and sq == 0:
                            fw += list(ready)
                        sub = H * 16 + half * 8 + sq
                        last = None
                        for hh in (0, 1):
                            bank = set_base + hh
                            lhsT = KBUF(kch, sub * 128, 128, p0=64 * hh, np_=64)
                            rhs = QBUF(kch, qt * QT, QT, p0=64 * hh, np_=64)
                            w = fw if hh == 0 else []
                            last = sch.op("t", (lambda t, o_=PB(bank, 0, QT), l_=lhsT,
                                                r_=rhs, tp=(64 * hh, 0):
                                                t.matmul(o_, l_, r_, start=True, stop=True,
                                                         tile_position=tp)),
                                          waits=w, mark=(hh == 1))
                        ew = [last]
                        if state["expt_free"][j % 2]:
                            ew.append(state["expt_free"][j % 2])
                        tile0 = (j % 2) * 16 + sq * 2
                        in_ap = PALL[:, set_base * 512: set_base * 512 + 1024].rearrange(
                            "p (b x) -> p b x", x=512)[:, :, 0:QT]
                        out_ap = TE[:, tile0 * QT: (tile0 + 2) * QT].rearrange(
                            "p (b x) -> p b x", x=QT)
                        etok = sch.op("s", (lambda ss, o_=out_ap, i_=in_ap:
                                            ss.activation(o_, i_, AF.Exp,
                                                          bias=BIASB[:, 63:64], scale=0.125)),
                                      waits=ew)
                        state["exp_prev"][par] = etok
                        toks.append(etok)
                    etoks[j] = toks

                def emit_P(j, qt=qt, batches=batches, etoks=etoks):
                    hp, H, half = batches[j]
                    b = PVB[hp % 2]
                    first_of_hp = (H == 0 and half == 0)
                    last_of_hp = (H == 1 and half == 1)
                    pv_last = None
                    for sq in range(8):
                        sub = H * 16 + half * 8 + sq
                        for hh in (0, 1):
                            hl = 2 * hp + hh
                            tile = (j % 2) * 16 + sq * 2 + hh
                            for qs in (0, 1, 2):
                                w = []
                                if hh == 0 and qs == 0:
                                    w.append(etoks[j][sq])
                                    if first_of_hp and sq == 0 and state["pvfree"][b]:
                                        w.append(state["pvfree"][b])
                                st = first_of_hp and sq == 0 and hh == 0 and qs == 0
                                lhsT = EXPT(tile, qs * 128, 128)
                                rhs = VTB(sub * 8 + hl, 0, 65)
                                out = PB(b, (hh * 3 + qs) * 65, 65)
                                pv_last = sch.op("t", (lambda t, o_=out, l_=lhsT, r_=rhs,
                                                       s0=st:
                                                       t.matmul(o_, l_, r_, start=s0, stop=False,
                                                                skip_group_check=True)),
                                                 waits=w,
                                                 mark=(sq == 7 and hh == 1 and qs == 2))
                    state["expt_free"][j % 2] = pv_last
                    if last_of_hp:
                        dens3 = PB(b, 0, 390).rearrange("p (s c) -> p s c", c=65)[:, :, 64:65]
                        t2 = sch.op("v", (lambda v, d3=dens3: v.reciprocal(SMALL[:, 80:86], d3)),
                                    waits=[pv_last])
                        ntok = None
                        for hh in (0, 1):
                            for qs in (0, 1, 2):
                                idx = hh * 3 + qs
                                hl = 2 * hp + hh
                                dst = ATMv(qs * 512 + hl * 64, 64)
                                w = [t2]
                                if state["atm_free"]:
                                    w.append(state["atm_free"])
                                ntok = sch.op("v", (lambda v, d_=dst,
                                                    s_=PB(b, idx * 65, 64),
                                                    r_=SMALL[:, 80 + idx: 81 + idx]:
                                                    v.tensor_scalar(d_, s_, r_, None, ALU.mult)),
                                              waits=w)
                        state["pvfree"][b] = ntok
                        state["pv_norm"] = ntok
                        if hook:
                            hook(3 * qt - 1, 2)

                emit_A(0)
                emit_A(1)
                for j in range(16):
                    emit_P(j)
                    if j + 2 < 16:
                        emit_A(j + 2)
                sp = None
                for qs in range(3):
                    sp = sch.dma("y", "dsp", (lambda y, r0=qt * QT + qs * 128,
                                              a_=ATMv(qs * 512, 512), g_=g:
                                              y.dma_start(out=ATT0[r0:r0 + 128, g_ * 512:(g_ + 1) * 512],
                                                          in_=a_)),
                                 waits=[state["pv_norm"]])
                state["atm_free"] = sp
                if hook:
                    hook(3 * qt + 2, 4)
            return state

        st_p1c = {"exp_prev": {0: None, 1: None, 2: None}, "pv_norm": None,
                  "expt_free": [None, None], "atm_free": None,
                  "pvfree": {6: None, 7: None}}
        v_done0, xf0 = emit_p1b(0, [t_vones])
        st_p1c = emit_p1c(0, [v_done0, tkq_nop, q_done], st_p1c, nsets=3)
        tg0 = sch.op("t", lambda t: t.nop(), mark=True)
        g0_exps = [t for t in (st_p1c["exp_prev"][0], st_p1c["exp_prev"][1],
                               st_p1c["exp_prev"][2]) if t]
        v_done1, xf1 = emit_p1b(1, [st_p1c["atm_free"], st_p1c["pv_norm"]])
        st_p1c["atm_free"] = xf1

        t_orow = quat_bias_row(0, OBI, as_f32r_off=0, final_extra=[xf1])

        def quat_w3(dst_fn, W_ap, firstw, stg_fn, extraw=()):
            lastv = [firstw] * 2
            tok = None
            for ic in range(8):
                comp, half = ic // 2, ic % 2
                for ob in range(4):
                    w_idx, sgn = QUAT_TAB[(ob, comp)]
                    slot = ob % 2
                    stg = stg_fn(slot)
                    tdma = sch.dma("y", "dw%d" % slot,
                                   (lambda y, s_=W_ap[w_idx, 128 * half:128 * half + 128, :],
                                    d_=stg: y.dma_start(out=d_, in_=s_)),
                                   waits=[lastv[slot]] + (list(extraw) if ic == 0 and ob < 2 else []))
                    dst = dst_fn(ic, ob)
                    tok = sch.op("v", (lambda v, d_=dst, s_=stg, g_=float(sgn):
                                       v.tensor_scalar(d_, s_, g_, None, ALU.mult)),
                                 waits=[tdma] + ([xf1] if ic == 0 and ob == 0 else []))
                    lastv[slot] = tok
            return tok

        two = quat_w3(
            (lambda ic, ob: TEV(24576, BF16, 8 * D)[:, ic * D + ob * 256: ic * D + ob * 256 + 256]),
            OW, twv,
            (lambda slot: TEV(22528 + slot * 1024, F32, 256)),
            extraw=[tg0] + g0_exps)

        def emit_quant(src_ap, dst_ap, wait0, slots):
            Aap, Bap, Cap, Dap = slots
            bmax = SMALL[:, 0:32]
            scl = SMALL[:, 32:64]
            rec = SMALL[:, 134:166]
            src3 = src_ap.rearrange("p (b w) -> p b w", w=32)
            t1 = sch.op("v", (lambda v: v.tensor_reduce(bmax, src3, AX.X, ALU.max,
                                                        apply_absolute_value=True)),
                        waits=list(wait0))
            t2 = sch.op("v", lambda v: v.tensor_scalar(scl, bmax, 1.0 / 7.5, 1e-30,
                                                       ALU.mult, ALU.max))
            t3 = sch.op("v", lambda v: v.reciprocal(rec, scl))
            rb = rec.unsqueeze(2).broadcast_to([128, 32, 32])
            sb = scl.unsqueeze(2).broadcast_to([128, 32, 32])
            A3 = Aap.rearrange("p (b w) -> p b w", w=32)
            t4 = sch.op("v", (lambda v: v.tensor_tensor(A3, src3, rb, ALU.mult)))
            t5 = sch.op("v", (lambda v: v.tensor_tensor(Bap, Aap, Aap, ALU.mult)))
            t6 = sch.op("v", lambda v: v.tensor_scalar(Cap, Bap, 4.0, None, ALU.is_lt))
            t7 = sch.op("v", lambda v: v.tensor_scalar(Bap, Bap, 16.0, None, ALU.is_lt))
            t8 = sch.op("v", lambda v: v.tensor_scalar(Dap, Bap, -0.25, 0.5, ALU.mult, ALU.add))
            t9 = sch.op("v", lambda v: v.scalar_tensor_tensor(Dap, Cap, -0.125, Dap,
                                                              ALU.mult, ALU.add))
            ta = sch.op("v", lambda v: v.scalar_tensor_tensor(Bap, Cap, 2.0, Bap,
                                                              ALU.mult, ALU.add))
            tb = sch.op("v", lambda v: v.tensor_scalar(Bap, Bap, 2.0, 2.0, ALU.mult, ALU.add))
            tc = sch.op("v", lambda v: v.tensor_tensor(Cap, Aap, Bap, ALU.mult))
            td = sch.op("v", lambda v: v.tensor_scalar(Cap, Cap, MAGIC, MAGIC,
                                                       ALU.add, ALU.subtract))
            te = sch.op("v", lambda v: v.tensor_tensor(Cap, Cap, Dap, ALU.mult))
            C3 = Cap.rearrange("p (b w) -> p b w", w=32)
            d3 = dst_ap.rearrange("p (b w) -> p b w", w=32)
            tf = sch.op("v", (lambda v: v.tensor_tensor(d3, C3, sb, ALU.mult)))
            return tf

        def QSLOT(i):
            return TY2.bitcast(F32)[:, i * 1024:(i + 1) * 1024]
        P2_SLOTS = (QSLOT(0), QSLOT(1), QSLOT(2), QSLOT(3))
        XDT_AP = TY2.bitcast(BF16)[:, 8192:9216]
        XDF_AP = TY2.bitcast(BF16)[:, 35072:36096]

        def Y1AP(p):
            off = 16384 if p == 0 else 24576
            return TY3.bitcast(F32)[:, off // 4: off // 4 + 1024]
        XTM_AP = TY3.bitcast(F32)[:, 20480 // 4: 20480 // 4 + 1024]

        def AF0(p, c, a, n):
            base = (28672 + p * 2048) // 2
            return TY3.bitcast(BF16)[:, base + c * 128 + a: base + c * 128 + a + n]

        p2s = {"oproj": {}, "y1evac": {}, "quant": {}, "transp": {},
               "ffn0": {}, "z2": {}, "spill": {}, "ln2": {}, "ln1e": {},
               "ln2e": {}}

        def p2_f1(sl):
            p = sl % 2
            dl = "dl0" if p == 0 else "dl1"
            tldA = None
            for c in range(8):
                w = []
                if c == 0:
                    w = [("dsp", sch.cnt["dsp"]), tg0]
                    if p2s["oproj"].get(sl - 2):
                        w.append(p2s["oproj"][sl - 2])
                tldA = sch.dma("y", dl, (lambda y, s_=ATT0[128 * sl:128 * sl + 128,
                                                           128 * c:128 * c + 128],
                                         d_=AF0(p, c, 0, 128):
                                         y.dma_start_transpose(d_, s_)), waits=w)
            w = [tg0] + ([p2s["y1evac"][sl - 1]] if sl >= 1 else [])
            tldX = sch.dma("g", "dxa", (lambda g, s_=XST[128 * sl:128 * sl + 128, :]:
                                        g.dma_start(out=XTM_AP, in_=s_)), waits=w)
            o_marks = []
            for nh in range(2):
                bank = 4 + nh
                for ic in range(8):
                    w = []
                    if ic == 0:
                        w = [tldA, two, t_orow]
                        if psum_free.get(bank):
                            w.append(psum_free[bank])
                    sch.op("t", (lambda t, o_=PB(bank, 0, 512),
                                 l_=AF0(p, ic, 0, 128),
                                 r_=TEV(24576, BF16, 8 * D)[:, ic * D + nh * 512: ic * D + nh * 512 + 512],
                                 s0=(ic == 0): t.matmul(o_, l_, r_, start=s0, stop=False)),
                           waits=w)
                last = sch.op("t", (lambda t, o_=PB(bank, 0, 512),
                                    r_=ROWR[0:1, nh * 512:(nh + 1) * 512]:
                                    t.matmul(o_, ONER, r_, start=False, stop=True)), mark=True)
                o_marks.append(last)
            p2s["oproj"][sl] = o_marks[-1]
            ev1 = None
            for nh in range(2):
                ev1 = sch.op("v", (lambda v, d_=Y1AP(p)[:, nh * 512:(nh + 1) * 512],
                                   a_=PB(4 + nh, 0, 512), b_=XTM_AP[:, nh * 512:(nh + 1) * 512]:
                                   v.tensor_tensor(d_, a_, b_, ALU.add)),
                             waits=[o_marks[nh], tldX])
                psum_free[4 + nh] = ev1
            p2s["y1evac"][sl] = ev1

        def p2_f2(sl):
            p = sl % 2
            y1ap = Y1AP(p)
            tsum = sch.op("v", (lambda v, z_=y1ap: v.tensor_scalar(QSLOT(1), z_, 1.0, 0.0,
                                                          ALU.mult, ALU.add,
                                                          accum_out=SMALL[:, 70:71])),
                          waits=[p2s["y1evac"][sl]])
            tsq = sch.op("v", (lambda v, z_=y1ap: v.scalar_tensor_tensor(QSLOT(2), z_, 1.0, z_,
                                                        ALU.mult, ALU.mult,
                                                        accum_out=SMALL[:, 71:72])))
            tm1 = sch.op("v", lambda v: v.tensor_scalar(SMALL[:, 72:73], SMALL[:, 70:71],
                                                        -1.0 / 1024.0, None, ALU.mult))
            tm2 = sch.op("v", lambda v: v.scalar_tensor_tensor(SMALL[:, 73:74], SMALL[:, 72:73],
                                                               1.0, SMALL[:, 72:73],
                                                               ALU.mult, ALU.mult))
            tvv = sch.op("v", lambda v: v.scalar_tensor_tensor(SMALL[:, 74:75], SMALL[:, 71:72],
                                                               1.0 / 1024.0, SMALL[:, 73:74],
                                                               ALU.mult, ALU.subtract))
            tl1 = sch.op("s", lambda ss: ss.activation(SMALL[:, 74:75], SMALL[:, 74:75], AF.Ln,
                                                       bias=BIASB[:, 61:62]), waits=[tvv, t_eps5])
            tl2 = sch.op("s", lambda ss: ss.activation(SMALL[:, 74:75], SMALL[:, 74:75], AF.Exp,
                                                       scale=-0.5))
            p2s["ln1e"][sl] = tl2

        def p2_f2b(sl):
            p = sl % 2
            y1ap = Y1AP(p)
            tap = sch.op("v", (lambda v, z_=y1ap: v.tensor_scalar(z_, z_, SMALL[:, 72:73],
                                                         SMALL[:, 74:75], ALU.add, ALU.mult)),
                         waits=[p2s["ln1e"][sl]])
            w0 = [tap, p2s["transp"][sl - 1]]
            p2s["quant"][sl] = emit_quant(y1ap, XDT_AP, w0, P2_SLOTS)

        def p2_back(sl):
            p = sl % 2
            last_tt = None
            evs = []
            for c in range(8):
                bank = 4 + (c % 2)
                w = [p2s["quant"][sl], l_idb]
                if psum_free.get(bank):
                    w.append(psum_free[bank])
                last_tt = sch.op("t", (lambda t, o_=PB(bank, (c // 2) * 128, 128, BF16),
                                  i_=XDT_AP[:, c * 128:(c + 1) * 128]:
                                  t.transpose(o_, i_, IDB)),
                            waits=w, mark=True)
                ew = [last_tt]
                if c == 0 and sl >= 1:
                    ew.append(p2s["spill"][sl - 1])
                ev = sch.op("v", (lambda v, d_=XDF_AP[:, c * 128:(c + 1) * 128],
                                  s_=PB(bank, (c // 2) * 128, 128, BF16): v.tensor_copy(d_, s_)),
                            waits=ew)
                psum_free[bank] = ev
                evs.append(ev)
            p2s["transp"][sl] = last_tt
            f_marks = []
            for nh in range(2):
                bank = 4 + nh
                for ic in range(8):
                    w = []
                    if ic == 0:
                        w = [evs[-1], wd_state["done"], t_fbrow]
                        if psum_free.get(bank):
                            w.append(psum_free[bank])
                    sch.op("t", (lambda t, o_=PB(bank, 0, 512),
                                 l_=XDF_AP[:, ic * 128:(ic + 1) * 128],
                                 r_=TY3V(0, BF16, 8 * D)[:, ic * D + nh * 512: ic * D + nh * 512 + 512],
                                 s0=(ic == 0): t.matmul(o_, l_, r_, start=s0, stop=False)),
                           waits=w)
                last = sch.op("t", (lambda t, o_=PB(bank, 0, 512),
                                    r_=ROWR[0:1, 1024 + nh * 512: 1024 + (nh + 1) * 512]:
                                    t.matmul(o_, ONER, r_, start=False, stop=True)), mark=True)
                f_marks.append(last)
            ev2 = None
            for nh in range(2):
                ev2 = sch.op("v", (lambda v, d_=QSLOT(0)[:, nh * 512:(nh + 1) * 512],
                                   a_=PB(4 + nh, 0, 512), b_=Y1AP(p)[:, nh * 512:(nh + 1) * 512]:
                                   v.tensor_tensor(d_, a_, b_, ALU.add)),
                             waits=[f_marks[nh]])
                psum_free[4 + nh] = ev2
            p2s["z2"][sl] = ev2
            zap = QSLOT(0)
            t2sum = sch.op("v", (lambda v, z_=zap: v.tensor_scalar(QSLOT(1), z_, 1.0, 0.0,
                                                           ALU.mult, ALU.add,
                                                           accum_out=SMALL[:, 75:76])),
                           waits=[ev2])
            t2sq = sch.op("v", (lambda v, z_=zap: v.scalar_tensor_tensor(QSLOT(3), z_, 1.0, z_,
                                                         ALU.mult, ALU.mult,
                                                         accum_out=SMALL[:, 76:77])))
            t2m1 = sch.op("v", lambda v: v.tensor_scalar(SMALL[:, 77:78], SMALL[:, 75:76],
                                                         -1.0 / 1024.0, None, ALU.mult))
            t2m2 = sch.op("v", lambda v: v.scalar_tensor_tensor(SMALL[:, 78:79], SMALL[:, 77:78],
                                                                1.0, SMALL[:, 77:78],
                                                                ALU.mult, ALU.mult))
            t2v = sch.op("v", lambda v: v.scalar_tensor_tensor(SMALL[:, 79:80], SMALL[:, 76:77],
                                                               1.0 / 1024.0, SMALL[:, 78:79],
                                                               ALU.mult, ALU.subtract))
            t2l1 = sch.op("s", lambda ss: ss.activation(SMALL[:, 79:80], SMALL[:, 79:80], AF.Ln,
                                                        bias=BIASB[:, 61:62]), waits=[t2v])
            t2l2 = sch.op("s", lambda ss: ss.activation(SMALL[:, 79:80], SMALL[:, 79:80], AF.Exp,
                                                        scale=-0.5))
            p2s["ln2e"][sl] = t2l2

        def p2_b2(sl):
            zap = QSLOT(0)
            t2ap = sch.op("v", (lambda v, z_=zap: v.tensor_scalar(XDF_AP, z_,
                                                          SMALL[:, 77:78], SMALL[:, 79:80],
                                                          ALU.add, ALU.mult)),
                          waits=[p2s["ln2e"][sl]])
            p2s["ln2"][sl] = t2ap
            p2s["spill"][sl] = sch.dma("y", "dy2", (lambda y, r0=128 * sl:
                                                    y.dma_start(out=Y2SP[r0:r0 + 128, :],
                                                                in_=XDF_AP)),
                                       waits=[t2ap])

        wd_state = {"tld": {}, "tq": {}, "done": None, "wt_last": None}

        def emit_wd(oc):
            if oc == 0:
                wd_state["tld"][0] = sch.dma(
                    "g", "da", (lambda g: g.dma_start(out=Y1AP(0), in_=FW[0:128, :])),
                    waits=[tg0])
            if oc + 1 < 8:
                w = [wd_state["tq"][oc - 1]] if oc >= 1 else [tg0]
                wd_state["tld"][oc + 1] = sch.dma(
                    "g", "da" if (oc + 1) % 2 == 0 else "dwd",
                    (lambda g, s_=FW[128 * (oc + 1):128 * (oc + 1) + 128, :],
                     d_=Y1AP((oc + 1) % 2): g.dma_start(out=d_, in_=s_)),
                    waits=w)
            w0 = [wd_state["tld"][oc], tg0]
            if oc >= 1:
                w0.append(wd_state["wt_last"])
            tq = emit_quant(Y1AP(oc % 2), XDT_AP, w0, P2_SLOTS)
            wd_state["tq"][oc] = tq
            evs = []
            for c in range(8):
                bank = 4 + (c % 2)
                w = [tq, l_idb]
                if psum_free.get(bank):
                    w.append(psum_free[bank])
                tt = sch.op("t", (lambda t, o_=PB(bank, (c // 2) * 128, 128, BF16),
                                  i_=XDT_AP[:, c * 128:(c + 1) * 128]:
                                  t.transpose(o_, i_, IDB)),
                            waits=w, mark=True)
                ev = sch.op("v", (lambda v, d_=TY3V(0, BF16, 8 * D)[:, c * D + oc * 128: c * D + oc * 128 + 128],
                                  s_=PB(bank, (c // 2) * 128, 128, BF16): v.tensor_copy(d_, s_)),
                            waits=[tt])
                psum_free[bank] = ev
                evs.append(ev)
                wd_state["wt_last"] = tt
            wd_state["done"] = evs[-1]
            if oc == 7:
                p2s["transp"][-1] = wd_state["wt_last"]

        p2drv = {"nf1": 0, "nf2a": 0, "nf2b": 0, "nb1": 0, "nb2": 0, "nwd": 0}

        def p2_pump(avail, budget):
            d = p2drv
            done = 0
            while done < budget:
                if d["nwd"] < 8:
                    emit_wd(d["nwd"]); d["nwd"] += 1; done += 1; continue
                k = d["nf1"]
                if k < NSL and k <= avail and d["nb1"] >= k - 1:
                    p2_f1(k); d["nf1"] += 1; done += 1; continue
                k = d["nb2"]
                if k < NSL and d["nb1"] >= k + 1:
                    p2_b2(k); d["nb2"] += 1; done += 1; continue
                k = d["nb1"]
                if k < NSL and d["nf2b"] >= k + 1 and d["nb2"] >= k:
                    p2_back(k); d["nb1"] += 1; done += 1; continue
                k = d["nf2b"]
                if k < NSL and d["nf2a"] >= k + 1 and d["nb2"] >= k and d["nb1"] >= k:
                    p2_f2b(k); d["nf2b"] += 1; done += 1; continue
                k = d["nf2a"]
                if k < NSL and d["nf1"] >= k + 1 and d["nf2b"] >= k:
                    p2_f2(k); d["nf2a"] += 1; done += 1; continue
                break

        st_p1c["expt_free"] = [two, two]
        st_p1c["expt_free"] = [two, two]
        st_p1c = emit_p1c(1, [v_done1], st_p1c, hook=p2_pump)
        att0_done = st_p1c["atm_free"]
        p2_pump(NSL, 10 ** 9)
        wdt_done = wd_state["done"]
        tg1 = sch.op("t", lambda t: t.nop(), mark=True)
        p2v_last = p2s["ln2"][NSL - 1]

        if stage == "p1c":
            for sl in range(NSL):
                sch.dma("g", "dout", (lambda gg, sl_=sl: gg.dma_start(
                    out=DBG[:, sl_ * 1024: (sl_ + 1) * 1024],
                    in_=ATT0[128 * sl_:128 * sl_ + 128, :])),
                    waits=[("dsp", sch.cnt["dsp"])])
            sch.op("y", lambda y: y.wait_ge(sem_handles["dout"], sch.cnt["dout"]))
            sch.emit(block, sem_handles)
            stack.close()
            return nc

        if stage == "p2":
            for sl in range(NSL):
                sch.dma("g", "dout", (lambda gg, sl_=sl: gg.dma_start(
                    out=DBG[:, sl_ * 1024: (sl_ + 1) * 1024],
                    in_=Y2SP[128 * sl_:128 * sl_ + 128, :])),
                    waits=[("dy2", 16 * NSL)])
            sch.op("y", lambda y: y.wait_ge(sem_handles["dout"], sch.cnt["dout"]))
            sch.emit(block, sem_handles)
            stack.close()
            return nc

        tp2_nop = sch.op("t", lambda t: t.nop(), mark=True)

        def EXL1(tile, a, n):
            return TE[:, tile * QT + a: tile * QT + a + n]

        def MT(var, a, n):
            base = 8192 // 2
            return TE[:, base + var * QT + a: base + var * QT + a + n]

        SIG_RANGE = {0: (0, 128), 1: (0, 256), 2: (0, 384), 3: (128, 384), 4: (256, 384)}

        l_wq1 = None
        for ic in range(8):
            l_wq1 = sch.dma("g", "da", (lambda g, s_=W1Q[128 * ic:128 * ic + 128, :],
                                        d_=TY3V(0, BF16, 8 * D)[:, ic * D: (ic + 1) * D]:
                                        g.dma_start(out=d_, in_=s_)),
                            waits=[tp2_nop] if ic == 0 else [])
        l_wk1 = None
        for ic in range(8):
            l_wk1 = sch.dma("g", "db", (lambda g, s_=W1K[128 * ic:128 * ic + 128, :],
                                        d_=TY3V(49408, BF16, 8 * 128)[:, ic * 128:(ic + 1) * 128]:
                                        g.dma_start(out=d_, in_=s_)),
                            waits=[tp2_nop] if ic == 0 else [])
        l_wv1 = None
        for ic in range(8):
            l_wv1 = sch.dma("g", "db", (lambda g, s_=W1V[128 * ic:128 * ic + 128, :],
                                        d_=TY3V(53504, BF16, 8 * 128)[:, ic * 128:(ic + 1) * 128]:
                                        g.dma_start(out=d_, in_=s_)),
                            waits=[l_wk1] if ic == 0 else [])
        l_b1q = sch.dma("y", "dm", lambda y: y.dma_start(out=BIASB[:, 32:40], in_=B1Q),
                        waits=[dm_chain[-1]])
        l_b1k = sch.dma("y", "dm", lambda y: y.dma_start(out=BIASB[:, 40:41], in_=B1K),
                        waits=[l_b1q])
        l_b1v = sch.dma("y", "dm", lambda y: y.dma_start(out=BIASB[:, 41:42], in_=B1VC),
                        waits=[l_b1k])
        l_mask = sch.dma("y", "dm", (lambda y: y.dma_start(out=MT(0, 0, 5 * QT), in_=MASKS)),
                         waits=[tp2_nop, l_b1v, wdt_done])
        def K1B(a, n, p0=0, np_=128):
            return TY3.bitcast(BF16)[p0:p0 + np_, 32768 // 2 + a: 32768 // 2 + a + n]

        def V1T(slot, a, n):
            return TY3.bitcast(BF16)[:, 37888 // 2 + slot * 65 + a: 37888 // 2 + slot * 65 + a + n]

        def Q1B(c, a, n, p0=0, np_=128):
            return TY3.bitcast(BF16)[p0:p0 + np_, 43264 // 2 + c * QT + a: 43264 // 2 + c * QT + a + n]

        def VFM(a, n):
            return TY3.bitcast(BF16)[:, 57600 // 2 + a: 57600 // 2 + a + n]

        z1 = sch.op("v", lambda v: v.memset(K1B(0, 128), 0.0), waits=[tp2_nop])
        z2 = sch.op("v", lambda v: v.memset(K1B(2432, 128), 0.0))
        V1T3 = TY3.bitcast(BF16)[:, 37888 // 2: 37888 // 2 + 40 * 65].rearrange(
            "p (s c) -> p s c", c=65)
        z3 = sch.op("v", lambda v: v.memset(V1T3[:, 0:2, :], 0.0))
        z4 = sch.op("v", lambda v: v.memset(V1T3[:, 38:40, :], 0.0))
        z5 = sch.op("v", lambda v: v.memset(V1T3[:, :, 64:65], 1.0))

        def YSTG(pb, ic, n):
            return TY3V(16384 + pb * 8192, BF16, 8 * 512)[:, ic * 512: ic * 512 + n]

        kv_done = None
        ystg_free = {0: None, 1: None}
        kv_cast = {}

        def emit_kv_cast(si, extra=()):
            n = 512 if si < 4 else 256
            pb = si % 2
            dls = "dl0" if pb == 0 else "dl1"
            cast_tok = None
            for ic in range(8):
                w = []
                if ic == 0:
                    w = [("dy2", 16 * NSL), tp2_nop, p2v_last] + list(extra)
                cast_tok = sch.dma("y", dls, (lambda y, d_=YSTG(pb, ic, n),
                                              s_=Y2SP[512 * si:512 * si + n,
                                                      128 * ic:128 * ic + 128]:
                                              y.dma_start_transpose(d_, s_)),
                                   waits=w)
            kv_cast[si] = cast_tok

        emit_kv_cast(0)
        emit_kv_cast(1)
        for si in range(5):
            n = 512 if si < 4 else 256
            pb = si % 2
            for which in (0, 1):
                bank = which
                woff = 49408 if which == 0 else 53504
                last = None
                for ic in range(8):
                    w = []
                    if ic == 0:
                        w = [l_wv1, l_b1v, kv_cast[si]]
                        if psum_free.get(bank):
                            w.append(psum_free[bank])
                    last = sch.op("t", (lambda t, o_=PB(bank, 0, n),
                                        l_=TY3V(woff, BF16, 8 * 128)[:, ic * 128:(ic + 1) * 128],
                                        r_=YSTG(pb, ic, n),
                                        s0=(ic == 0), s1=(ic == 7):
                                        t.matmul(o_, l_, r_, start=s0, stop=s1)),
                                  waits=w, mark=(ic == 7))
                if which == 0:
                    ev = sch.op("v", (lambda v, d_=K1B(128 + 512 * si, n), p_=PB(bank, 0, n):
                                      v.tensor_scalar(d_, p_, BIASB[:, 40:41], None, ALU.add)),
                                waits=[last, z2])
                else:
                    ev = sch.op("v", (lambda v, d_=VFM(512 * si, n), p_=PB(bank, 0, n):
                                      v.tensor_scalar(d_, p_, BIASB[:, 41:42], None, ALU.add)),
                                waits=[last])
                psum_free[bank] = ev
                kv_done = ev
            ystg_free[pb] = sch.op("t", lambda t: t.nop(), mark=True)
            if si + 2 < 5:
                emit_kv_cast(si + 2, extra=[ystg_free[pb]])
        vt_done = None
        for sub in range(NSL):
            bank = 4 + (sub % 2)
            tt = sch.op("t", (lambda t, o_=PB(bank, 0, 128, BF16),
                              i_=VFM(sub * 128, 128): t.transpose(o_, i_, IDB)),
                        waits=[kv_done, l_idb] + ([psum_free.get(bank)] if psum_free.get(bank) else []),
                        mark=True)
            slot0 = (sub + 1) * 2
            dst3 = TY3.bitcast(BF16)[:, 37888 // 2 + slot0 * 65: 37888 // 2 + slot0 * 65 + 2 * 65].rearrange(
                "p (s c) -> p s c", c=65)[:, :, 0:64]
            src3 = PB(bank, 0, 128, BF16).rearrange("p (s c) -> p s c", c=64)
            ev = sch.op("v", (lambda v, d_=dst3, s_=src3: v.tensor_copy(d_, s_)),
                        waits=[tt, z5])
            psum_free[bank] = ev
            vt_done = ev

        l_w1o = None
        for ic in range(8):
            l_w1o = sch.dma("g", "da", (lambda g, s_=W1O[128 * ic:128 * ic + 128, :],
                                        d_=TEV(24576, BF16, 8 * D)[:, ic * D:(ic + 1) * D]:
                                        g.dma_start(out=d_, in_=s_)),
                            waits=[l_wq1, tp2_nop, p2v_last] if ic == 0 else [])
        l_b1o = sch.dma("g", "dgm", lambda g: g.dma_start(out=ROWR[0:1, 0:1024], in_=B1O),
                        waits=[t_fbrow, tp2_nop, p2v_last])

        def AF1(p, c, a, n):
            base = p * 3072
            return TY2.bitcast(BF16)[:, base + c * QT + a: base + c * QT + a + n]

        def Y3B(p):
            return TY2.bitcast(BF16)[:, 6144 + p * 1024: 6144 + p * 1024 + 1024]

        Z3AP = TEV(12288, F32, 1024)
        JUNK3 = TEV(16384, F32, 1024)

        p4s = {"oproj": {}, "z3": {}, "spill": {}, "af1": {}, "ln3e": {},
               "n": 0, "nb": 0}

        def p4_slice(sl):
            p = sl % 2
            g = sl // 3
            pg = g % 2
            if sl % 3 == 0:
                tldA = None
                for c in range(8):
                    w = []
                    if c == 0:
                        w = [("dsp", sch.cnt["dsp"])]
                        if p4s["oproj"].get(3 * (g - 2) + 2):
                            w.append(p4s["oproj"][3 * (g - 2) + 2])
                        if g < 2:
                            w += [tp2_nop, p2v_last]
                    tldA = sch.dma("y", "dw2" if pg == 0 else "dw3",
                                   (lambda y, s_=ATT1[QT * g:QT * g + QT,
                                                      128 * c:128 * c + 128],
                                    d_=AF1(pg, c, 0, QT): y.dma_start_transpose(d_, s_)),
                                   waits=w)
                p4s["af1"][g] = tldA
            tldA = p4s["af1"][g]
            wy = [p4s["z3"][sl - 2]] if sl >= 2 else [("dy2", 16 * NSL), p2v_last]
            ldy2 = sch.dma("g", "dx0" if p == 0 else "dx1",
                           (lambda g, d_=TEV(20480 + p * 2048, BF16, 1024),
                            s_=Y2SP[128 * sl:128 * sl + 128, :]:
                            g.dma_start(out=d_, in_=s_)), waits=wy)
            o_marks = []
            for nh in range(2):
                bank = 4 + nh
                for ic in range(8):
                    w = []
                    if ic == 0:
                        w = [tldA, l_w1o, l_b1o]
                        if psum_free.get(bank):
                            w.append(psum_free[bank])
                    sch.op("t", (lambda t, o_=PB(bank, 0, 512),
                                 l_=AF1(pg, ic, (sl % 3) * 128, 128),
                                 r_=TEV(24576, BF16, 8 * D)[:, ic * D + nh * 512: ic * D + nh * 512 + 512],
                                 s0=(ic == 0): t.matmul(o_, l_, r_, start=s0, stop=False)),
                           waits=w)
                last = sch.op("t", (lambda t, o_=PB(bank, 0, 512),
                                    r_=ROWR[0:1, nh * 512:(nh + 1) * 512]:
                                    t.matmul(o_, ONER, r_, start=False, stop=True)), mark=True)
                o_marks.append(last)
            p4s["oproj"][sl] = o_marks[-1]
            ev3 = None
            for nh in range(2):
                ev3 = sch.op("v", (lambda v, d_=Z3AP[:, nh * 512:(nh + 1) * 512],
                                   a_=PB(4 + nh, 0, 512),
                                   b_=TEV(20480 + p * 2048, BF16, 1024)[:, nh * 512:(nh + 1) * 512]:
                                   v.tensor_tensor(d_, a_, b_, ALU.add)),
                             waits=[o_marks[nh], ldy2])
                psum_free[4 + nh] = ev3
            p4s["z3"][sl] = ev3
            t3s = sch.op("v", (lambda v: v.tensor_scalar(JUNK3, Z3AP, 1.0, 0.0,
                                                         ALU.mult, ALU.add,
                                                         accum_out=SMALL[:, 70:71])),
                         waits=[ev3])
            t3q = sch.op("v", (lambda v: v.scalar_tensor_tensor(JUNK3, Z3AP, 1.0, Z3AP,
                                                        ALU.mult, ALU.mult,
                                                        accum_out=SMALL[:, 71:72])))
            t3m = sch.op("v", lambda v: v.tensor_scalar(SMALL[:, 72:73], SMALL[:, 70:71],
                                                        -1.0 / 1024.0, None, ALU.mult))
            t3m2 = sch.op("v", lambda v: v.scalar_tensor_tensor(SMALL[:, 73:74], SMALL[:, 72:73],
                                                                1.0, SMALL[:, 72:73],
                                                                ALU.mult, ALU.mult))
            t3v = sch.op("v", lambda v: v.scalar_tensor_tensor(SMALL[:, 74:75], SMALL[:, 71:72],
                                                               1.0 / 1024.0, SMALL[:, 73:74],
                                                               ALU.mult, ALU.subtract))
            t3l = sch.op("s", lambda ss: ss.activation(SMALL[:, 74:75], SMALL[:, 74:75], AF.Ln,
                                                       bias=BIASB[:, 61:62]), waits=[t3v])
            t3e = sch.op("s", lambda ss: ss.activation(SMALL[:, 74:75], SMALL[:, 74:75], AF.Exp,
                                                       scale=-0.5))
            p4s["ln3e"][sl] = t3e

        def p4_apply(sl):
            p = sl % 2
            wa = [p4s["ln3e"][sl]]
            if sl >= 2:
                wa.append(p4s["spill"][sl - 2])
            t3a = sch.op("v", (lambda v, y_=Y3B(p): v.tensor_scalar(y_, Z3AP,
                                                          SMALL[:, 72:73], SMALL[:, 74:75],
                                                          ALU.add, ALU.mult)),
                         waits=wa)
            wsp = [t3a]
            if sl >= 1:
                wsp.append(p4s["spill"][sl - 1])
            p4s["spill"][sl] = sch.dma("y", "dy3", (lambda y, r0=128 * sl, y_=Y3B(p):
                                                    y.dma_start(out=Y3SP[r0:r0 + 128, :],
                                                                in_=y_)),
                                       waits=wsp)

        def p4_pump(avail, budget):
            done = 0
            while done < budget:
                k = p4s["nb"]
                if k < NSL and p4s["n"] >= k + 1:
                    p4_apply(k); p4s["nb"] += 1; done += 1; continue
                k = p4s["n"]
                if k < NSL and k <= avail and p4s["nb"] >= k:
                    p4_slice(k); p4s["n"] += 1; done += 1; continue
                break

        st1 = {"exp_prev": {0: None, 1: None, 2: None}, "pv_norm": None,
               "b6_free": None, "atm_free": None, "expt_free": None}
        q_cast = {}

        def emit_q_cast(qt, extra=()):
            pb = qt % 2
            dls = "dl0" if pb == 0 else "dl1"
            tok = None
            for ic in range(8):
                w = list(extra) if ic == 0 else []
                tok = sch.dma("y", dls, (lambda y, d_=YSTG(pb, ic, QT),
                                         s_=Y2SP[qt * QT:(qt + 1) * QT,
                                                 128 * ic:128 * ic + 128]:
                                         y.dma_start_transpose(d_, s_)),
                              waits=w)
            q_cast[qt] = tok

        emit_q_cast(0, [ystg_free[0], vt_done])
        emit_q_cast(1, [ystg_free[1], vt_done])
        qfree = {}
        for qt in range(NQT):
            qcast = q_cast[qt]
            q_marks = []
            for oc in range(8):
                bank = oc % 2
                last = None
                for ic in range(8):
                    w = []
                    if ic == 0:
                        w = [l_wq1, l_b1q, vt_done, qcast]
                        if psum_free.get(bank):
                            w.append(psum_free[bank])
                    last = sch.op("t", (lambda t, o_=PB(bank, 0, QT),
                                        l_=TY3V(0, BF16, 8 * D)[:, ic * D + oc * 128: ic * D + oc * 128 + 128],
                                        r_=YSTG(qt % 2, ic, QT),
                                        s0=(ic == 0), s1=(ic == 7):
                                        t.matmul(o_, l_, r_, start=s0, stop=s1)),
                                  waits=w, mark=(ic == 7))
                ev = sch.op("v", (lambda v, d_=Q1B(oc, 0, QT), p_=PB(bank, 0, QT),
                                  b_=BIASB[:, 32 + oc:33 + oc]:
                                  v.tensor_scalar(d_, p_, b_, None, ALU.add)),
                            waits=[last] + ([st1["pv_norm"]] if st1["pv_norm"] else []))
                psum_free[bank] = ev
                q_marks.append(ev)
            q_ready = q_marks[-1]
            qfree[qt] = sch.op("t", lambda t: t.nop(), mark=True)
            if qt + 2 < NQT:
                emit_q_cast(qt + 2, [qfree[qt]])
            for hp in range(8):
                exp_toks = {}
                for sg in range(5):
                    sb = qt * 3 + sg
                    lo, hi = SIG_RANGE[sg]
                    w_ = hi - lo
                    par = sg % 2
                    set_base = 2 * par
                    fw = [q_ready, l_mask]
                    if st1["exp_prev"][par]:
                        fw.append(st1["exp_prev"][par])
                    last = None
                    for hh in (0, 1):
                        bank = set_base + hh
                        lhsT = K1B(sb * 128, 128, p0=64 * hh, np_=64)
                        rhs = Q1B(hp, lo, w_, p0=64 * hh, np_=64)
                        ww = fw if hh == 0 else []
                        last = sch.op("t", (lambda t, o_=PB(bank, lo, w_), l_=lhsT, r_=rhs,
                                            tp=(64 * hh, 0):
                                            t.matmul(o_, l_, r_, start=True, stop=True,
                                                     tile_position=tp)),
                                      waits=ww, mark=(hh == 1))
                    ew = [last]
                    if st1["expt_free"]:
                        ew.append(st1["expt_free"])
                    in_ap = PALL[:, set_base * 512 + lo: set_base * 512 + lo + 1024].rearrange(
                        "p (b x) -> p b x", x=512)[:, :, 0:w_]
                    out_ap = TE[:, (sg * 2) * QT + lo: (sg * 2) * QT + lo + w_] if False else None
                    out3 = TE[:, 0:2 * QT].rearrange("p (b x) -> p b x", x=QT)[
                        :, 0:2, 0:1]
                    o3 = TE[:, (sg * 2) * QT + lo: (sg * 2 + 1) * QT + lo + w_] if False else None
                    out_ap = TE[:, (sg * 2) * QT: (sg * 2 + 2) * QT].rearrange(
                        "p (b x) -> p b x", x=QT)[:, :, lo:lo + w_]
                    etok = sch.op("s", (lambda ss, o_=out_ap, i_=in_ap:
                                        ss.activation(o_, i_, AF.Exp,
                                                      bias=BIASB[:, 60:61], scale=1.0)),
                                  waits=ew)
                    st1["exp_prev"][par] = etok
                    mtok = None
                    for hh in (0, 1):
                        mtok = sch.op("v", (lambda v, d_=EXL1(sg * 2 + hh, lo, w_),
                                            m_=MT(sg, lo, w_):
                                            v.tensor_tensor(d_, d_, m_, ALU.mult)),
                                      waits=[etok])
                    exp_toks[sg] = mtok
                pv_first = True
                pv_last = None
                for qs in (0, 1, 2):
                    for hh in (0, 1):
                        for sg in (qs, qs + 1, qs + 2):
                            w = [exp_toks[sg]]
                            if pv_first and st1["b6_free"]:
                                w.append(st1["b6_free"])
                            lhsT = EXL1(sg * 2 + hh, qs * 128, 128)
                            rhs = V1T((qt * 3 + sg) * 2 + hh, 0, 65)
                            out = PB(6, (hh * 3 + qs) * 65, 65)
                            pv_last = sch.op("t", (lambda t, o_=out, l_=lhsT, r_=rhs,
                                                   s0=pv_first:
                                                   t.matmul(o_, l_, r_, start=s0, stop=False,
                                                            skip_group_check=True)),
                                             waits=w,
                                             mark=(qs == 2 and hh == 1 and sg == qs + 2))
                            pv_first = False
                st1["expt_free"] = pv_last
                dens3 = PB(6, 0, 390).rearrange("p (s c) -> p s c", c=65)[:, :, 64:65]
                t2 = sch.op("v", (lambda v: v.reciprocal(SMALL[:, 86:92], dens3)),
                            waits=[pv_last])
                ntok = None
                for hh in (0, 1):
                    for qs in (0, 1, 2):
                        idx = hh * 3 + qs
                        hcol = (hp + 8 * hh) * 64
                        dst = ATMv(qs * 1024 + hcol, 64)
                        w = [t2]
                        if st1["atm_free"]:
                            w.append(st1["atm_free"])
                        ntok = sch.op("v", (lambda v, d_=dst,
                                            s_=PB(6, idx * 65, 64),
                                            r_=SMALL[:, 86 + idx: 87 + idx]:
                                            v.tensor_scalar(d_, s_, r_, None, ALU.mult)),
                                      waits=w)
                st1["pv_norm"] = ntok
                st1["b6_free"] = ntok
                p4_pump(3 * qt - 1, 1)
            sp = None
            for qs in range(3):
                sp = sch.dma("y", "dsp", (lambda y, r0=qt * QT + qs * 128,
                                          a_=ATMv(qs * 1024, 1024):
                                          y.dma_start(out=ATT1[r0:r0 + 128, :], in_=a_)),
                             waits=[st1["pv_norm"]])
            st1["atm_free"] = sp
            p4_pump(3 * qt + 2, 2)
        att1_done = st1["atm_free"]
        p4_pump(NSL, 10 ** 9)

        if stage == "p3":
            for sl in range(NSL):
                sch.dma("g", "dout", (lambda gg, sl_=sl: gg.dma_start(
                    out=DBG[:, sl_ * 1024: (sl_ + 1) * 1024],
                    in_=ATT1[128 * sl_:128 * sl_ + 128, :])),
                    waits=[("dsp", sch.cnt["dsp"])])
            sch.op("y", lambda y: y.wait_ge(sem_handles["dout"], sch.cnt["dout"]))
            sch.emit(block, sem_handles)
            stack.close()
            return nc

        tp3_nop = sch.op("t", lambda t: t.nop(), mark=True)

        zc = sch.op("v", lambda v: v.memset(SMALL[:, 166:167], 0.0))
        RLG = [(0, 512), (512, 512), (1024, 512), (1536, 512), (2048, 256)]
        rl_grp = {}
        for gi, (a0, ng) in enumerate(RLG):
            dl = "dl0" if gi % 2 == 0 else "dl1"
            tok = None
            for c in range(8):
                w = []
                if c == 0:
                    w = [("dy3", 16 * ((a0 + ng) // 128)), tp3_nop, st1["pv_norm"]]
                    if gi >= 2:
                        w.append(rl_grp[gi - 2])
                tok = sch.dma("y", dl, (lambda y, d_=Y3C(c, a0, ng),
                                        s_=Y3SP[a0:a0 + ng, 128 * c:128 * c + 128]:
                                        y.dma_start_transpose(d_, s_)), waits=w)
            rl_grp[gi] = tok

        if stage == "p4":
            for sl in range(NSL):
                sch.dma("g", "dout", (lambda gg, sl_=sl: gg.dma_start(
                    out=DBG[:, sl_ * 1024: (sl_ + 1) * 1024],
                    in_=Y3SP[128 * sl_:128 * sl_ + 128, :])),
                    waits=[("dy3", 16 * NSL)])
            sch.op("y", lambda y: y.wait_ge(sem_handles["dout"], sch.cnt["dout"]))
            sch.emit(block, sem_handles)
            stack.close()
            return nc

        tp4_nop = sch.op("t", lambda t: t.nop(), mark=True)

        l_mur = sch.dma("y", "dm", lambda y: y.dma_start(out=BIASB[:, 42:50], in_=MUR),
                        waits=[l_mask])
        l_muk = sch.dma("y", "dm", lambda y: y.dma_start(out=BIASB[:, 50:58], in_=MUK),
                        waits=[l_mur])
        l_wr = None
        for ic in range(8):
            l_wr = sch.dma("g", "da", (lambda g, s_=WRT[128 * ic:128 * ic + 128, :],
                                       d_=TY2V(0, BF16, 8 * D)[:, ic * D:(ic + 1) * D]:
                                       g.dma_start(out=d_, in_=s_)),
                           waits=[tp4_nop, ("dy3", 16 * NSL), l_w1o] if ic == 0 else [])
        TILES = [(0, 512), (512, 512), (1024, 512), (1536, 512), (2048, 256)]

        def rl_need(ti):
            return [rl_grp[ti]] + ([rl_grp[ti - 1]] if ti >= 1 else [])
        stash_tok = None
        for ti in range(1, 5):
            a = TILES[ti][0]
            for c in range(8):
                stash_tok = sch.op("v", (lambda v, d_=SMALL[:, 96 + (ti - 1) * 8 + c: 97 + (ti - 1) * 8 + c],
                                         s_=Y3C(c, a - 1, 1): v.tensor_copy(d_, s_)),
                                   waits=[rl_grp[ti - 1]] if c == 0 else [])
        r_done = None
        xr_free = None
        for ti, (a, n) in enumerate(TILES):
            xr_toks = []
            for ic in range(8):
                if ti == 0:
                    td = sch.op("v", (lambda v, d_=TEV(36864, F32, 512)[:, 1:n],
                                      p_=Y3C(ic, 1, n - 1), q_=Y3C(ic, 0, n - 1):
                                      v.tensor_tensor(d_, p_, q_, ALU.subtract)),
                                waits=rl_need(ti) + [zc, l_muk] + ([xr_free] if xr_free else []))
                    td2 = sch.op("v", (lambda v, d_=TEV(36864, F32, 512)[:, 0:1],
                                       p_=Y3C(ic, 0, 1), q_=SMALL[:, 166:167]:
                                       v.tensor_tensor(d_, p_, q_, ALU.subtract)))
                    tx = sch.op("v", (lambda v, d_=TY2V(32768, BF16, 8 * 512)[:, ic * 512 + 1: ic * 512 + n],
                                      s_=TEV(36864, F32, 512)[:, 1:n], m_=BIASB[:, 42 + ic:43 + ic],
                                      q_=Y3C(ic, 0, n - 1):
                                      v.scalar_tensor_tensor(d_, s_, m_, q_, ALU.mult, ALU.add)))
                    tx = sch.op("v", (lambda v, d_=TY2V(32768, BF16, 8 * 512)[:, ic * 512: ic * 512 + 1],
                                      s_=TEV(36864, F32, 512)[:, 0:1], m_=BIASB[:, 42 + ic:43 + ic],
                                      q_=SMALL[:, 166:167]:
                                      v.scalar_tensor_tensor(d_, s_, m_, q_, ALU.mult, ALU.add)))
                else:
                    td = sch.op("v", (lambda v, d_=TEV(36864, F32, 512)[:, 0:n],
                                      p_=Y3C(ic, a, n), q_=Y3C(ic, a - 1, n):
                                      v.tensor_tensor(d_, p_, q_, ALU.subtract)),
                                waits=rl_need(ti) + [zc, l_muk] + ([xr_free] if xr_free else []))
                    tx = sch.op("v", (lambda v, d_=TY2V(32768, BF16, 8 * 512)[:, ic * 512: ic * 512 + n],
                                      s_=TEV(36864, F32, 512)[:, 0:n], m_=BIASB[:, 42 + ic:43 + ic],
                                      q_=Y3C(ic, a - 1, n):
                                      v.scalar_tensor_tensor(d_, s_, m_, q_, ALU.mult, ALU.add)))
                xr_toks.append(tx)
            for oc in range(8):
                bank = oc % 2
                last = None
                for ic in range(8):
                    w = []
                    if ic == 0:
                        w = [xr_toks[-1], l_wr]
                        if psum_free.get(bank):
                            w.append(psum_free[bank])
                    last = sch.op("t", (lambda t, o_=PB(bank, 0, n),
                                        l_=TY2V(0, BF16, 8 * D)[:, ic * D + oc * 128: ic * D + oc * 128 + 128],
                                        r_=TY2V(32768, BF16, 8 * 512)[:, ic * 512: ic * 512 + n],
                                        s0=(ic == 0), s1=(ic == 7):
                                        t.matmul(o_, l_, r_, start=s0, stop=s1)),
                                  waits=w, mark=(ic == 7))
                ev = sch.op("s", (lambda ss, d_=TE[:, oc * SPAN + a: oc * SPAN + a + n],
                                  p_=PB(bank, 0, n):
                                  ss.activation(d_, p_, AF.Sigmoid)),
                            waits=[last])
                psum_free[bank] = ev
                r_done = ev
            xr_free = sch.op("t", lambda t: t.nop(), mark=True)
        l_wk = None
        for ic in range(8):
            l_wk = sch.dma("g", "db", (lambda g, s_=WKT[128 * ic:128 * ic + 128, :],
                                       d_=TY2V(0, BF16, 8 * D)[:, ic * D:(ic + 1) * D]:
                                       g.dma_start(out=d_, in_=s_)),
                           waits=[r_done] if ic == 0 else [])
        l_wv = None
        for ic in range(8):
            l_wv = sch.dma("g", "db", (lambda g, s_=WKT[0:1, 0:1] if False else WVT[128 * ic:128 * ic + 128, :],
                                       d_=TY2V(49152, BF16, 8 * D)[:, ic * D:(ic + 1) * D]:
                                       g.dma_start(out=d_, in_=s_)),
                           waits=[l_wk] if ic == 0 else [])
        out_tok = None
        xk_free = None
        y4_free = [None]
        p5s = {"kact": {}, "z4": {}}

        def p5_s1(ti):
            a, n = TILES[ti]
            xk_toks = []
            for ic in range(8):
                dslot = TEV(36864, F32, 512)
                std = (SMALL[:, 166:167] if ti == 0 else
                       SMALL[:, 96 + (ti - 1) * 8 + ic: 97 + (ti - 1) * 8 + ic])
                w = [l_wk, l_muk, stash_tok]
                if xk_free:
                    w.append(xk_free)
                if y4_free[0]:
                    w.append(y4_free[0])
                td = sch.op("v", (lambda v, d_=dslot[:, 1:n],
                                  p_=Y3C(ic, 1 + a, n - 1), q_=Y3C(ic, a, n - 1):
                                  v.tensor_tensor(d_, p_, q_, ALU.subtract)),
                            waits=w)
                td2 = sch.op("v", (lambda v, d_=dslot[:, 0:1],
                                   p_=Y3C(ic, a, 1), q_=std:
                                   v.tensor_tensor(d_, p_, q_, ALU.subtract)))
                tx = sch.op("v", (lambda v, d_=TY2V(32768, BF16, 8 * 512)[:, ic * 512 + 1: ic * 512 + n],
                                  s_=dslot[:, 1:n], m_=BIASB[:, 50 + ic:51 + ic],
                                  q_=Y3C(ic, a, n - 1):
                                  v.scalar_tensor_tensor(d_, s_, m_, q_, ALU.mult, ALU.add)))
                tx = sch.op("v", (lambda v, d_=TY2V(32768, BF16, 8 * 512)[:, ic * 512: ic * 512 + 1],
                                  s_=dslot[:, 0:1], m_=BIASB[:, 50 + ic:51 + ic],
                                  q_=std: v.scalar_tensor_tensor(d_, s_, m_, q_, ALU.mult, ALU.add)))
                xk_toks.append(tx)
            kact_done = None
            for oc in range(8):
                bank = oc % 2
                last = None
                for ic in range(8):
                    w = []
                    if ic == 0:
                        w = [xk_toks[-1]]
                        if psum_free.get(bank):
                            w.append(psum_free[bank])
                    last = sch.op("t", (lambda t, o_=PB(bank, 0, n),
                                        l_=TY2V(0, BF16, 8 * D)[:, ic * D + oc * 128: ic * D + oc * 128 + 128],
                                        r_=TY2V(32768, BF16, 8 * 512)[:, ic * 512: ic * 512 + n],
                                        s0=(ic == 0), s1=(ic == 7):
                                        t.matmul(o_, l_, r_, start=s0, stop=s1)),
                                  waits=w, mark=(ic == 7))
                trl = sch.op("v", (lambda v, d_=TEV(38912, F32, 512)[:, 0:n], p_=PB(bank, 0, n):
                                   v.tensor_scalar(d_, p_, 0.0, None, ALU.max)),
                             waits=[last])
                tsq2 = sch.op("v", (lambda v, d_=TY2V(65536, BF16, 8 * 512)[:, oc * 512: oc * 512 + n],
                                    s_=TEV(38912, F32, 512)[:, 0:n]:
                                    v.tensor_tensor(d_, s_, s_, ALU.mult)))
                psum_free[bank] = tsq2
                kact_done = tsq2
            p5s["kact"][ti] = kact_done
            return sch.op("t", lambda t: t.nop(), mark=True)

        def p5_s2(ti):
            a, n = TILES[ti]
            z4_done = None
            for oc in range(8):
                bank = 2 + (oc % 2)
                last = None
                for ic in range(8):
                    w = []
                    if ic == 0:
                        w = [p5s["kact"][ti], l_wv]
                        if psum_free.get(bank):
                            w.append(psum_free[bank])
                    last = sch.op("t", (lambda t, o_=PB(bank, 0, n),
                                        l_=TY2V(49152, BF16, 8 * D)[:, ic * D + oc * 128: ic * D + oc * 128 + 128],
                                        r_=TY2V(65536, BF16, 8 * 512)[:, ic * 512: ic * 512 + n],
                                        s0=(ic == 0), s1=(ic == 7):
                                        t.matmul(o_, l_, r_, start=s0, stop=s1)),
                                  waits=w, mark=(ic == 7))
                to1 = sch.op("v", (lambda v, d_=TEV(38912, F32, 512)[:, 0:n], p_=PB(bank, 0, n),
                                   r_=TE[:, oc * SPAN + a: oc * SPAN + a + n]:
                                   v.tensor_tensor(d_, p_, r_, ALU.mult)),
                             waits=[last] + ([y4_free[0]] if (oc == 0 and y4_free[0]) else []))
                tz4 = sch.op("v", (lambda v, d_=Y3C(oc, a, n),
                                   s_=TEV(38912, F32, 512)[:, 0:n]:
                                   v.tensor_tensor(d_, d_, s_, ALU.add)))
                psum_free[bank] = tz4
                z4_done = tz4
            p5s["z4"][ti] = z4_done

        def p5_s3(ti):
            a, n = TILES[ti]
            z4_done = p5s["z4"][ti]
            for ls in range(n // 128):
                tok0 = a + ls * 128
                tts = []
                for c in range(8):
                    bank = 4
                    tt = sch.op("t", (lambda t, o_=PB(bank, c * 128, 128, BF16),
                                      i_=Y3C(c, tok0, 128):
                                      t.transpose(o_, i_, IDB)),
                                waits=[z4_done] + ([psum_free[bank]] if psum_free.get(bank) else []),
                                mark=True)
                    tts.append(tt)
                    psum_free[bank] = None
                z4p = PB(4, 0, 1024, BF16)
                junk = TY3V(40960, F32, 1024)
                t4s = sch.op("v", (lambda v, z_=z4p, j_=junk: v.tensor_scalar(
                    j_, z_, 1.0, 0.0, ALU.mult, ALU.add, accum_out=SMALL[:, 70:71])),
                             waits=[tts[-1]])
                t4q = sch.op("s", (lambda ss, z_=z4p, j_=junk: ss.activation(
                    j_, z_, AF.Square, accum_out=SMALL[:, 71:72])),
                             waits=[t4s])
                t4m = sch.op("v", lambda v: v.tensor_scalar(SMALL[:, 72:73], SMALL[:, 70:71],
                                                            -1.0 / 1024.0, None, ALU.mult),
                             waits=[t4q])
                t4m2 = sch.op("v", lambda v: v.scalar_tensor_tensor(SMALL[:, 73:74], SMALL[:, 72:73],
                                                                    1.0, SMALL[:, 72:73],
                                                                    ALU.mult, ALU.mult))
                t4v = sch.op("v", lambda v: v.scalar_tensor_tensor(SMALL[:, 74:75], SMALL[:, 71:72],
                                                                   1.0 / 1024.0, SMALL[:, 73:74],
                                                                   ALU.mult, ALU.subtract))
                t4l = sch.op("s", lambda ss: ss.activation(SMALL[:, 74:75], SMALL[:, 74:75], AF.Ln,
                                                           bias=BIASB[:, 61:62]), waits=[t4v])
                t4e = sch.op("s", lambda ss: ss.activation(SMALL[:, 74:75], SMALL[:, 74:75], AF.Exp,
                                                           scale=-0.5))
                y4ap = TEV(36864, F32, 1024)
                t4a = sch.op("v", (lambda v, z_=z4p, y_=y4ap: v.tensor_scalar(
                    y_, z_, SMALL[:, 72:73], SMALL[:, 74:75], ALU.add, ALU.mult)),
                             waits=[t4e, y4_free[0]] if y4_free[0] else [t4e])
                for bank in (4, 5):
                    psum_free[bank] = t4a
                ot = sch.dma("y", "dout", (lambda y, r0=tok0, z_=y4ap:
                                           y.dma_start(out=YOUT[r0:r0 + 128, :], in_=z_)),
                             waits=[t4a])
                y4_free[0] = ot

        xk_free = p5_s1(0)
        p5_s2(0)
        xk_free = p5_s1(1)
        for ti in range(len(TILES)):
            p5_s3(ti)
            if ti + 1 < len(TILES):
                p5_s2(ti + 1)
            if ti + 2 < len(TILES):
                xk_free = p5_s1(ti + 2)

        sch.op("y", lambda y: y.wait_ge(sem_handles["dout"], sch.cnt["dout"]))

        sch.emit(block, sem_handles)
        stack.close()
    return nc


def make_inputs(core, inp):
    b = core // 2
    half = core % 2
    s0 = 0 if half == 0 else SEQ - SPAN
    x = np.asarray(inp["L0_attn_x"][b], np.float32)
    xT = np.ascontiguousarray(x.T)
    perm = np.array([[h, h + 8] for h in range(8)], np.int64).reshape(-1)
    colperm = (perm[:, None] * 64 + np.arange(64)[None, :]).reshape(-1)
    wq = np.asarray(inp["L1_attn_Wq"], np.float32)
    bq = np.asarray(inp["L1_attn_bq"], np.float32)
    i = np.arange(128)[:, None]
    j = np.arange(QT)[None, :]
    masks = np.zeros((128, 5 * QT), np.float32)
    masks[:, 0 * QT:0 * QT + 128] = (i >= j[:, 0:128]).astype(np.float32)
    masks[:, 1 * QT:1 * QT + 256] = ((j[:, 0:256] - i) <= 128).astype(np.float32)
    masks[:, 2 * QT:3 * QT] = (((j - i) >= 1) & ((j - i) <= 256)).astype(np.float32)
    masks[:, 3 * QT + 128:4 * QT] = ((j[:, 128:QT] - i) >= 129).astype(np.float32)
    masks[:, 4 * QT + 256:5 * QT] = ((j[:, 256:QT] - i) >= 257).astype(np.float32)
    d = {
        "XSF": np.ascontiguousarray(xT[:, s0:s0 + SPAN]),
        "XFF": xT,
        "XST": np.ascontiguousarray(x[s0:s0 + SPAN]),
        "QW": np.asarray(inp["L0_attn_qW"], np.float32),
        "QBI": np.asarray(inp["L0_attn_qb"], np.float32),
        "KW": np.asarray(inp["L0_attn_kW"], np.float32),
        "KBI": np.asarray(inp["L0_attn_kb"], np.float32),
        "VW": np.asarray(inp["L0_attn_vW"], np.float32),
        "VBI": np.asarray(inp["L0_attn_vb"], np.float32),
        "OW": np.asarray(inp["L0_attn_oW"], np.float32),
        "OBI": np.asarray(inp["L0_attn_ob"], np.float32),
        "JV": np.asarray(inp["L0_attn_J"], np.float32).reshape(8, 128),
        "FW": np.asarray(inp["L0_ffn_weight"], np.float32),
        "FB": np.asarray(inp["L0_ffn_bias"], np.float32).reshape(1, D),
        "W1Q": np.ascontiguousarray(wq[:, colperm]),
        "B1Q": np.ascontiguousarray(bq[colperm]).reshape(8, 128).T.copy(),
        "W1K": np.asarray(inp["L1_attn_Wk"], np.float32),
        "B1K": np.asarray(inp["L1_attn_bk"], np.float32).reshape(128, 1),
        "W1V": np.asarray(inp["L1_attn_Wv"], np.float32),
        "B1VC": np.asarray(inp["L1_attn_bv"], np.float32).reshape(128, 1),
        "W1O": np.asarray(inp["L1_attn_Wo"], np.float32),
        "B1O": np.asarray(inp["L1_attn_bo"], np.float32).reshape(1, D),
        "WRT": np.ascontiguousarray(np.asarray(inp["L1_ffn_W_r"], np.float32).T),
        "WKT": np.ascontiguousarray(np.asarray(inp["L1_ffn_W_k"], np.float32).T),
        "WVT": np.ascontiguousarray(np.asarray(inp["L1_ffn_W_v"], np.float32).T),
        "MUR": np.asarray(inp["L1_ffn_mu_r"], np.float32).reshape(8, 128).T.copy(),
        "MUK": np.asarray(inp["L1_ffn_mu_k"], np.float32).reshape(8, 128).T.copy(),
        "MASKS": masks.astype(ml_dtypes.bfloat16),
        "IDENT": np.eye(128, dtype=np.float32),
    }
    return d


def kernel(**inputs):
    nc = build_program()
    in_maps = [make_inputs(c, inputs) for c in range(8)]
    res = run_bass_kernel_spmd(nc, in_maps, core_ids=list(range(8)))
    out = np.empty((4, SEQ, D), np.float32)
    for c in range(8):
        b, half = c // 2, c % 2
        v0 = 0 if half == 0 else SPAN - 2048
        out[b, half * 2048:(half + 1) * 2048] = res.results[c]["YOUT"][v0:v0 + 2048]
    return out


# version 15
# speedup vs baseline: 1.2269x; 1.0376x over previous
import numpy as np
import ml_dtypes
import concourse.bass as bass
import concourse.mybir as mybir
from concourse.bass_utils import run_bass_kernel_spmd

F32 = mybir.dt.float32
BF16 = mybir.dt.bfloat16
F32R = mybir.dt.float32r
AF = mybir.ActivationFunctionType
ALU = mybir.AluOpType
AX = mybir.AxisListType

MAGIC = float(np.float32(1.5 * 2 ** 23))
D = 1024
SEQ = 4096
SPAN = 2304
NSL = SPAN // 128
NSUB = SEQ // 128
QT = 384
NQT = SPAN // QT

QUAT_TAB = {
    (0, 0): (0, 1), (0, 1): (1, -1), (0, 2): (2, -1), (0, 3): (3, -1),
    (1, 0): (1, 1), (1, 1): (0, 1), (1, 2): (3, -1), (1, 3): (2, 1),
    (2, 0): (2, 1), (2, 1): (3, 1), (2, 2): (0, 1), (2, 3): (1, -1),
    (3, 0): (3, 1), (3, 1): (2, -1), (3, 2): (1, 1), (3, 3): (0, 1),
}
BIAS_SGN = np.array([[1, -1, -1, -1], [1, 1, 1, -1], [1, -1, 1, 1], [1, 1, -1, 1]],
                    np.float32)

ENGS = ("t", "v", "s", "g", "y")
DMA_SEMS = ("da", "db", "dw0", "dw1", "dw2", "dw3", "dm", "dgm", "dsp", "dy2", "dy3", "dwd", "dl0", "dl1", "dx0", "dx1", "dxa", "dxb", "dout")


class Sched:
    def __init__(self):
        self.ops = {e: [] for e in ENGS}
        self.cnt = {n: 0 for n in ENGS + DMA_SEMS}

    def op(self, eng, fn, waits=(), mark=False):
        waits = list(waits)
        if eng in ("v", "s"):
            mark = True
        if mark:
            waits.append((eng, self.cnt[eng]))
        best = {}
        for (sn, sv) in waits:
            if sv > 0 and sv > best.get(sn, 0):
                best[sn] = sv
        self.ops[eng].append((tuple(best.items()), fn, eng if mark else None, 1))
        if mark:
            self.cnt[eng] += 1
        return (eng, self.cnt[eng])

    def dma(self, eng, sem, fn, waits=()):
        best = {}
        for (sn, sv) in waits:
            if sv > 0 and sv > best.get(sn, 0):
                best[sn] = sv
        self.ops[eng].append((tuple(best.items()), fn, sem, 16))
        self.cnt[sem] += 16
        return (sem, self.cnt[sem])

    def emit(self, block, sems):
        def run(engobj, eng):
            for waits, fn, incsem, incval in self.ops[eng]:
                for (sn, sv) in waits:
                    engobj.wait_ge(sems[sn], sv)
                inst = fn(engobj)
                if incsem is not None:
                    inst.then_inc(sems[incsem], incval)

        @block.tensor
        def _(t):
            run(t, "t")

        @block.vector
        def _(v):
            run(v, "v")

        @block.scalar
        def _(s):
            run(s, "s")

        @block.gpsimd
        def _(g):
            run(g, "g")

        @block.sync
        def _(y):
            run(y, "y")


def build_program(stage="full"):
    nc = bass.Bass("TRN2", target_bir_lowering=False)

    def din(name, shape, dt=F32):
        return nc.dram_tensor(name, shape, dt, kind="ExternalInput").ap()

    XSF = din("XSF", [D, SPAN])
    XFF = din("XFF", [D, SEQ])
    XST = din("XST", [SPAN, D])
    QW = din("QW", [4, 256, 256]); QBI = din("QBI", [4, 256])
    KW = din("KW", [4, 256, 256]); KBI = din("KBI", [4, 256])
    VW = din("VW", [4, 256, 256]); VBI = din("VBI", [4, 256])
    OW = din("OW", [4, 256, 256]); OBI = din("OBI", [4, 256])
    JV = din("JV", [8, 128])
    FW = din("FW", [D, D]); FB = din("FB", [1, D])
    W1Q = din("W1Q", [D, D]); B1Q = din("B1Q", [128, 8])
    W1K = din("W1K", [D, 128]); B1K = din("B1K", [128, 1])
    W1V = din("W1V", [D, 128]); B1VC = din("B1VC", [128, 1])
    W1O = din("W1O", [D, D]); B1O = din("B1O", [1, D])
    WRT = din("WRT", [D, D]); WKT = din("WKT", [D, D]); WVT = din("WVT", [D, D])
    MUR = din("MUR", [128, 8]); MUK = din("MUK", [128, 8])
    MASKS = din("MASKS", [128, 5 * QT], BF16)
    BSGN = din("BSGN", [4, 8])
    IDENT = din("IDENT", [128, 128])

    ATT0 = nc.dram_tensor("ATT0", [SPAN, D], BF16, kind="Internal").ap()
    Y2SP = nc.dram_tensor("Y2SP", [SPAN, D], BF16, kind="Internal").ap()
    Y3SP = nc.dram_tensor("Y3SP", [SPAN, D], BF16, kind="Internal").ap()
    ATT1 = nc.dram_tensor("ATT1", [SPAN, D], BF16, kind="Internal").ap()
    YOUT = nc.dram_tensor("YOUT", [SPAN, D], F32, kind="ExternalOutput").ap()
    DBG = nc.dram_tensor("DBG", [128, 8 * SPAN], F32, kind="ExternalOutput").ap()

    sch = Sched()

    import contextlib
    stack = contextlib.ExitStack()
    with (
        nc.sbuf_tensor("TY2", [128, 18440], F32) as TY2t,
        nc.sbuf_tensor("TY3", [128, 18440], F32) as TY3t,
        nc.sbuf_tensor("TE", [128, 20480], BF16) as TEt,
        nc.sbuf_tensor("ROWB", [1, 1280], F32) as ROWB_t,
        nc.sbuf_tensor("ROWR", [1, 2048], F32R) as ROWR_t,
        nc.sbuf_tensor("BIASB", [128, 64], F32) as BIASB_t,
        nc.sbuf_tensor("ONER", [1, 128], F32R) as ONER_t,
        nc.sbuf_tensor("ONEF", [1, 128], F32) as ONEF_t,
        nc.sbuf_tensor("SMALL", [128, 192], F32) as SMALL_t,
        nc.sbuf_tensor("PVAB", [128, 408], F32) as PVA_t,
        nc.sbuf_tensor("IDF", [128, 128], F32) as IDF_t,
        nc.sbuf_tensor("IDB", [128, 128], BF16) as IDB_t,
        nc.psum_tensor("PALL", [128, 4096], F32) as PALL_t,
        nc.Block() as block,
    ):
        sem_handles = {}
        for n in ENGS + DMA_SEMS:
            sem_handles[n] = stack.enter_context(nc.semaphore("SEM_" + n))

        TY2 = TY2t.ap(); TY3 = TY3t.ap(); TE = TEt.ap()
        ROWB = ROWB_t.ap(); ROWR = ROWR_t.ap(); BIASB = BIASB_t.ap()
        ONER = ONER_t.ap(); ONEF = ONEF_t.ap(); SMALL = SMALL_t.ap()
        PVA = PVA_t.ap(); PALL = PALL_t.ap(); IDF = IDF_t.ap(); IDB = IDB_t.ap()

        def QBUF(c, a, n, p0=0, np_=128):
            return TY2.bitcast(BF16)[p0:p0 + np_, c * SPAN + a: c * SPAN + a + n]

        def VTB(slot, a, n, p0=0, np_=128):
            base = 36864 // 2
            return TY2.bitcast(BF16)[p0:p0 + np_, base + slot * 65 + a: base + slot * 65 + a + n]

        def XSG(p, c, a, n):
            base = (36864 + p * 8192) // 2
            return TY2.bitcast(BF16)[:, base + c * 512 + a: base + c * 512 + a + n]

        def P1FS1(c, n):
            return TY2.bitcast(F32)[:, 53248 // 4 + c * 512: 53248 // 4 + c * 512 + n]

        def P1FS2(c, n):
            if c >= 2:
                return PALL[:, 512 * 6: 512 * 6 + n]
            return TY2.bitcast(F32)[:, 69632 // 4 + c * 512: 69632 // 4 + c * 512 + n]

        def TY2V(byte_off, dt, cols, p0=0, np_=128):
            esz = 2 if dt == BF16 else 4
            return TY2.bitcast(dt)[p0:p0 + np_, byte_off // esz: byte_off // esz + cols]

        def KBUF(c, a, n, p0=0, np_=128):
            return TY3.bitcast(BF16)[p0:p0 + np_, c * SEQ + a: c * SEQ + a + n]

        def TY3V(byte_off, dt, cols, p0=0, np_=128):
            esz = 2 if dt == BF16 else 4
            return TY3.bitcast(dt)[p0:p0 + np_, byte_off // esz: byte_off // esz + cols]

        def Y3C(c, a, n, dt=BF16, p0=0, np_=128):
            return TY3.bitcast(dt)[p0:p0 + np_, c * SPAN + a: c * SPAN + a + n]

        def TEV(byte_off, dt, cols, p0=0, np_=128):
            esz = 2 if dt == BF16 else 4
            return TE.bitcast(dt)[p0:p0 + np_, byte_off // esz: byte_off // esz + cols]

        def EXPT(tile, a, n, p0=0, np_=128):
            return TE[p0:p0 + np_, tile * QT + a: tile * QT + a + n]

        def PB(b, a, n, dt=None):
            if dt is not None:
                esz = 2 if dt == BF16 else 4
                scale = 4 // esz
                return PALL.bitcast(dt)[:, scale * 512 * b + a: scale * 512 * b + a + n]
            return PALL[:, 512 * b + a: 512 * b + a + n]

        t_onef = sch.op("v", lambda v: v.memset(ONEF, 1.0))
        t_oner = sch.op("v", lambda v: v.tensor_copy(ONER, ONEF), waits=[t_onef])
        t_eps6 = sch.op("v", lambda v: v.memset(BIASB[:, 62:63], 1e-6))
        t_eps5 = sch.op("v", lambda v: v.memset(BIASB[:, 61:62], 1e-5))
        t_m30 = sch.op("v", lambda v: v.memset(BIASB[:, 60:61], -30.0))
        t_ones8 = sch.op("v", lambda v: v.memset(SMALL[0:8, 131:132], 1.0))

        tJ = sch.dma("y", "dm", lambda y: y.dma_start(out=SMALL[0:8, 0:128], in_=JV))
        t_sin = sch.op("s", lambda s: s.activation(SMALL[32:40, 0:128], SMALL[0:8, 0:128],
                                                   AF.Sin, accum_out=SMALL[0:8, 130:131]),
                       waits=[tJ])
        t_csum = sch.op("t", lambda t: t.matmul(
            PALL[0:1, 0:1], SMALL[0:8, 131:132], SMALL[0:8, 130:131],
            start=True, stop=True), waits=[t_sin, t_ones8], mark=True)
        t_cs = sch.op("v", lambda v: v.tensor_scalar(SMALL[0:1, 132:133], PALL[0:1, 0:1],
                                                     0.1 / 1024.0, None, ALU.mult),
                      waits=[t_csum])
        t_crep = sch.op("t", lambda t: t.matmul(
            PB(1, 0, 1), ONEF, SMALL[0:1, 132:133],
            start=True, stop=True), waits=[t_cs, t_onef], mark=True)
        t_cv = sch.op("v", lambda v: v.tensor_copy(BIASB[:, 63:64], PB(1, 0, 1)),
                      waits=[t_crep])

        dm_chain = [tJ]

        l_bsgn = sch.dma("y", "dm", lambda y: y.dma_start(out=SMALL[0:4, 168:176], in_=BSGN),
                         waits=[dm_chain[-1]])
        dm_chain.append(l_bsgn)

        def q_bias_cols(col0, B_ap):
            stg = TY2.bitcast(F32)[0:4, 0:256]
            tdma = sch.dma("y", "dm", (lambda y: y.dma_start(out=stg, in_=B_ap)),
                           waits=[dm_chain[-1]])
            mms = []
            for oc in range(8):
                ob, half = oc // 2, oc % 2
                mms.append(sch.op("t", (lambda t, o_=PB(2, oc, 1),
                                        l_=stg[:, 128 * half:128 * half + 128],
                                        r_=SMALL[0:4, 168 + ob:169 + ob]:
                                        t.matmul(o_, l_, r_, start=True, stop=True)),
                           waits=([tdma, l_bsgn] if oc == 0 else []), mark=(oc == 7)))
            tok = sch.op("v", (lambda v, d_=BIASB[:, col0:col0 + 8], s_=PB(2, 0, 8):
                               v.tensor_copy(d_, s_)), waits=[mms[-1]])
            dm_chain.append(tok)
            return tok

        def quat_bias_row(dst_off, B_ap, as_f32r_off=None, final_extra=()):
            tok = dm_chain[-1]
            for ob in range(4):
                for w in range(4):
                    sgn = float(BIAS_SGN[ob, w])
                    stg = ROWB[0:1, 1024:1280]
                    tdma = sch.dma("y", "dm", (lambda y, s_=B_ap[w:w + 1, :], d_=stg:
                                               y.dma_start(out=d_, in_=s_)), waits=[tok])
                    dst = ROWB[0:1, dst_off + ob * 256: dst_off + (ob + 1) * 256]
                    if w == 0:
                        tok = sch.op("v", (lambda v, d_=dst, s_=stg, g_=sgn:
                                           v.tensor_scalar(d_, s_, g_, None, ALU.mult)),
                                     waits=[tdma])
                    else:
                        tok = sch.op("v", (lambda v, d_=dst, s_=stg, g_=sgn:
                                           v.scalar_tensor_tensor(d_, s_, g_, d_, ALU.mult, ALU.add)),
                                     waits=[tdma])
            if as_f32r_off is not None:
                tok = sch.op("v", (lambda v, o_=as_f32r_off: v.tensor_copy(
                    ROWR[0:1, o_:o_ + 1024], ROWB[0:1, 0:1024])),
                             waits=[tok] + list(final_extra))
            dm_chain.append(tok)
            return tok

        tbq = q_bias_cols(0, QBI)
        tbk = q_bias_cols(8, KBI)

        t_id = sch.dma("y", "dm", lambda y: y.dma_start(out=IDF, in_=IDENT),
                       waits=[dm_chain[-1]])
        dm_chain.append(t_id)
        l_idb = sch.dma("g", "dgm", lambda g: g.dma_start(out=IDB, in_=IDENT))
        t_fbrow = sch.dma("g", "dgm", lambda g: g.dma_start(
            out=ROWR[0:1, 1024:2048], in_=FB), waits=[l_idb])

        quatw_lastv = [dm_chain[-1]] * 4

        def quat_w(dst_off, W_ap):
            lastv = quatw_lastv
            tok = None
            for ic in range(8):
                comp, half = ic // 2, ic % 2
                for ob in range(4):
                    w_idx, sgn = QUAT_TAB[(ob, comp)]
                    stg = P1FS1(ob, 256)
                    tdma = sch.dma("y", "dw%d" % ob, (lambda y, s_=W_ap[w_idx, 128 * half:128 * half + 128, :],
                                               d_=stg: y.dma_start(out=d_, in_=s_)),
                                   waits=[lastv[ob]])
                    dst = TEV(dst_off, BF16, 8 * D)[:, ic * D + ob * 256: ic * D + ob * 256 + 256]
                    tok = sch.op("v", (lambda v, d_=dst, s_=stg, g_=float(sgn):
                                       v.tensor_scalar(d_, s_, g_, None, ALU.mult)),
                                 waits=[tdma])
                    lastv[ob] = tok
            return tok

        twq = quat_w(0, QW)
        twk = quat_w(16384, KW)

        psum_free = {}
        stg_free = [None, None]
        norm_free = [None]

        def proj_norm(si, n, src_fn, wbig_off, bias_col, dst_fn, extra_first=()):
            p = si % 2
            dsem = "dx0" if p == 0 else "dx1"
            tld = None
            for ic in range(8):
                wt = [stg_free[p]] if stg_free[p] else []
                tld = sch.dma("g", dsem, (lambda g, s_=src_fn(ic), d_=XSG(p, ic, 0, n):
                                          g.dma_start(out=d_, in_=s_)), waits=wt)
            ev = []
            for oc in range(8):
                bank = oc % 4
                last_mm = None
                for ic in range(8):
                    wt = []
                    if ic == 0:
                        wt = [tld] + list(extra_first)
                        if psum_free.get(bank):
                            wt.append(psum_free[bank])
                    lhsT = TEV(wbig_off, BF16, 8 * D)[:, ic * D + oc * 128: ic * D + oc * 128 + 128]
                    rhs = XSG(p, ic, 0, n)
                    last_mm = sch.op("t", (lambda t, o_=PB(bank, 0, n), l_=lhsT, r_=rhs,
                                           s0=(ic == 0), s1=(ic == 7):
                                           t.matmul(o_, l_, r_, start=s0, stop=s1)),
                                     mark=(ic == 7), waits=wt)
                ev.append(sch.op("v", (lambda v, d_=P1FS1(oc, n), p_=PB(bank, 0, n),
                                       b_=BIASB[:, bias_col + oc: bias_col + oc + 1]:
                                       v.tensor_scalar(d_, p_, b_, None, ALU.add)),
                                 waits=[last_mm]))
                psum_free[bank] = ev[-1]
            stg_free[p] = sch.op("t", lambda t: t.nop(), mark=True)

            out_toks = []
            for h in range(2):
                ssum = P1FS2(h, n)
                wt = [ev[h]] + ([norm_free[0]] if norm_free[0] else [])
                t0s = sch.op("s", (lambda ss, d_=ssum, a_=P1FS1(h, n):
                                   ss.activation(d_, a_, AF.Square)),
                             waits=wt)
                t0 = None
                for j in (2, 4, 6):
                    tmp = P1FS2(2, n)
                    tsq = sch.op("s", (lambda ss, d_=tmp, a_=P1FS1(h + j, n):
                                       ss.activation(d_, a_, AF.Square)),
                                 waits=[ev[h + j]] + ([t0] if t0 else []))
                    t0 = sch.op("v", (lambda v, d_=ssum, a_=tmp:
                                      v.tensor_tensor(d_, d_, a_, ALU.add)),
                                waits=[tsq] + ([t0s] if j == 2 else []))
                t2 = sch.op("s", (lambda s, d_=ssum: s.activation(d_, d_, AF.Ln,
                                                                  bias=BIASB[:, 62:63])),
                            waits=[t0, t_eps6])
                t3 = sch.op("s", (lambda s, d_=ssum: s.activation(d_, d_, AF.Exp, scale=-0.5)),
                            waits=[t2])
                out_toks.append(t3)
            final = None
            for oc in range(8):
                h = oc % 2
                final = sch.op("v", (lambda v, d_=dst_fn(oc), a_=P1FS1(oc, n), b_=P1FS2(h, n):
                                     v.tensor_tensor(d_, a_, b_, ALU.mult)),
                               waits=[out_toks[h]])
            norm_free[0] = final
            return final

        wready = (twk, tbk, t_cv)
        k_done = q_done = None
        for si in range(8):
            k_done = proj_norm(
                si, 512,
                (lambda ic, si_=si: XFF[128 * ic:128 * ic + 128, 512 * si_:512 * si_ + 512]),
                16384, 8,
                (lambda oc, si_=si: KBUF(oc, 512 * si_, 512)),
                extra_first=(wready if si == 0 else ()))
        for sj in range(5):
            n = 512 if sj < 4 else 256
            q_done = proj_norm(
                8 + sj, n,
                (lambda ic, s_=sj, n_=n: XSF[128 * ic:128 * ic + 128, 512 * s_:512 * s_ + n_]),
                0, 0,
                (lambda oc, s_=sj, n_=n: QBUF(oc, 512 * s_, n_)))

        if stage == "p1a":
            for c in range(8):
                sch.dma("g", "dout", (lambda g, c_=c, q_=QBUF(c, 0, SPAN): g.dma_start(
                    out=DBG[:, c_ * SPAN: (c_ + 1) * SPAN],
                    in_=q_)), waits=[q_done, k_done])
            sch.op("y", lambda y: y.wait_ge(sem_handles["dout"], sch.cnt["dout"]))
            sch.emit(block, sem_handles)
            stack.close()
            return nc

        tk_t = None
        tkq_nop = sch.op("t", lambda t: t.nop(), mark=True)

        t_vrow = quat_bias_row(0, VBI, as_f32r_off=0)

        def quat_w2(dst_off, W_ap, firstw):
            lastv = [firstw] * 4
            tok = None
            for ic in range(8):
                comp, half = ic // 2, ic % 2
                for ob in range(4):
                    w_idx, sgn = QUAT_TAB[(ob, comp)]
                    stg = TEV(16384 + ob * 1024, F32, 256)
                    tdma = sch.dma("y", "dw%d" % ob,
                                   (lambda y, s_=W_ap[w_idx, 128 * half:128 * half + 128, :],
                                    d_=stg: y.dma_start(out=d_, in_=s_)),
                                   waits=[lastv[ob]])
                    dst = TEV(dst_off, BF16, 8 * D)[:, ic * D + ob * 256: ic * D + ob * 256 + 256]
                    tok = sch.op("v", (lambda v, d_=dst, s_=stg, g_=float(sgn):
                                       v.tensor_scalar(d_, s_, g_, None, ALU.mult)),
                                 waits=[tdma])
                    lastv[ob] = tok
            return tok

        twv = quat_w2(24576, VW, tkq_nop)

        VT3 = TY2.bitcast(BF16)[:, 36864 // 2: 36864 // 2 + 256 * 65].rearrange(
            "p (s c) -> p s c", c=65)
        t_vones = sch.op("v", lambda v: v.memset(VT3[:, :, 64:65], 1.0),
                         waits=[q_done, tkq_nop])

        def XSG2(c, a, n):
            base = 65536 // 2
            return TY3.bitcast(BF16)[:, base + c * 512 + a: base + c * 512 + a + n]

        def ATMv(a, n):
            base = 65536 // 2
            return TY3.bitcast(BF16)[:, base + a: base + a + n]

        def emit_p1b(g, first_waits):
            xsg2_free = None
            done = None
            for blk in range(8):
                wt = list(first_waits) + ([xsg2_free] if xsg2_free else [])
                tld0 = tld1 = None
                for ic in range(8):
                    sem = "db" if ic % 2 == 0 else "dx0"
                    t_ = sch.dma("g", sem,
                                 (lambda gg, s_=XFF[128 * ic:128 * ic + 128, 512 * blk:512 * blk + 512],
                                  d_=XSG2(ic, 0, 512): gg.dma_start(out=d_, in_=s_)),
                                 waits=wt if ic < 2 else [])
                    if ic % 2 == 0:
                        tld0 = t_
                    else:
                        tld1 = t_
                tld = tld0
                for sub4 in range(4):
                    bank = sub4 % 2
                    last = None
                    for ic in range(8):
                        w = []
                        if ic == 0:
                            w = [tld0, tld1]
                            if psum_free.get(bank):
                                w.append(psum_free[bank])
                            if ic == 0 and blk == 0 and sub4 == 0:
                                w.append(twv)
                                w.append(t_vrow)
                        lhsT = XSG2(ic, sub4 * 128, 128)
                        rhs = TEV(24576, BF16, 8 * D)[:, ic * D + g * 512: ic * D + g * 512 + 512]
                        last = sch.op("t", (lambda t, o_=PB(bank, 0, 512), l_=lhsT, r_=rhs,
                                            s0=(ic == 0): t.matmul(o_, l_, r_, start=s0, stop=False)),
                                      waits=w)
                    last = sch.op("t", (lambda t, o_=PB(bank, 0, 512),
                                        r_=ROWR[0:1, g * 512:(g + 1) * 512]:
                                        t.matmul(o_, ONER, r_, start=False, stop=True)),
                                  mark=True)
                    slot0 = (blk * 4 + sub4) * 8
                    dst3 = TY2.bitcast(BF16)[:, 36864 // 2 + slot0 * 65: 36864 // 2 + slot0 * 65 + 8 * 65].rearrange(
                        "p (s c) -> p s c", c=65)[:, :, 0:64]
                    src3 = PB(bank, 0, 512).rearrange("p (s c) -> p s c", c=64)
                    ev = sch.op("v", (lambda v, d_=dst3, s_=src3: v.tensor_copy(d_, s_)),
                                waits=[last, t_vones])
                    psum_free[bank] = ev
                    done = ev
                xsg2_free = sch.op("t", lambda t: t.nop(), mark=True)
            return done, xsg2_free

        def emit_p1c(g, ready, state, hook=None, nsets=2):
            PVB = (6, 7)
            for qt in range(NQT):
                batches = [(hp, H, half) for hp in range(4)
                           for H in (0, 1) for half in (0, 1)]
                etoks = {}

                def emit_A(j, qt=qt, batches=batches, etoks=etoks):
                    hp, H, half = batches[j]
                    kch = g * 4 + hp
                    toks = []
                    for sq in range(8):
                        par = (j * 8 + sq) % nsets
                        set_base = 2 * par
                        fw = []
                        if state["exp_prev"][par]:
                            fw.append(state["exp_prev"][par])
                        if qt == 0 and j == 0 and sq == 0:
                            fw += list(ready)
                        sub = H * 16 + half * 8 + sq
                        last = None
                        for hh in (0, 1):
                            bank = set_base + hh
                            lhsT = KBUF(kch, sub * 128, 128, p0=64 * hh, np_=64)
                            rhs = QBUF(kch, qt * QT, QT, p0=64 * hh, np_=64)
                            w = fw if hh == 0 else []
                            last = sch.op("t", (lambda t, o_=PB(bank, 0, QT), l_=lhsT,
                                                r_=rhs, tp=(64 * hh, 0):
                                                t.matmul(o_, l_, r_, start=True, stop=True,
                                                         tile_position=tp)),
                                          waits=w, mark=(hh == 1))
                        ew = [last]
                        if state["expt_free"][j % 2]:
                            ew.append(state["expt_free"][j % 2])
                        tile0 = (j % 2) * 16 + sq * 2
                        in_ap = PALL[:, set_base * 512: set_base * 512 + 1024].rearrange(
                            "p (b x) -> p b x", x=512)[:, :, 0:QT]
                        out_ap = TE[:, tile0 * QT: (tile0 + 2) * QT].rearrange(
                            "p (b x) -> p b x", x=QT)
                        etok = sch.op("s", (lambda ss, o_=out_ap, i_=in_ap:
                                            ss.activation(o_, i_, AF.Exp,
                                                          bias=BIASB[:, 63:64], scale=0.125)),
                                      waits=ew)
                        state["exp_prev"][par] = etok
                        toks.append(etok)
                    etoks[j] = toks

                def emit_P(j, qt=qt, batches=batches, etoks=etoks):
                    hp, H, half = batches[j]
                    b = PVB[hp % 2]
                    first_of_hp = (H == 0 and half == 0)
                    last_of_hp = (H == 1 and half == 1)
                    pv_last = None
                    for sq in range(8):
                        sub = H * 16 + half * 8 + sq
                        for hh in (0, 1):
                            hl = 2 * hp + hh
                            tile = (j % 2) * 16 + sq * 2 + hh
                            for qs in (0, 1, 2):
                                w = []
                                if hh == 0 and qs == 0:
                                    w.append(etoks[j][sq])
                                    if first_of_hp and sq == 0 and state["pvfree"][b]:
                                        w.append(state["pvfree"][b])
                                st = first_of_hp and sq == 0 and hh == 0 and qs == 0
                                lhsT = EXPT(tile, qs * 128, 128)
                                rhs = VTB(sub * 8 + hl, 0, 65)
                                out = PB(b, (hh * 3 + qs) * 65, 65)
                                pv_last = sch.op("t", (lambda t, o_=out, l_=lhsT, r_=rhs,
                                                       s0=st:
                                                       t.matmul(o_, l_, r_, start=s0, stop=False,
                                                                skip_group_check=True)),
                                                 waits=w,
                                                 mark=(sq == 7 and hh == 1 and qs == 2))
                    state["expt_free"][j % 2] = pv_last
                    if last_of_hp:
                        dens3 = PB(b, 0, 390).rearrange("p (s c) -> p s c", c=65)[:, :, 64:65]
                        t2 = sch.op("v", (lambda v, d3=dens3: v.reciprocal(SMALL[:, 80:86], d3)),
                                    waits=[pv_last])
                        ntok = None
                        for hh in (0, 1):
                            for qs in (0, 1, 2):
                                idx = hh * 3 + qs
                                hl = 2 * hp + hh
                                dst = ATMv(qs * 512 + hl * 64, 64)
                                w = [t2]
                                if state["atm_free"]:
                                    w.append(state["atm_free"])
                                ntok = sch.op("v", (lambda v, d_=dst,
                                                    s_=PB(b, idx * 65, 64),
                                                    r_=SMALL[:, 80 + idx: 81 + idx]:
                                                    v.tensor_scalar(d_, s_, r_, None, ALU.mult)),
                                              waits=w)
                        state["pvfree"][b] = ntok
                        state["pv_norm"] = ntok
                        if hook:
                            hook(3 * qt - 1, 2)

                emit_A(0)
                emit_A(1)
                for j in range(16):
                    emit_P(j)
                    if j + 2 < 16:
                        emit_A(j + 2)
                sp = None
                for qs in range(3):
                    sp = sch.dma("y", "dsp", (lambda y, r0=qt * QT + qs * 128,
                                              a_=ATMv(qs * 512, 512), g_=g:
                                              y.dma_start(out=ATT0[r0:r0 + 128, g_ * 512:(g_ + 1) * 512],
                                                          in_=a_)),
                                 waits=[state["pv_norm"]])
                state["atm_free"] = sp
                if hook:
                    hook(3 * qt + 2, 4)
            return state

        st_p1c = {"exp_prev": {0: None, 1: None, 2: None}, "pv_norm": None,
                  "expt_free": [None, None], "atm_free": None,
                  "pvfree": {6: None, 7: None}}
        v_done0, xf0 = emit_p1b(0, [t_vones])
        st_p1c = emit_p1c(0, [v_done0, tkq_nop, q_done], st_p1c, nsets=3)
        tg0 = sch.op("t", lambda t: t.nop(), mark=True)
        g0_exps = [t for t in (st_p1c["exp_prev"][0], st_p1c["exp_prev"][1],
                               st_p1c["exp_prev"][2]) if t]
        v_done1, xf1 = emit_p1b(1, [st_p1c["atm_free"], st_p1c["pv_norm"]])
        st_p1c["atm_free"] = xf1

        t_orow = quat_bias_row(0, OBI, as_f32r_off=0, final_extra=[xf1])

        def quat_w3(dst_fn, W_ap, firstw, stg_fn, extraw=()):
            lastv = [firstw] * 2
            tok = None
            for ic in range(8):
                comp, half = ic // 2, ic % 2
                for ob in range(4):
                    w_idx, sgn = QUAT_TAB[(ob, comp)]
                    slot = ob % 2
                    stg = stg_fn(slot)
                    tdma = sch.dma("y", "dw%d" % slot,
                                   (lambda y, s_=W_ap[w_idx, 128 * half:128 * half + 128, :],
                                    d_=stg: y.dma_start(out=d_, in_=s_)),
                                   waits=[lastv[slot]] + (list(extraw) if ic == 0 and ob < 2 else []))
                    dst = dst_fn(ic, ob)
                    tok = sch.op("v", (lambda v, d_=dst, s_=stg, g_=float(sgn):
                                       v.tensor_scalar(d_, s_, g_, None, ALU.mult)),
                                 waits=[tdma] + ([xf1] if ic == 0 and ob == 0 else []))
                    lastv[slot] = tok
            return tok

        two = quat_w3(
            (lambda ic, ob: TEV(24576, BF16, 8 * D)[:, ic * D + ob * 256: ic * D + ob * 256 + 256]),
            OW, twv,
            (lambda slot: TEV(22528 + slot * 1024, F32, 256)),
            extraw=[tg0] + g0_exps)

        def emit_quant(src_ap, dst_ap, wait0, slots):
            Aap, Bap, Cap, Dap = slots
            bmax = SMALL[:, 0:32]
            scl = SMALL[:, 32:64]
            rec = SMALL[:, 134:166]
            src3 = src_ap.rearrange("p (b w) -> p b w", w=32)
            t1 = sch.op("v", (lambda v: v.tensor_reduce(bmax, src3, AX.X, ALU.max,
                                                        apply_absolute_value=True)),
                        waits=list(wait0))
            t2 = sch.op("v", lambda v: v.tensor_scalar(scl, bmax, 1.0 / 7.5, 1e-30,
                                                       ALU.mult, ALU.max))
            t3 = sch.op("v", lambda v: v.reciprocal(rec, scl))
            rb = rec.unsqueeze(2).broadcast_to([128, 32, 32])
            sb = scl.unsqueeze(2).broadcast_to([128, 32, 32])
            A3 = Aap.rearrange("p (b w) -> p b w", w=32)
            t4 = sch.op("v", (lambda v: v.tensor_tensor(A3, src3, rb, ALU.mult)))
            t5 = sch.op("v", (lambda v: v.tensor_tensor(Bap, Aap, Aap, ALU.mult)))
            t6 = sch.op("v", lambda v: v.tensor_scalar(Cap, Bap, 4.0, None, ALU.is_lt))
            t7 = sch.op("v", lambda v: v.tensor_scalar(Bap, Bap, 16.0, None, ALU.is_lt))
            t8 = sch.op("v", lambda v: v.tensor_scalar(Dap, Bap, -0.25, 0.5, ALU.mult, ALU.add))
            t9 = sch.op("v", lambda v: v.scalar_tensor_tensor(Dap, Cap, -0.125, Dap,
                                                              ALU.mult, ALU.add))
            ta = sch.op("v", lambda v: v.scalar_tensor_tensor(Bap, Cap, 2.0, Bap,
                                                              ALU.mult, ALU.add))
            tb = sch.op("v", lambda v: v.tensor_scalar(Bap, Bap, 2.0, 2.0, ALU.mult, ALU.add))
            tc = sch.op("v", lambda v: v.tensor_tensor(Cap, Aap, Bap, ALU.mult))
            td = sch.op("v", lambda v: v.tensor_scalar(Cap, Cap, MAGIC, MAGIC,
                                                       ALU.add, ALU.subtract))
            te = sch.op("v", lambda v: v.tensor_tensor(Cap, Cap, Dap, ALU.mult))
            C3 = Cap.rearrange("p (b w) -> p b w", w=32)
            d3 = dst_ap.rearrange("p (b w) -> p b w", w=32)
            tf = sch.op("v", (lambda v: v.tensor_tensor(d3, C3, sb, ALU.mult)))
            return tf

        def QSLOT(i):
            return TY2.bitcast(F32)[:, i * 1024:(i + 1) * 1024]
        P2_SLOTS = (QSLOT(0), QSLOT(1), QSLOT(2), QSLOT(3))
        XDT_AP = TY2.bitcast(BF16)[:, 8192:9216]
        XDF_AP = TY2.bitcast(BF16)[:, 35072:36096]

        def Y1AP(p):
            off = 16384 if p == 0 else 24576
            return TY3.bitcast(F32)[:, off // 4: off // 4 + 1024]
        XTM_AP = TY3.bitcast(F32)[:, 20480 // 4: 20480 // 4 + 1024]

        def AF0(p, c, a, n):
            base = (28672 + p * 2048) // 2
            return TY3.bitcast(BF16)[:, base + c * 128 + a: base + c * 128 + a + n]

        p2s = {"oproj": {}, "y1evac": {}, "quant": {}, "transp": {},
               "ffn0": {}, "z2": {}, "spill": {}, "ln2": {}, "ln1e": {},
               "ln2e": {}}

        def p2_f1(sl):
            p = sl % 2
            dl = "dl0" if p == 0 else "dl1"
            tldA = None
            for c in range(8):
                w = []
                if c == 0:
                    w = [("dsp", sch.cnt["dsp"]), tg0]
                    if p2s["oproj"].get(sl - 2):
                        w.append(p2s["oproj"][sl - 2])
                tldA = sch.dma("y", dl, (lambda y, s_=ATT0[128 * sl:128 * sl + 128,
                                                           128 * c:128 * c + 128],
                                         d_=AF0(p, c, 0, 128):
                                         y.dma_start_transpose(d_, s_)), waits=w)
            w = [tg0] + ([p2s["y1evac"][sl - 1]] if sl >= 1 else [])
            tldX = sch.dma("g", "dxa", (lambda g, s_=XST[128 * sl:128 * sl + 128, :]:
                                        g.dma_start(out=XTM_AP, in_=s_)), waits=w)
            o_marks = []
            for nh in range(2):
                bank = 4 + nh
                for ic in range(8):
                    w = []
                    if ic == 0:
                        w = [tldA, two, t_orow]
                        if psum_free.get(bank):
                            w.append(psum_free[bank])
                    sch.op("t", (lambda t, o_=PB(bank, 0, 512),
                                 l_=AF0(p, ic, 0, 128),
                                 r_=TEV(24576, BF16, 8 * D)[:, ic * D + nh * 512: ic * D + nh * 512 + 512],
                                 s0=(ic == 0): t.matmul(o_, l_, r_, start=s0, stop=False)),
                           waits=w)
                last = sch.op("t", (lambda t, o_=PB(bank, 0, 512),
                                    r_=ROWR[0:1, nh * 512:(nh + 1) * 512]:
                                    t.matmul(o_, ONER, r_, start=False, stop=True)), mark=True)
                o_marks.append(last)
            p2s["oproj"][sl] = o_marks[-1]
            ev1 = None
            for nh in range(2):
                ev1 = sch.op("v", (lambda v, d_=Y1AP(p)[:, nh * 512:(nh + 1) * 512],
                                   a_=PB(4 + nh, 0, 512), b_=XTM_AP[:, nh * 512:(nh + 1) * 512]:
                                   v.tensor_tensor(d_, a_, b_, ALU.add)),
                             waits=[o_marks[nh], tldX])
                psum_free[4 + nh] = ev1
            p2s["y1evac"][sl] = ev1

        def p2_f2(sl):
            p = sl % 2
            y1ap = Y1AP(p)
            tsum = sch.op("v", (lambda v, z_=y1ap: v.tensor_scalar(QSLOT(1), z_, 1.0, 0.0,
                                                          ALU.mult, ALU.add,
                                                          accum_out=SMALL[:, 70:71])),
                          waits=[p2s["y1evac"][sl]])
            tsq = sch.op("v", (lambda v, z_=y1ap: v.scalar_tensor_tensor(QSLOT(2), z_, 1.0, z_,
                                                        ALU.mult, ALU.mult,
                                                        accum_out=SMALL[:, 71:72])))
            tm1 = sch.op("v", lambda v: v.tensor_scalar(SMALL[:, 72:73], SMALL[:, 70:71],
                                                        -1.0 / 1024.0, None, ALU.mult))
            tm2 = sch.op("v", lambda v: v.scalar_tensor_tensor(SMALL[:, 73:74], SMALL[:, 72:73],
                                                               1.0, SMALL[:, 72:73],
                                                               ALU.mult, ALU.mult))
            tvv = sch.op("v", lambda v: v.scalar_tensor_tensor(SMALL[:, 74:75], SMALL[:, 71:72],
                                                               1.0 / 1024.0, SMALL[:, 73:74],
                                                               ALU.mult, ALU.subtract))
            tl1 = sch.op("s", lambda ss: ss.activation(SMALL[:, 74:75], SMALL[:, 74:75], AF.Ln,
                                                       bias=BIASB[:, 61:62]), waits=[tvv, t_eps5])
            tl2 = sch.op("s", lambda ss: ss.activation(SMALL[:, 74:75], SMALL[:, 74:75], AF.Exp,
                                                       scale=-0.5))
            p2s["ln1e"][sl] = tl2

        def p2_f2b(sl):
            p = sl % 2
            y1ap = Y1AP(p)
            tap = sch.op("v", (lambda v, z_=y1ap: v.tensor_scalar(z_, z_, SMALL[:, 72:73],
                                                         SMALL[:, 74:75], ALU.add, ALU.mult)),
                         waits=[p2s["ln1e"][sl]])
            w0 = [tap, p2s["transp"][sl - 1]]
            p2s["quant"][sl] = emit_quant(y1ap, XDT_AP, w0, P2_SLOTS)

        def p2_back(sl):
            p = sl % 2
            last_tt = None
            evs = []
            for c in range(8):
                bank = 4 + (c % 2)
                w = [p2s["quant"][sl], l_idb]
                if psum_free.get(bank):
                    w.append(psum_free[bank])
                last_tt = sch.op("t", (lambda t, o_=PB(bank, (c // 2) * 128, 128, BF16),
                                  i_=XDT_AP[:, c * 128:(c + 1) * 128]:
                                  t.transpose(o_, i_, IDB)),
                            waits=w, mark=True)
                ew = [last_tt]
                if c == 0 and sl >= 1:
                    ew.append(p2s["spill"][sl - 1])
                ev = sch.op("v", (lambda v, d_=XDF_AP[:, c * 128:(c + 1) * 128],
                                  s_=PB(bank, (c // 2) * 128, 128, BF16): v.tensor_copy(d_, s_)),
                            waits=ew)
                psum_free[bank] = ev
                evs.append(ev)
            p2s["transp"][sl] = last_tt
            f_marks = []
            for nh in range(2):
                bank = 4 + nh
                for ic in range(8):
                    w = []
                    if ic == 0:
                        w = [evs[-1], wd_state["done"], t_fbrow]
                        if psum_free.get(bank):
                            w.append(psum_free[bank])
                    sch.op("t", (lambda t, o_=PB(bank, 0, 512),
                                 l_=XDF_AP[:, ic * 128:(ic + 1) * 128],
                                 r_=TY3V(0, BF16, 8 * D)[:, ic * D + nh * 512: ic * D + nh * 512 + 512],
                                 s0=(ic == 0): t.matmul(o_, l_, r_, start=s0, stop=False)),
                           waits=w)
                last = sch.op("t", (lambda t, o_=PB(bank, 0, 512),
                                    r_=ROWR[0:1, 1024 + nh * 512: 1024 + (nh + 1) * 512]:
                                    t.matmul(o_, ONER, r_, start=False, stop=True)), mark=True)
                f_marks.append(last)
            ev2 = None
            for nh in range(2):
                ev2 = sch.op("v", (lambda v, d_=QSLOT(0)[:, nh * 512:(nh + 1) * 512],
                                   a_=PB(4 + nh, 0, 512), b_=Y1AP(p)[:, nh * 512:(nh + 1) * 512]:
                                   v.tensor_tensor(d_, a_, b_, ALU.add)),
                             waits=[f_marks[nh]])
                psum_free[4 + nh] = ev2
            p2s["z2"][sl] = ev2
            zap = QSLOT(0)
            t2sum = sch.op("v", (lambda v, z_=zap: v.tensor_scalar(QSLOT(1), z_, 1.0, 0.0,
                                                           ALU.mult, ALU.add,
                                                           accum_out=SMALL[:, 75:76])),
                           waits=[ev2])
            t2sq = sch.op("v", (lambda v, z_=zap: v.scalar_tensor_tensor(QSLOT(3), z_, 1.0, z_,
                                                         ALU.mult, ALU.mult,
                                                         accum_out=SMALL[:, 76:77])))
            t2m1 = sch.op("v", lambda v: v.tensor_scalar(SMALL[:, 77:78], SMALL[:, 75:76],
                                                         -1.0 / 1024.0, None, ALU.mult))
            t2m2 = sch.op("v", lambda v: v.scalar_tensor_tensor(SMALL[:, 78:79], SMALL[:, 77:78],
                                                                1.0, SMALL[:, 77:78],
                                                                ALU.mult, ALU.mult))
            t2v = sch.op("v", lambda v: v.scalar_tensor_tensor(SMALL[:, 79:80], SMALL[:, 76:77],
                                                               1.0 / 1024.0, SMALL[:, 78:79],
                                                               ALU.mult, ALU.subtract))
            t2l1 = sch.op("s", lambda ss: ss.activation(SMALL[:, 79:80], SMALL[:, 79:80], AF.Ln,
                                                        bias=BIASB[:, 61:62]), waits=[t2v])
            t2l2 = sch.op("s", lambda ss: ss.activation(SMALL[:, 79:80], SMALL[:, 79:80], AF.Exp,
                                                        scale=-0.5))
            p2s["ln2e"][sl] = t2l2

        def p2_b2(sl):
            zap = QSLOT(0)
            t2ap = sch.op("v", (lambda v, z_=zap: v.tensor_scalar(XDF_AP, z_,
                                                          SMALL[:, 77:78], SMALL[:, 79:80],
                                                          ALU.add, ALU.mult)),
                          waits=[p2s["ln2e"][sl]])
            p2s["ln2"][sl] = t2ap
            p2s["spill"][sl] = sch.dma("y", "dy2", (lambda y, r0=128 * sl:
                                                    y.dma_start(out=Y2SP[r0:r0 + 128, :],
                                                                in_=XDF_AP)),
                                       waits=[t2ap])

        wd_state = {"tld": {}, "tq": {}, "done": None, "wt_last": None}

        def emit_wd(oc):
            if oc == 0:
                wd_state["tld"][0] = sch.dma(
                    "g", "da", (lambda g: g.dma_start(out=Y1AP(0), in_=FW[0:128, :])),
                    waits=[tg0])
            if oc + 1 < 8:
                w = [wd_state["tq"][oc - 1]] if oc >= 1 else [tg0]
                wd_state["tld"][oc + 1] = sch.dma(
                    "g", "da" if (oc + 1) % 2 == 0 else "dwd",
                    (lambda g, s_=FW[128 * (oc + 1):128 * (oc + 1) + 128, :],
                     d_=Y1AP((oc + 1) % 2): g.dma_start(out=d_, in_=s_)),
                    waits=w)
            w0 = [wd_state["tld"][oc], tg0]
            if oc >= 1:
                w0.append(wd_state["wt_last"])
            tq = emit_quant(Y1AP(oc % 2), XDT_AP, w0, P2_SLOTS)
            wd_state["tq"][oc] = tq
            evs = []
            for c in range(8):
                bank = 4 + (c % 2)
                w = [tq, l_idb]
                if psum_free.get(bank):
                    w.append(psum_free[bank])
                tt = sch.op("t", (lambda t, o_=PB(bank, (c // 2) * 128, 128, BF16),
                                  i_=XDT_AP[:, c * 128:(c + 1) * 128]:
                                  t.transpose(o_, i_, IDB)),
                            waits=w, mark=True)
                ev = sch.op("v", (lambda v, d_=TY3V(0, BF16, 8 * D)[:, c * D + oc * 128: c * D + oc * 128 + 128],
                                  s_=PB(bank, (c // 2) * 128, 128, BF16): v.tensor_copy(d_, s_)),
                            waits=[tt])
                psum_free[bank] = ev
                evs.append(ev)
                wd_state["wt_last"] = tt
            wd_state["done"] = evs[-1]
            if oc == 7:
                p2s["transp"][-1] = wd_state["wt_last"]

        p2drv = {"nf1": 0, "nf2a": 0, "nf2b": 0, "nb1": 0, "nb2": 0, "nwd": 0}

        def p2_pump(avail, budget):
            d = p2drv
            done = 0
            while done < budget:
                if d["nwd"] < 8:
                    emit_wd(d["nwd"]); d["nwd"] += 1; done += 1; continue
                k = d["nf1"]
                if k < NSL and k <= avail and d["nb1"] >= k - 1:
                    p2_f1(k); d["nf1"] += 1; done += 1; continue
                k = d["nb2"]
                if k < NSL and d["nb1"] >= k + 1:
                    p2_b2(k); d["nb2"] += 1; done += 1; continue
                k = d["nb1"]
                if k < NSL and d["nf2b"] >= k + 1 and d["nb2"] >= k:
                    p2_back(k); d["nb1"] += 1; done += 1; continue
                k = d["nf2b"]
                if k < NSL and d["nf2a"] >= k + 1 and d["nb2"] >= k and d["nb1"] >= k:
                    p2_f2b(k); d["nf2b"] += 1; done += 1; continue
                k = d["nf2a"]
                if k < NSL and d["nf1"] >= k + 1 and d["nf2b"] >= k:
                    p2_f2(k); d["nf2a"] += 1; done += 1; continue
                break

        st_p1c["expt_free"] = [two, two]
        st_p1c["expt_free"] = [two, two]
        st_p1c = emit_p1c(1, [v_done1], st_p1c, hook=p2_pump)
        att0_done = st_p1c["atm_free"]
        p2_pump(NSL, 10 ** 9)
        wdt_done = wd_state["done"]
        tg1 = sch.op("t", lambda t: t.nop(), mark=True)
        p2v_last = p2s["ln2"][NSL - 1]

        if stage == "p1c":
            for sl in range(NSL):
                sch.dma("g", "dout", (lambda gg, sl_=sl: gg.dma_start(
                    out=DBG[:, sl_ * 1024: (sl_ + 1) * 1024],
                    in_=ATT0[128 * sl_:128 * sl_ + 128, :])),
                    waits=[("dsp", sch.cnt["dsp"])])
            sch.op("y", lambda y: y.wait_ge(sem_handles["dout"], sch.cnt["dout"]))
            sch.emit(block, sem_handles)
            stack.close()
            return nc

        if stage == "p2":
            for sl in range(NSL):
                sch.dma("g", "dout", (lambda gg, sl_=sl: gg.dma_start(
                    out=DBG[:, sl_ * 1024: (sl_ + 1) * 1024],
                    in_=Y2SP[128 * sl_:128 * sl_ + 128, :])),
                    waits=[("dy2", 16 * NSL)])
            sch.op("y", lambda y: y.wait_ge(sem_handles["dout"], sch.cnt["dout"]))
            sch.emit(block, sem_handles)
            stack.close()
            return nc

        tp2_nop = sch.op("t", lambda t: t.nop(), mark=True)

        def EXL1(tile, a, n):
            return TE[:, tile * QT + a: tile * QT + a + n]

        def MT(var, a, n):
            base = 8192 // 2
            return TE[:, base + var * QT + a: base + var * QT + a + n]

        SIG_RANGE = {0: (0, 128), 1: (0, 256), 2: (0, 384), 3: (128, 384), 4: (256, 384)}

        l_wq1 = None
        for ic in range(8):
            l_wq1 = sch.dma("g", "da", (lambda g, s_=W1Q[128 * ic:128 * ic + 128, :],
                                        d_=TY3V(0, BF16, 8 * D)[:, ic * D: (ic + 1) * D]:
                                        g.dma_start(out=d_, in_=s_)),
                            waits=[tp2_nop] if ic == 0 else [])
        l_wk1 = None
        for ic in range(8):
            l_wk1 = sch.dma("g", "db", (lambda g, s_=W1K[128 * ic:128 * ic + 128, :],
                                        d_=TY3V(49408, BF16, 8 * 128)[:, ic * 128:(ic + 1) * 128]:
                                        g.dma_start(out=d_, in_=s_)),
                            waits=[tp2_nop] if ic == 0 else [])
        l_wv1 = None
        for ic in range(8):
            l_wv1 = sch.dma("g", "db", (lambda g, s_=W1V[128 * ic:128 * ic + 128, :],
                                        d_=TY3V(53504, BF16, 8 * 128)[:, ic * 128:(ic + 1) * 128]:
                                        g.dma_start(out=d_, in_=s_)),
                            waits=[l_wk1] if ic == 0 else [])
        l_b1q = sch.dma("y", "dm", lambda y: y.dma_start(out=BIASB[:, 32:40], in_=B1Q),
                        waits=[dm_chain[-1]])
        l_b1k = sch.dma("y", "dm", lambda y: y.dma_start(out=BIASB[:, 40:41], in_=B1K),
                        waits=[l_b1q])
        l_b1v = sch.dma("y", "dm", lambda y: y.dma_start(out=BIASB[:, 41:42], in_=B1VC),
                        waits=[l_b1k])
        l_mask = sch.dma("y", "dm", (lambda y: y.dma_start(out=MT(0, 0, 5 * QT), in_=MASKS)),
                         waits=[tp2_nop, l_b1v, wdt_done])
        def K1B(a, n, p0=0, np_=128):
            return TY3.bitcast(BF16)[p0:p0 + np_, 32768 // 2 + a: 32768 // 2 + a + n]

        def V1T(slot, a, n):
            return TY3.bitcast(BF16)[:, 37888 // 2 + slot * 65 + a: 37888 // 2 + slot * 65 + a + n]

        def Q1B(c, a, n, p0=0, np_=128):
            return TY3.bitcast(BF16)[p0:p0 + np_, 43264 // 2 + c * QT + a: 43264 // 2 + c * QT + a + n]

        def VFM(a, n):
            return TY3.bitcast(BF16)[:, 57600 // 2 + a: 57600 // 2 + a + n]

        z1 = sch.op("v", lambda v: v.memset(K1B(0, 128), 0.0), waits=[tp2_nop])
        z2 = sch.op("v", lambda v: v.memset(K1B(2432, 128), 0.0))
        V1T3 = TY3.bitcast(BF16)[:, 37888 // 2: 37888 // 2 + 40 * 65].rearrange(
            "p (s c) -> p s c", c=65)
        z3 = sch.op("v", lambda v: v.memset(V1T3[:, 0:2, :], 0.0))
        z4 = sch.op("v", lambda v: v.memset(V1T3[:, 38:40, :], 0.0))
        z5 = sch.op("v", lambda v: v.memset(V1T3[:, :, 64:65], 1.0))

        def YSTG(pb, ic, n):
            return TY3V(16384 + pb * 8192, BF16, 8 * 512)[:, ic * 512: ic * 512 + n]

        kv_done = None
        ystg_free = {0: None, 1: None}
        kv_cast = {}

        def emit_kv_cast(si, extra=()):
            n = 512 if si < 4 else 256
            pb = si % 2
            dls = "dl0" if pb == 0 else "dl1"
            cast_tok = None
            for ic in range(8):
                w = []
                if ic == 0:
                    w = [("dy2", 16 * NSL), tp2_nop, p2v_last] + list(extra)
                cast_tok = sch.dma("y", dls, (lambda y, d_=YSTG(pb, ic, n),
                                              s_=Y2SP[512 * si:512 * si + n,
                                                      128 * ic:128 * ic + 128]:
                                              y.dma_start_transpose(d_, s_)),
                                   waits=w)
            kv_cast[si] = cast_tok

        emit_kv_cast(0)
        emit_kv_cast(1)
        for si in range(5):
            n = 512 if si < 4 else 256
            pb = si % 2
            for which in (0, 1):
                bank = which
                woff = 49408 if which == 0 else 53504
                last = None
                for ic in range(8):
                    w = []
                    if ic == 0:
                        w = [l_wv1, l_b1v, kv_cast[si]]
                        if psum_free.get(bank):
                            w.append(psum_free[bank])
                    last = sch.op("t", (lambda t, o_=PB(bank, 0, n),
                                        l_=TY3V(woff, BF16, 8 * 128)[:, ic * 128:(ic + 1) * 128],
                                        r_=YSTG(pb, ic, n),
                                        s0=(ic == 0), s1=(ic == 7):
                                        t.matmul(o_, l_, r_, start=s0, stop=s1)),
                                  waits=w, mark=(ic == 7))
                if which == 0:
                    ev = sch.op("v", (lambda v, d_=K1B(128 + 512 * si, n), p_=PB(bank, 0, n):
                                      v.tensor_scalar(d_, p_, BIASB[:, 40:41], None, ALU.add)),
                                waits=[last, z2])
                else:
                    ev = sch.op("v", (lambda v, d_=VFM(512 * si, n), p_=PB(bank, 0, n):
                                      v.tensor_scalar(d_, p_, BIASB[:, 41:42], None, ALU.add)),
                                waits=[last])
                psum_free[bank] = ev
                kv_done = ev
            ystg_free[pb] = sch.op("t", lambda t: t.nop(), mark=True)
            if si + 2 < 5:
                emit_kv_cast(si + 2, extra=[ystg_free[pb]])
        vt_done = None
        for sub in range(NSL):
            bank = 4 + (sub % 2)
            tt = sch.op("t", (lambda t, o_=PB(bank, 0, 128, BF16),
                              i_=VFM(sub * 128, 128): t.transpose(o_, i_, IDB)),
                        waits=[kv_done, l_idb] + ([psum_free.get(bank)] if psum_free.get(bank) else []),
                        mark=True)
            slot0 = (sub + 1) * 2
            dst3 = TY3.bitcast(BF16)[:, 37888 // 2 + slot0 * 65: 37888 // 2 + slot0 * 65 + 2 * 65].rearrange(
                "p (s c) -> p s c", c=65)[:, :, 0:64]
            src3 = PB(bank, 0, 128, BF16).rearrange("p (s c) -> p s c", c=64)
            ev = sch.op("v", (lambda v, d_=dst3, s_=src3: v.tensor_copy(d_, s_)),
                        waits=[tt, z5])
            psum_free[bank] = ev
            vt_done = ev

        l_w1o = None
        for ic in range(8):
            l_w1o = sch.dma("g", "da", (lambda g, s_=W1O[128 * ic:128 * ic + 128, :],
                                        d_=TEV(24576, BF16, 8 * D)[:, ic * D:(ic + 1) * D]:
                                        g.dma_start(out=d_, in_=s_)),
                            waits=[l_wq1, tp2_nop, p2v_last] if ic == 0 else [])
        l_b1o = sch.dma("g", "dgm", lambda g: g.dma_start(out=ROWR[0:1, 0:1024], in_=B1O),
                        waits=[t_fbrow, tp2_nop, p2v_last])

        def AF1(p, c, a, n):
            base = p * 3072
            return TY2.bitcast(BF16)[:, base + c * QT + a: base + c * QT + a + n]

        def Y3B(p):
            return TY2.bitcast(BF16)[:, 6144 + p * 1024: 6144 + p * 1024 + 1024]

        Z3AP = TEV(12288, F32, 1024)
        JUNK3 = TEV(16384, F32, 1024)

        p4s = {"oproj": {}, "z3": {}, "spill": {}, "af1": {}, "ln3e": {},
               "n": 0, "nb": 0}

        def p4_slice(sl):
            p = sl % 2
            g = sl // 3
            pg = g % 2
            if sl % 3 == 0:
                tldA = None
                for c in range(8):
                    w = []
                    if c == 0:
                        w = [("dsp", sch.cnt["dsp"])]
                        if p4s["oproj"].get(3 * (g - 2) + 2):
                            w.append(p4s["oproj"][3 * (g - 2) + 2])
                        if g < 2:
                            w += [tp2_nop, p2v_last]
                    tldA = sch.dma("y", "dw2" if pg == 0 else "dw3",
                                   (lambda y, s_=ATT1[QT * g:QT * g + QT,
                                                      128 * c:128 * c + 128],
                                    d_=AF1(pg, c, 0, QT): y.dma_start_transpose(d_, s_)),
                                   waits=w)
                p4s["af1"][g] = tldA
            tldA = p4s["af1"][g]
            wy = [p4s["z3"][sl - 2]] if sl >= 2 else [("dy2", 16 * NSL), p2v_last]
            ldy2 = sch.dma("g", "dx0" if p == 0 else "dx1",
                           (lambda g, d_=TEV(20480 + p * 2048, BF16, 1024),
                            s_=Y2SP[128 * sl:128 * sl + 128, :]:
                            g.dma_start(out=d_, in_=s_)), waits=wy)
            o_marks = []
            for nh in range(2):
                bank = 4 + nh
                for ic in range(8):
                    w = []
                    if ic == 0:
                        w = [tldA, l_w1o, l_b1o]
                        if psum_free.get(bank):
                            w.append(psum_free[bank])
                    sch.op("t", (lambda t, o_=PB(bank, 0, 512),
                                 l_=AF1(pg, ic, (sl % 3) * 128, 128),
                                 r_=TEV(24576, BF16, 8 * D)[:, ic * D + nh * 512: ic * D + nh * 512 + 512],
                                 s0=(ic == 0): t.matmul(o_, l_, r_, start=s0, stop=False)),
                           waits=w)
                last = sch.op("t", (lambda t, o_=PB(bank, 0, 512),
                                    r_=ROWR[0:1, nh * 512:(nh + 1) * 512]:
                                    t.matmul(o_, ONER, r_, start=False, stop=True)), mark=True)
                o_marks.append(last)
            p4s["oproj"][sl] = o_marks[-1]
            ev3 = None
            for nh in range(2):
                ev3 = sch.op("v", (lambda v, d_=Z3AP[:, nh * 512:(nh + 1) * 512],
                                   a_=PB(4 + nh, 0, 512),
                                   b_=TEV(20480 + p * 2048, BF16, 1024)[:, nh * 512:(nh + 1) * 512]:
                                   v.tensor_tensor(d_, a_, b_, ALU.add)),
                             waits=[o_marks[nh], ldy2])
                psum_free[4 + nh] = ev3
            p4s["z3"][sl] = ev3
            t3s = sch.op("v", (lambda v: v.tensor_scalar(JUNK3, Z3AP, 1.0, 0.0,
                                                         ALU.mult, ALU.add,
                                                         accum_out=SMALL[:, 70:71])),
                         waits=[ev3])
            t3q = sch.op("v", (lambda v: v.scalar_tensor_tensor(JUNK3, Z3AP, 1.0, Z3AP,
                                                        ALU.mult, ALU.mult,
                                                        accum_out=SMALL[:, 71:72])))
            t3m = sch.op("v", lambda v: v.tensor_scalar(SMALL[:, 72:73], SMALL[:, 70:71],
                                                        -1.0 / 1024.0, None, ALU.mult))
            t3m2 = sch.op("v", lambda v: v.scalar_tensor_tensor(SMALL[:, 73:74], SMALL[:, 72:73],
                                                                1.0, SMALL[:, 72:73],
                                                                ALU.mult, ALU.mult))
            t3v = sch.op("v", lambda v: v.scalar_tensor_tensor(SMALL[:, 74:75], SMALL[:, 71:72],
                                                               1.0 / 1024.0, SMALL[:, 73:74],
                                                               ALU.mult, ALU.subtract))
            t3l = sch.op("s", lambda ss: ss.activation(SMALL[:, 74:75], SMALL[:, 74:75], AF.Ln,
                                                       bias=BIASB[:, 61:62]), waits=[t3v])
            t3e = sch.op("s", lambda ss: ss.activation(SMALL[:, 74:75], SMALL[:, 74:75], AF.Exp,
                                                       scale=-0.5))
            p4s["ln3e"][sl] = t3e

        def p4_apply(sl):
            p = sl % 2
            wa = [p4s["ln3e"][sl]]
            if sl >= 2:
                wa.append(p4s["spill"][sl - 2])
            t3a = sch.op("v", (lambda v, y_=Y3B(p): v.tensor_scalar(y_, Z3AP,
                                                          SMALL[:, 72:73], SMALL[:, 74:75],
                                                          ALU.add, ALU.mult)),
                         waits=wa)
            wsp = [t3a]
            if sl >= 1:
                wsp.append(p4s["spill"][sl - 1])
            p4s["spill"][sl] = sch.dma("y", "dy3", (lambda y, r0=128 * sl, y_=Y3B(p):
                                                    y.dma_start(out=Y3SP[r0:r0 + 128, :],
                                                                in_=y_)),
                                       waits=wsp)

        def p4_pump(avail, budget):
            done = 0
            while done < budget:
                k = p4s["nb"]
                if k < NSL and p4s["n"] >= k + 1:
                    p4_apply(k); p4s["nb"] += 1; done += 1; continue
                k = p4s["n"]
                if k < NSL and k <= avail and p4s["nb"] >= k:
                    p4_slice(k); p4s["n"] += 1; done += 1; continue
                break

        st1 = {"exp_prev": {0: None, 1: None, 2: None}, "pv_norm": None,
               "b6_free": None, "atm_free": None, "expt_free": None}
        q_cast = {}

        def emit_q_cast(qt, extra=()):
            pb = qt % 2
            dls = "dl0" if pb == 0 else "dl1"
            tok = None
            for ic in range(8):
                w = list(extra) if ic == 0 else []
                tok = sch.dma("y", dls, (lambda y, d_=YSTG(pb, ic, QT),
                                         s_=Y2SP[qt * QT:(qt + 1) * QT,
                                                 128 * ic:128 * ic + 128]:
                                         y.dma_start_transpose(d_, s_)),
                              waits=w)
            q_cast[qt] = tok

        emit_q_cast(0, [ystg_free[0], vt_done])
        emit_q_cast(1, [ystg_free[1], vt_done])
        qfree = {}
        for qt in range(NQT):
            qcast = q_cast[qt]
            q_marks = []
            for oc in range(8):
                bank = oc % 2
                last = None
                for ic in range(8):
                    w = []
                    if ic == 0:
                        w = [l_wq1, l_b1q, vt_done, qcast]
                        if psum_free.get(bank):
                            w.append(psum_free[bank])
                    last = sch.op("t", (lambda t, o_=PB(bank, 0, QT),
                                        l_=TY3V(0, BF16, 8 * D)[:, ic * D + oc * 128: ic * D + oc * 128 + 128],
                                        r_=YSTG(qt % 2, ic, QT),
                                        s0=(ic == 0), s1=(ic == 7):
                                        t.matmul(o_, l_, r_, start=s0, stop=s1)),
                                  waits=w, mark=(ic == 7))
                ev = sch.op("v", (lambda v, d_=Q1B(oc, 0, QT), p_=PB(bank, 0, QT),
                                  b_=BIASB[:, 32 + oc:33 + oc]:
                                  v.tensor_scalar(d_, p_, b_, None, ALU.add)),
                            waits=[last] + ([st1["pv_norm"]] if st1["pv_norm"] else []))
                psum_free[bank] = ev
                q_marks.append(ev)
            q_ready = q_marks[-1]
            qfree[qt] = sch.op("t", lambda t: t.nop(), mark=True)
            if qt + 2 < NQT:
                emit_q_cast(qt + 2, [qfree[qt]])
            for hp in range(8):
                exp_toks = {}
                for sg in range(5):
                    sb = qt * 3 + sg
                    lo, hi = SIG_RANGE[sg]
                    w_ = hi - lo
                    par = sg % 2
                    set_base = 2 * par
                    fw = [q_ready, l_mask]
                    if st1["exp_prev"][par]:
                        fw.append(st1["exp_prev"][par])
                    last = None
                    for hh in (0, 1):
                        bank = set_base + hh
                        lhsT = K1B(sb * 128, 128, p0=64 * hh, np_=64)
                        rhs = Q1B(hp, lo, w_, p0=64 * hh, np_=64)
                        ww = fw if hh == 0 else []
                        last = sch.op("t", (lambda t, o_=PB(bank, lo, w_), l_=lhsT, r_=rhs,
                                            tp=(64 * hh, 0):
                                            t.matmul(o_, l_, r_, start=True, stop=True,
                                                     tile_position=tp)),
                                      waits=ww, mark=(hh == 1))
                    ew = [last]
                    if st1["expt_free"]:
                        ew.append(st1["expt_free"])
                    in_ap = PALL[:, set_base * 512 + lo: set_base * 512 + lo + 1024].rearrange(
                        "p (b x) -> p b x", x=512)[:, :, 0:w_]
                    out_ap = TE[:, (sg * 2) * QT + lo: (sg * 2) * QT + lo + w_] if False else None
                    out3 = TE[:, 0:2 * QT].rearrange("p (b x) -> p b x", x=QT)[
                        :, 0:2, 0:1]
                    o3 = TE[:, (sg * 2) * QT + lo: (sg * 2 + 1) * QT + lo + w_] if False else None
                    out_ap = TE[:, (sg * 2) * QT: (sg * 2 + 2) * QT].rearrange(
                        "p (b x) -> p b x", x=QT)[:, :, lo:lo + w_]
                    etok = sch.op("s", (lambda ss, o_=out_ap, i_=in_ap:
                                        ss.activation(o_, i_, AF.Exp,
                                                      bias=BIASB[:, 60:61], scale=1.0)),
                                  waits=ew)
                    st1["exp_prev"][par] = etok
                    mtok = None
                    for hh in (0, 1):
                        mtok = sch.op("v", (lambda v, d_=EXL1(sg * 2 + hh, lo, w_),
                                            m_=MT(sg, lo, w_):
                                            v.tensor_tensor(d_, d_, m_, ALU.mult)),
                                      waits=[etok])
                    exp_toks[sg] = mtok
                pv_first = True
                pv_last = None
                for qs in (0, 1, 2):
                    for hh in (0, 1):
                        for sg in (qs, qs + 1, qs + 2):
                            w = [exp_toks[sg]]
                            if pv_first and st1["b6_free"]:
                                w.append(st1["b6_free"])
                            lhsT = EXL1(sg * 2 + hh, qs * 128, 128)
                            rhs = V1T((qt * 3 + sg) * 2 + hh, 0, 65)
                            out = PB(6, (hh * 3 + qs) * 65, 65)
                            pv_last = sch.op("t", (lambda t, o_=out, l_=lhsT, r_=rhs,
                                                   s0=pv_first:
                                                   t.matmul(o_, l_, r_, start=s0, stop=False,
                                                            skip_group_check=True)),
                                             waits=w,
                                             mark=(qs == 2 and hh == 1 and sg == qs + 2))
                            pv_first = False
                st1["expt_free"] = pv_last
                dens3 = PB(6, 0, 390).rearrange("p (s c) -> p s c", c=65)[:, :, 64:65]
                t2 = sch.op("v", (lambda v: v.reciprocal(SMALL[:, 86:92], dens3)),
                            waits=[pv_last])
                ntok = None
                for hh in (0, 1):
                    for qs in (0, 1, 2):
                        idx = hh * 3 + qs
                        hcol = (hp + 8 * hh) * 64
                        dst = ATMv(qs * 1024 + hcol, 64)
                        w = [t2]
                        if st1["atm_free"]:
                            w.append(st1["atm_free"])
                        ntok = sch.op("v", (lambda v, d_=dst,
                                            s_=PB(6, idx * 65, 64),
                                            r_=SMALL[:, 86 + idx: 87 + idx]:
                                            v.tensor_scalar(d_, s_, r_, None, ALU.mult)),
                                      waits=w)
                st1["pv_norm"] = ntok
                st1["b6_free"] = ntok
                p4_pump(3 * qt - 1, 1)
            sp = None
            for qs in range(3):
                sp = sch.dma("y", "dsp", (lambda y, r0=qt * QT + qs * 128,
                                          a_=ATMv(qs * 1024, 1024):
                                          y.dma_start(out=ATT1[r0:r0 + 128, :], in_=a_)),
                             waits=[st1["pv_norm"]])
            st1["atm_free"] = sp
            p4_pump(3 * qt + 2, 2)
        att1_done = st1["atm_free"]
        p4_pump(NSL, 10 ** 9)

        if stage == "p3":
            for sl in range(NSL):
                sch.dma("g", "dout", (lambda gg, sl_=sl: gg.dma_start(
                    out=DBG[:, sl_ * 1024: (sl_ + 1) * 1024],
                    in_=ATT1[128 * sl_:128 * sl_ + 128, :])),
                    waits=[("dsp", sch.cnt["dsp"])])
            sch.op("y", lambda y: y.wait_ge(sem_handles["dout"], sch.cnt["dout"]))
            sch.emit(block, sem_handles)
            stack.close()
            return nc

        tp3_nop = sch.op("t", lambda t: t.nop(), mark=True)

        zc = sch.op("v", lambda v: v.memset(SMALL[:, 166:167], 0.0))
        RLG = [(0, 512), (512, 512), (1024, 512), (1536, 512), (2048, 256)]
        rl_grp = {}
        for gi, (a0, ng) in enumerate(RLG):
            dl = "dl0" if gi % 2 == 0 else "dl1"
            tok = None
            for c in range(8):
                w = []
                if c == 0:
                    w = [("dy3", 16 * ((a0 + ng) // 128)), tp3_nop, st1["pv_norm"]]
                    if gi >= 2:
                        w.append(rl_grp[gi - 2])
                tok = sch.dma("y", dl, (lambda y, d_=Y3C(c, a0, ng),
                                        s_=Y3SP[a0:a0 + ng, 128 * c:128 * c + 128]:
                                        y.dma_start_transpose(d_, s_)), waits=w)
            rl_grp[gi] = tok

        if stage == "p4":
            for sl in range(NSL):
                sch.dma("g", "dout", (lambda gg, sl_=sl: gg.dma_start(
                    out=DBG[:, sl_ * 1024: (sl_ + 1) * 1024],
                    in_=Y3SP[128 * sl_:128 * sl_ + 128, :])),
                    waits=[("dy3", 16 * NSL)])
            sch.op("y", lambda y: y.wait_ge(sem_handles["dout"], sch.cnt["dout"]))
            sch.emit(block, sem_handles)
            stack.close()
            return nc

        tp4_nop = sch.op("t", lambda t: t.nop(), mark=True)

        l_mur = sch.dma("y", "dm", lambda y: y.dma_start(out=BIASB[:, 42:50], in_=MUR),
                        waits=[l_mask])
        l_muk = sch.dma("y", "dm", lambda y: y.dma_start(out=BIASB[:, 50:58], in_=MUK),
                        waits=[l_mur])
        l_wr = None
        for ic in range(8):
            l_wr = sch.dma("g", "da", (lambda g, s_=WRT[128 * ic:128 * ic + 128, :],
                                       d_=TY2V(0, BF16, 8 * D)[:, ic * D:(ic + 1) * D]:
                                       g.dma_start(out=d_, in_=s_)),
                           waits=[tp4_nop, ("dy3", 16 * NSL), l_w1o] if ic == 0 else [])
        TILES = [(0, 512), (512, 512), (1024, 512), (1536, 512), (2048, 256)]

        def rl_need(ti):
            return [rl_grp[ti]] + ([rl_grp[ti - 1]] if ti >= 1 else [])
        stash_tok = None
        for ti in range(1, 5):
            a = TILES[ti][0]
            for c in range(8):
                stash_tok = sch.op("v", (lambda v, d_=SMALL[:, 96 + (ti - 1) * 8 + c: 97 + (ti - 1) * 8 + c],
                                         s_=Y3C(c, a - 1, 1): v.tensor_copy(d_, s_)),
                                   waits=[rl_grp[ti - 1]] if c == 0 else [])
        r_done = None
        xr_free = None
        for ti, (a, n) in enumerate(TILES):
            xr_toks = []
            for ic in range(8):
                if ti == 0:
                    td = sch.op("v", (lambda v, d_=TEV(36864, F32, 512)[:, 1:n],
                                      p_=Y3C(ic, 1, n - 1), q_=Y3C(ic, 0, n - 1):
                                      v.tensor_tensor(d_, p_, q_, ALU.subtract)),
                                waits=rl_need(ti) + [zc, l_muk] + ([xr_free] if xr_free else []))
                    td2 = sch.op("v", (lambda v, d_=TEV(36864, F32, 512)[:, 0:1],
                                       p_=Y3C(ic, 0, 1), q_=SMALL[:, 166:167]:
                                       v.tensor_tensor(d_, p_, q_, ALU.subtract)))
                    tx = sch.op("v", (lambda v, d_=TY2V(32768, BF16, 8 * 512)[:, ic * 512 + 1: ic * 512 + n],
                                      s_=TEV(36864, F32, 512)[:, 1:n], m_=BIASB[:, 42 + ic:43 + ic],
                                      q_=Y3C(ic, 0, n - 1):
                                      v.scalar_tensor_tensor(d_, s_, m_, q_, ALU.mult, ALU.add)))
                    tx = sch.op("v", (lambda v, d_=TY2V(32768, BF16, 8 * 512)[:, ic * 512: ic * 512 + 1],
                                      s_=TEV(36864, F32, 512)[:, 0:1], m_=BIASB[:, 42 + ic:43 + ic],
                                      q_=SMALL[:, 166:167]:
                                      v.scalar_tensor_tensor(d_, s_, m_, q_, ALU.mult, ALU.add)))
                else:
                    td = sch.op("v", (lambda v, d_=TEV(36864, F32, 512)[:, 0:n],
                                      p_=Y3C(ic, a, n), q_=Y3C(ic, a - 1, n):
                                      v.tensor_tensor(d_, p_, q_, ALU.subtract)),
                                waits=rl_need(ti) + [zc, l_muk] + ([xr_free] if xr_free else []))
                    tx = sch.op("v", (lambda v, d_=TY2V(32768, BF16, 8 * 512)[:, ic * 512: ic * 512 + n],
                                      s_=TEV(36864, F32, 512)[:, 0:n], m_=BIASB[:, 42 + ic:43 + ic],
                                      q_=Y3C(ic, a - 1, n):
                                      v.scalar_tensor_tensor(d_, s_, m_, q_, ALU.mult, ALU.add)))
                xr_toks.append(tx)
            for oc in range(8):
                bank = oc % 2
                last = None
                for ic in range(8):
                    w = []
                    if ic == 0:
                        w = [xr_toks[-1], l_wr]
                        if psum_free.get(bank):
                            w.append(psum_free[bank])
                    last = sch.op("t", (lambda t, o_=PB(bank, 0, n),
                                        l_=TY2V(0, BF16, 8 * D)[:, ic * D + oc * 128: ic * D + oc * 128 + 128],
                                        r_=TY2V(32768, BF16, 8 * 512)[:, ic * 512: ic * 512 + n],
                                        s0=(ic == 0), s1=(ic == 7):
                                        t.matmul(o_, l_, r_, start=s0, stop=s1)),
                                  waits=w, mark=(ic == 7))
                ev = sch.op("s", (lambda ss, d_=TE[:, oc * SPAN + a: oc * SPAN + a + n],
                                  p_=PB(bank, 0, n):
                                  ss.activation(d_, p_, AF.Sigmoid)),
                            waits=[last])
                psum_free[bank] = ev
                r_done = ev
            xr_free = sch.op("t", lambda t: t.nop(), mark=True)
        l_wk = None
        for ic in range(8):
            l_wk = sch.dma("g", "db", (lambda g, s_=WKT[128 * ic:128 * ic + 128, :],
                                       d_=TY2V(0, BF16, 8 * D)[:, ic * D:(ic + 1) * D]:
                                       g.dma_start(out=d_, in_=s_)),
                           waits=[r_done] if ic == 0 else [])
        l_wv = None
        for ic in range(8):
            l_wv = sch.dma("g", "db", (lambda g, s_=WKT[0:1, 0:1] if False else WVT[128 * ic:128 * ic + 128, :],
                                       d_=TY2V(49152, BF16, 8 * D)[:, ic * D:(ic + 1) * D]:
                                       g.dma_start(out=d_, in_=s_)),
                           waits=[l_wk] if ic == 0 else [])
        out_tok = None
        xk_free = None
        y4_free = [None]
        p5s = {"kact": {}, "z4": {}}

        def p5_s1(ti):
            a, n = TILES[ti]
            xk_toks = []
            for ic in range(8):
                dslot = TEV(36864, F32, 512)
                std = (SMALL[:, 166:167] if ti == 0 else
                       SMALL[:, 96 + (ti - 1) * 8 + ic: 97 + (ti - 1) * 8 + ic])
                w = [l_wk, l_muk, stash_tok]
                if xk_free:
                    w.append(xk_free)
                if y4_free[0]:
                    w.append(y4_free[0])
                td = sch.op("v", (lambda v, d_=dslot[:, 1:n],
                                  p_=Y3C(ic, 1 + a, n - 1), q_=Y3C(ic, a, n - 1):
                                  v.tensor_tensor(d_, p_, q_, ALU.subtract)),
                            waits=w)
                td2 = sch.op("v", (lambda v, d_=dslot[:, 0:1],
                                   p_=Y3C(ic, a, 1), q_=std:
                                   v.tensor_tensor(d_, p_, q_, ALU.subtract)))
                tx = sch.op("v", (lambda v, d_=TY2V(32768, BF16, 8 * 512)[:, ic * 512 + 1: ic * 512 + n],
                                  s_=dslot[:, 1:n], m_=BIASB[:, 50 + ic:51 + ic],
                                  q_=Y3C(ic, a, n - 1):
                                  v.scalar_tensor_tensor(d_, s_, m_, q_, ALU.mult, ALU.add)))
                tx = sch.op("v", (lambda v, d_=TY2V(32768, BF16, 8 * 512)[:, ic * 512: ic * 512 + 1],
                                  s_=dslot[:, 0:1], m_=BIASB[:, 50 + ic:51 + ic],
                                  q_=std: v.scalar_tensor_tensor(d_, s_, m_, q_, ALU.mult, ALU.add)))
                xk_toks.append(tx)
            kact_done = None
            for oc in range(8):
                bank = oc % 2
                last = None
                for ic in range(8):
                    w = []
                    if ic == 0:
                        w = [xk_toks[-1]]
                        if psum_free.get(bank):
                            w.append(psum_free[bank])
                    last = sch.op("t", (lambda t, o_=PB(bank, 0, n),
                                        l_=TY2V(0, BF16, 8 * D)[:, ic * D + oc * 128: ic * D + oc * 128 + 128],
                                        r_=TY2V(32768, BF16, 8 * 512)[:, ic * 512: ic * 512 + n],
                                        s0=(ic == 0), s1=(ic == 7):
                                        t.matmul(o_, l_, r_, start=s0, stop=s1)),
                                  waits=w, mark=(ic == 7))
                trl = sch.op("v", (lambda v, d_=TEV(38912, F32, 512)[:, 0:n], p_=PB(bank, 0, n):
                                   v.tensor_scalar(d_, p_, 0.0, None, ALU.max)),
                             waits=[last])
                tsq2 = sch.op("v", (lambda v, d_=TY2V(65536, BF16, 8 * 512)[:, oc * 512: oc * 512 + n],
                                    s_=TEV(38912, F32, 512)[:, 0:n]:
                                    v.tensor_tensor(d_, s_, s_, ALU.mult)))
                psum_free[bank] = tsq2
                kact_done = tsq2
            p5s["kact"][ti] = kact_done
            return sch.op("t", lambda t: t.nop(), mark=True)

        def p5_s2(ti):
            a, n = TILES[ti]
            z4_done = None
            for oc in range(8):
                bank = 2 + (oc % 2)
                last = None
                for ic in range(8):
                    w = []
                    if ic == 0:
                        w = [p5s["kact"][ti], l_wv]
                        if psum_free.get(bank):
                            w.append(psum_free[bank])
                    last = sch.op("t", (lambda t, o_=PB(bank, 0, n),
                                        l_=TY2V(49152, BF16, 8 * D)[:, ic * D + oc * 128: ic * D + oc * 128 + 128],
                                        r_=TY2V(65536, BF16, 8 * 512)[:, ic * 512: ic * 512 + n],
                                        s0=(ic == 0), s1=(ic == 7):
                                        t.matmul(o_, l_, r_, start=s0, stop=s1)),
                                  waits=w, mark=(ic == 7))
                to1 = sch.op("v", (lambda v, d_=TEV(38912, F32, 512)[:, 0:n], p_=PB(bank, 0, n),
                                   r_=TE[:, oc * SPAN + a: oc * SPAN + a + n]:
                                   v.tensor_tensor(d_, p_, r_, ALU.mult)),
                             waits=[last] + ([y4_free[0]] if (oc == 0 and y4_free[0]) else []))
                tz4 = sch.op("v", (lambda v, d_=Y3C(oc, a, n),
                                   s_=TEV(38912, F32, 512)[:, 0:n]:
                                   v.tensor_tensor(d_, d_, s_, ALU.add)))
                psum_free[bank] = tz4
                z4_done = tz4
            p5s["z4"][ti] = z4_done

        def p5_s3(ti):
            a, n = TILES[ti]
            z4_done = p5s["z4"][ti]
            for ls in range(n // 128):
                tok0 = a + ls * 128
                tts = []
                for c in range(8):
                    bank = 4
                    tt = sch.op("t", (lambda t, o_=PB(bank, c * 128, 128, BF16),
                                      i_=Y3C(c, tok0, 128):
                                      t.transpose(o_, i_, IDB)),
                                waits=[z4_done] + ([psum_free[bank]] if psum_free.get(bank) else []),
                                mark=True)
                    tts.append(tt)
                    psum_free[bank] = None
                z4p = PB(4, 0, 1024, BF16)
                junk = TY3V(40960, F32, 1024)
                t4s = sch.op("v", (lambda v, z_=z4p, j_=junk: v.tensor_scalar(
                    j_, z_, 1.0, 0.0, ALU.mult, ALU.add, accum_out=SMALL[:, 70:71])),
                             waits=[tts[-1]])
                t4q = sch.op("s", (lambda ss, z_=z4p, j_=junk: ss.activation(
                    j_, z_, AF.Square, accum_out=SMALL[:, 71:72])),
                             waits=[t4s])
                t4m = sch.op("v", lambda v: v.tensor_scalar(SMALL[:, 72:73], SMALL[:, 70:71],
                                                            -1.0 / 1024.0, None, ALU.mult),
                             waits=[t4q])
                t4m2 = sch.op("v", lambda v: v.scalar_tensor_tensor(SMALL[:, 73:74], SMALL[:, 72:73],
                                                                    1.0, SMALL[:, 72:73],
                                                                    ALU.mult, ALU.mult))
                t4v = sch.op("v", lambda v: v.scalar_tensor_tensor(SMALL[:, 74:75], SMALL[:, 71:72],
                                                                   1.0 / 1024.0, SMALL[:, 73:74],
                                                                   ALU.mult, ALU.subtract))
                t4l = sch.op("s", lambda ss: ss.activation(SMALL[:, 74:75], SMALL[:, 74:75], AF.Ln,
                                                           bias=BIASB[:, 61:62]), waits=[t4v])
                t4e = sch.op("s", lambda ss: ss.activation(SMALL[:, 74:75], SMALL[:, 74:75], AF.Exp,
                                                           scale=-0.5))
                y4ap = TEV(36864, F32, 1024)
                t4a = sch.op("v", (lambda v, z_=z4p, y_=y4ap: v.tensor_scalar(
                    y_, z_, SMALL[:, 72:73], SMALL[:, 74:75], ALU.add, ALU.mult)),
                             waits=[t4e, y4_free[0]] if y4_free[0] else [t4e])
                for bank in (4, 5):
                    psum_free[bank] = t4a
                ot = sch.dma("y", "dout", (lambda y, r0=tok0, z_=y4ap:
                                           y.dma_start(out=YOUT[r0:r0 + 128, :], in_=z_)),
                             waits=[t4a])
                y4_free[0] = ot

        xk_free = p5_s1(0)
        p5_s2(0)
        xk_free = p5_s1(1)
        for ti in range(len(TILES)):
            p5_s3(ti)
            if ti + 1 < len(TILES):
                p5_s2(ti + 1)
            if ti + 2 < len(TILES):
                xk_free = p5_s1(ti + 2)

        sch.op("y", lambda y: y.wait_ge(sem_handles["dout"], sch.cnt["dout"]))

        sch.emit(block, sem_handles)
        stack.close()
    return nc


def make_inputs(core, inp):
    b = core // 2
    half = core % 2
    s0 = 0 if half == 0 else SEQ - SPAN
    x = np.asarray(inp["L0_attn_x"][b], np.float32)
    xT = np.ascontiguousarray(x.T)
    perm = np.array([[h, h + 8] for h in range(8)], np.int64).reshape(-1)
    colperm = (perm[:, None] * 64 + np.arange(64)[None, :]).reshape(-1)
    wq = np.asarray(inp["L1_attn_Wq"], np.float32)
    bq = np.asarray(inp["L1_attn_bq"], np.float32)
    i = np.arange(128)[:, None]
    j = np.arange(QT)[None, :]
    masks = np.zeros((128, 5 * QT), np.float32)
    masks[:, 0 * QT:0 * QT + 128] = (i >= j[:, 0:128]).astype(np.float32)
    masks[:, 1 * QT:1 * QT + 256] = ((j[:, 0:256] - i) <= 128).astype(np.float32)
    masks[:, 2 * QT:3 * QT] = (((j - i) >= 1) & ((j - i) <= 256)).astype(np.float32)
    masks[:, 3 * QT + 128:4 * QT] = ((j[:, 128:QT] - i) >= 129).astype(np.float32)
    masks[:, 4 * QT + 256:5 * QT] = ((j[:, 256:QT] - i) >= 257).astype(np.float32)
    d = {
        "XSF": np.ascontiguousarray(xT[:, s0:s0 + SPAN]),
        "XFF": xT,
        "XST": np.ascontiguousarray(x[s0:s0 + SPAN]),
        "QW": np.asarray(inp["L0_attn_qW"], np.float32),
        "QBI": np.asarray(inp["L0_attn_qb"], np.float32),
        "KW": np.asarray(inp["L0_attn_kW"], np.float32),
        "KBI": np.asarray(inp["L0_attn_kb"], np.float32),
        "VW": np.asarray(inp["L0_attn_vW"], np.float32),
        "VBI": np.asarray(inp["L0_attn_vb"], np.float32),
        "OW": np.asarray(inp["L0_attn_oW"], np.float32),
        "OBI": np.asarray(inp["L0_attn_ob"], np.float32),
        "JV": np.asarray(inp["L0_attn_J"], np.float32).reshape(8, 128),
        "FW": np.asarray(inp["L0_ffn_weight"], np.float32),
        "FB": np.asarray(inp["L0_ffn_bias"], np.float32).reshape(1, D),
        "W1Q": np.ascontiguousarray(wq[:, colperm]),
        "B1Q": np.ascontiguousarray(bq[colperm]).reshape(8, 128).T.copy(),
        "W1K": np.asarray(inp["L1_attn_Wk"], np.float32),
        "B1K": np.asarray(inp["L1_attn_bk"], np.float32).reshape(128, 1),
        "W1V": np.asarray(inp["L1_attn_Wv"], np.float32),
        "B1VC": np.asarray(inp["L1_attn_bv"], np.float32).reshape(128, 1),
        "W1O": np.asarray(inp["L1_attn_Wo"], np.float32),
        "B1O": np.asarray(inp["L1_attn_bo"], np.float32).reshape(1, D),
        "WRT": np.ascontiguousarray(np.asarray(inp["L1_ffn_W_r"], np.float32).T),
        "WKT": np.ascontiguousarray(np.asarray(inp["L1_ffn_W_k"], np.float32).T),
        "WVT": np.ascontiguousarray(np.asarray(inp["L1_ffn_W_v"], np.float32).T),
        "MUR": np.asarray(inp["L1_ffn_mu_r"], np.float32).reshape(8, 128).T.copy(),
        "MUK": np.asarray(inp["L1_ffn_mu_k"], np.float32).reshape(8, 128).T.copy(),
        "MASKS": masks.astype(ml_dtypes.bfloat16),
        "BSGN": np.concatenate([BIAS_SGN.T, np.zeros((4, 4), np.float32)], axis=1),
        "IDENT": np.eye(128, dtype=np.float32),
    }
    return d


def kernel(**inputs):
    nc = build_program()
    in_maps = [make_inputs(c, inputs) for c in range(8)]
    res = run_bass_kernel_spmd(nc, in_maps, core_ids=list(range(8)))
    out = np.empty((4, SEQ, D), np.float32)
    for c in range(8):
        b, half = c // 2, c % 2
        v0 = 0 if half == 0 else SPAN - 2048
        out[b, half * 2048:(half + 1) * 2048] = res.results[c]["YOUT"][v0:v0 + 2048]
    return out
